# version 06
# baseline (speedup 1.0000x reference)
import numpy as np
import ml_dtypes
from contextlib import ExitStack

import concourse.bass as bass
import concourse.mybir as mybir
from concourse.bass_utils import run_bass_kernel_spmd

F32 = mybir.dt.float32
BF16 = mybir.dt.bfloat16
AF = mybir.ActivationFunctionType
ALU = mybir.AluOpType
AX = mybir.AxisListType

T = 4096
D = 1024
KC = D // 128
NT128 = T // 128
FF = 4096
NFC = FF // 128
H, HD = 8, 64
E = 8
HID = 2816
NHC = HID // 128
TT = 512
NTT = T // TT
ETA, DTC, GFLOOR, EPS = 0.1, 0.01, 0.1, 1e-5
CB = 32
HCH = 128
NHCH = T // HCH

V_M1, V_RSTD1, V_RSTD12, V_MR12 = 0, 1, 2, 3
V_POTV, V_POTG = 5, 6
V_GSC = 7
V_CUM = 8
V_BKRE, V_BKIM = 9, 10
V_MA, V_RSTDA, V_MRA = 11, 12, 13
V_M3, V_RSTD3, V_MR3 = 14, 15, 16
V_M4, V_RSTD4, V_MR4 = 17, 18, 19
V_SCEXP = 20


class Ctr:
    __slots__ = ("h", "n")

    def __init__(self, h):
        self.h = h
        self.n = 0

    def inc(self, k=1):
        self.n += k
        return k


class B:

    def __init__(self):
        self.stack = ExitStack()
        self.nc = bass.Bass("TRN2", target_bir_lowering=False,
                            detect_race_conditions=False)
        self.S = self.stack.enter_context
        self.pool = []
        self.keep = 8
        self._idx = 0

    def init_pool(self, n=48):
        self.pool = [self.S(self.nc.semaphore(f"sp{i}")) for i in range(n)]
        self._idx = 0

    def sem(self, name=None):
        assert self._idx < len(self.pool), "sem pool exhausted"
        h = self.pool[self._idx]
        self._idx += 1
        return Ctr(h)

    def new_phase(self):
        nc = self.nc
        self.keep += 1
        ids = sorted(h.num for h in self.pool)
        assert ids == list(range(ids[0], ids[-1] + 1)), "pool not contiguous"
        rng = range(ids[0], ids[-1] + 1)
        with nc.Block(f"semclr{self.keep}") as block:
            @block.gpsimd
            def _(g):
                g.dma_reset(rng)
                g.sem_clear(rng)
        self._idx = 0


def fm(dram_ap, c=KC):
    return dram_ap.rearrange("(c p) t -> p c t", p=128)


def mk_sp(v, jk):
    return lambda: v.tensor_scalar_mul(jk[:, 0:1], jk[:, 1:2], 1.0)


def build_program(nphases=None):
    b = B()
    nc = b.nc
    S = b.S
    DT = nc.dram_tensor

    b.init_pool(48)
    io = {}
    for name, shape in [
        ("xin", [T, D]), ("ln1_g", [D]), ("ln1_b", [D]), ("ln2_g", [D]),
        ("ln2_b", [D]), ("ln3_g", [D]), ("ln3_b", [D]), ("w_pot", [D, 2]),
        ("b_pot", [2]), ("w_bk", [2, D]), ("b_bk", [D]),
        ("w_qkv", [D, 1536]), ("b_qkv", [1536]), ("w_out", [512, D]),
        ("b_out", [D]), ("w_ff1", [D, FF]), ("b_ff1", [FF]),
        ("w_ff2", [FF, D]), ("b_ff2", [D]), ("gate_w", [E, D]),
        ("w1", [E, D, HID]), ("w2", [E, HID, D]), ("w3", [E, D, HID]),
        ("cid", [128, 128]), ("ctrileta", [128, 128]),
        ("csign", [1, 512]),
    ]:
        io[name] = DT(name, shape, F32, kind="ExternalInput").ap()
    io["cidb"] = DT("cidb", [128, 128], BF16, kind="ExternalInput").ap()
    io["cidb2"] = DT("cidb2", [128, 64], BF16, kind="ExternalInput").ap()
    io["out"] = DT("out", [T, D], F32, kind="ExternalOutput").ap()

    sc = {}
    for name, shape, dt in [
        ("x0T", [D, T], F32), ("x1T", [D, T], F32), ("y1T", [D, T], F32),
        ("y2T", [D, T], F32), ("x2T", [D, T], F32), ("hT", [D, T], BF16),
        ("facc", [D, T], F32), ("masksd", [E, T], F32),
        ("vecs", [32, T], F32),
    ]:
        sc[name] = DT(name, shape, dt, kind="Internal").ap()

    ids = S(nc.sbuf_tensor("ids", [128, 128], F32))
    idsb = S(nc.sbuf_tensor("idsb", [128, 128], BF16))
    idsb2 = S(nc.sbuf_tensor("idsb2", [128, 64], BF16))
    trileta = S(nc.sbuf_tensor("trileta", [128, 128], F32))
    ones_r = S(nc.sbuf_tensor("ones_r", [1, 128], F32))
    ones_c = S(nc.sbuf_tensor("ones_c", [128, 1], F32))
    gv = S(nc.sbuf_tensor("gv", [128, 3 * KC], F32))
    bv = S(nc.sbuf_tensor("bv", [128, 3 * KC], F32))
    bqkvc = S(nc.sbuf_tensor("bqkvc", [128, 12], F32))
    boutc = S(nc.sbuf_tensor("boutc", [128, KC], F32))
    bbkc = S(nc.sbuf_tensor("bbkc", [128, KC], F32))
    bff1c = S(nc.sbuf_tensor("bff1c", [128, NFC], F32))
    bff2c = S(nc.sbuf_tensor("bff2c", [128, KC], F32))
    gateT = S(nc.sbuf_tensor("gateT", [128, KC * E], F32))
    cbank = S(nc.sbuf_tensor("cbank", [128, 8], F32))
    _cvals = [0.0, 1.0, -0.5, -1.0, EPS, 0.5, 2.0, -2.0]
    for _i, _vv in enumerate(_cvals):
        nc.const_aps.register(F32, _vv, cbank[:, _i:_i + 1])
    per = dict(ids=ids, idsb=idsb, idsb2=idsb2, trileta=trileta,
               ones_r=ones_r, ones_c=ones_c, gv=gv, bv=bv, bqkvc=bqkvc,
               boutc=boutc, bbkc=bbkc, bff1c=bff1c, bff2c=bff2c, gateT=gateT)

    ld0 = b.sem("ld0")
    with nc.Block("consts") as block:
        @block.sync
        def _(s):
            cp = lambda dst, src: s.dma_start(
                dst, src, allow_slow_non_contiguous=True
            ).then_inc(ld0.h, ld0.inc(16))
            cp(ids[:, :], io["cid"])
            cp(idsb[:, :], io["cidb"])
            cp(idsb2[:, :], io["cidb2"])
            cp(trileta[:, :], io["ctrileta"])
            r = lambda a: a.rearrange("(c p) -> p c", p=128)
            cp(gv[:, 0:KC], r(io["ln1_g"]))
            cp(gv[:, KC:2 * KC], r(io["ln2_g"]))
            cp(gv[:, 2 * KC:3 * KC], r(io["ln3_g"]))
            cp(bv[:, 0:KC], r(io["ln1_b"]))
            cp(bv[:, KC:2 * KC], r(io["ln2_b"]))
            cp(bv[:, 2 * KC:3 * KC], r(io["ln3_b"]))
            cp(bqkvc[:, :], r(io["b_qkv"]))
            cp(boutc[:, :], r(io["b_out"]))
            cp(bbkc[:, :], r(io["b_bk"]))
            cp(bff1c[:, :], r(io["b_ff1"]))
            cp(bff2c[:, :], r(io["b_ff2"]))
            s.wait_ge(ld0.h, ld0.n)

        @block.vector
        def _(v):
            v.memset(ones_r[:, :], 1.0)
            v.memset(ones_c[:, :], 1.0)
            for _i, _vv in enumerate(_cvals):
                v.memset(cbank[:, _i:_i + 1], _vv)

    with ExitStack() as ph:
        gsb = ph.enter_context(nc.sbuf_tensor("gp_gsb", [8, D], F32))
        psg = [ph.enter_context(nc.psum_tensor(f"gp_ps{j}", [128, 8], F32))
               for j in range(2)]
        gl = b.sem()
        tp = b.sem()
        ev = b.sem()
        with nc.Block("gateprep") as block:
            @block.sync
            def _(s):
                s.dma_start(gsb[:, :], io["gate_w"]).then_inc(gl.h, gl.inc(16))

            @block.tensor
            def _(t):
                t.wait_ge(gl.h, 16)
                t.wait_ge(ld0.h, ld0.n)
                for k in range(KC):
                    if k >= 2:
                        t.wait_ge(ev.h, k - 1)
                    t.transpose(psg[k % 2][:, :], gsb[:, k * 128:(k + 1) * 128],
                                ids[0:8, 0:8]).then_inc(tp.h, tp.inc())

            @block.vector
            def _(v):
                for k in range(KC):
                    v.wait_ge(tp.h, k + 1)
                    v.tensor_copy(gateT[:, k * 8:(k + 1) * 8],
                                  psg[k % 2][:, :]).then_inc(ev.h, ev.inc())

    plist = [
        (emit_p0, ()),
        (emit_moe, (0,)),
        (emit_combine_pot, ()),
        (emit_bk_scan, ()),
        (emit_attn_core, ()),
        (emit_y2stats, ()),
        (emit_ffn_x2, ()),
        (emit_h3_gate, ()),
        (emit_moe, (1,)),
        (emit_final, ()),
    ]
    if nphases is not None:
        plist = plist[:nphases]
    for fn, args in plist:
        b.new_phase()
        fn(b, io, sc, per, *args)

    b.stack.close()
    return nc


def _ln_rstd(eng, rstd, var_plus, tmp):
    eng.activation(tmp, var_plus, AF.Ln, bias=EPS, scale=1.0)
    eng.activation(rstd, tmp, AF.Exp, bias=0.0, scale=-0.5)


def emit_p0(b, io, sc, per):
    nc = b.nc
    ids, gateT = per["ids"], per["gateT"]
    with ExitStack() as ph:
        P = ph.enter_context
        xts = [P(nc.sbuf_tensor(f"p0_x{j}", [128, D], F32)) for j in range(2)]
        sqt = P(nc.sbuf_tensor("p0_sq", [128, D], F32))
        stat = P(nc.sbuf_tensor("p0_st", [128, 8], F32))
        h1t = P(nc.sbuf_tensor("p0_h1", [128, D], F32))
        xst = P(nc.sbuf_tensor("p0_xs", [128, D], F32))
        hst = P(nc.sbuf_tensor("p0_hs", [128, D], F32))
        sctile = P(nc.sbuf_tensor("p0_sc", [128, 10], F32))
        msc = P(nc.sbuf_tensor("p0_msc", [128, 8], F32))
        jk = P(nc.sbuf_tensor("p0_jk", [128, 2], F32))
        gevr = P(nc.sbuf_tensor("p0_gev", [8, 128], F32))
        mst = [P(nc.sbuf_tensor(f"p0_ms{j}", [8, 128], F32)) for j in range(2)]
        psT = [P(nc.psum_tensor(f"p0_psT{j}", [128, 128], F32)) for j in range(2)]
        psG = [P(nc.psum_tensor(f"p0_psG{j}", [8, 128], F32)) for j in range(2)]
        psS = P(nc.psum_tensor("p0_psS", [128, 8], F32))

        ldx = [b.sem() for _ in range(2)]
        dst = b.sem()
        acr = b.sem()
        dap = b.sem()
        tpx = b.sem()
        evx = b.sem()
        gmm = b.sem()
        gev = b.sem()
        top = b.sem()
        smm = b.sem()
        sev = b.sem()
        wbx = b.sem()
        wbh = b.sem()
        wbm = b.sem()

        x0T3, hT3 = fm(sc["x0T"]), fm(sc["hT"])

        with nc.Block("p0") as block:
            @block.sync
            def _(s):
                for i in range(NT128):
                    j = i % 2
                    if i >= 2:
                        s.wait_ge(dap.h, i - 1)
                        s.wait_ge(tpx.h, 16 * (i - 2) + 8)
                    s.dma_start(xts[j][:, :],
                                io["xin"][i * 128:(i + 1) * 128, :]
                                ).then_inc(ldx[j].h, ldx[j].inc(16))

            @block.vector
            def _(v):
                sp = mk_sp(v, jk)
                v.memset(jk[:, :], 0.0)
                for i in range(NT128):
                    j = i % 2
                    v.wait_ge(ldx[j].h, 16 * (i // 2 + 1))
                    xt = xts[j]
                    v.reduce_sum(stat[:, 0:1], xt[:, :], AX.X)
                    v.tensor_mul(sqt[:, :], xt[:, :], xt[:, :])
                    v.reduce_sum(stat[:, 1:2], sqt[:, :], AX.X)
                    v.tensor_scalar_mul(stat[:, 2:3], stat[:, 0:1], 1.0 / D)
                    v.tensor_scalar_mul(stat[:, 3:4], stat[:, 1:2], 1.0 / D)
                    v.tensor_mul(stat[:, 4:5], stat[:, 2:3], stat[:, 2:3])
                    sp()
                    v.tensor_sub(stat[:, 5:6], stat[:, 3:4], stat[:, 4:5]
                                 ).then_inc(dst.h, dst.inc())
                    v.wait_ge(acr.h, 2 * i + 1)
                    if i >= 1:
                        v.wait_ge(tpx.h, 16 * i)
                    v.tensor_scalar(h1t[:, :], xt[:, :], stat[:, 2:3],
                                    stat[:, 6:7], ALU.subtract, ALU.mult
                                    ).then_inc(dap.h, dap.inc())
                    v.wait_ge(gmm.h, i + 1)
                    v.tensor_copy(gevr[:, :], psG[i % 2][:, :]).then_inc(
                        gev.h, gev.inc())
                    v.wait_ge(smm.h, 2 * i + 1)
                    s8 = sctile[:, 0:8]
                    v.tensor_copy(s8, psS[:, :])
                    sp()
                    v.reduce_max(sctile[:, 8:9], s8, AX.X)
                    sp()
                    v.tensor_scalar_mul(sctile[:, 9:10], sctile[:, 8:9], -1.0
                                        ).then_inc(top.h, top.inc())
                    v.wait_ge(acr.h, 2 * i + 2)
                    v.reduce_sum(sctile[:, 8:9], s8, AX.X)
                    sp()
                    v.reciprocal(sctile[:, 9:10], sctile[:, 8:9])
                    sp()
                    v.tensor_scalar_mul(s8, s8, sctile[:, 9:10])
                    sp()
                    v.reduce_max(sctile[:, 8:9], s8, AX.X)
                    sp()
                    v.tensor_single_scalar(msc[:, :], s8, sctile[:, 8:9],
                                           ALU.is_ge)
                    sp()
                    v.tensor_scalar_mul(msc[:, :], msc[:, :], 2.0)
                    sp()
                    v.tensor_sub(msc[:, :], s8, msc[:, :])
                    sp()
                    v.reduce_max(sctile[:, 8:9], msc[:, :], AX.X)
                    sp()
                    v.tensor_single_scalar(msc[:, :], s8, sctile[:, 8:9],
                                           ALU.is_ge)
                    sp()
                    v.tensor_mul(msc[:, :], s8, msc[:, :]).then_inc(
                        top.h, top.inc())
                    v.wait_ge(smm.h, 2 * i + 2)
                    if i >= 2:
                        v.wait_ge(wbm.h, 16 * (i - 1))
                    v.tensor_copy(mst[i % 2][:, :], psG[i % 2][:, :]).then_inc(
                        sev.h, sev.inc())

            @block.scalar
            def _(s):
                for i in range(NT128):
                    s.wait_ge(dst.h, i + 1)
                    s.activation(stat[:, 7:8], stat[:, 5:6], AF.Ln,
                                 bias=EPS, scale=1.0)
                    s.activation(stat[:, 6:7], stat[:, 7:8], AF.Exp,
                                 bias=0.0, scale=-0.5).then_inc(acr.h, acr.inc())
                    for k in range(16):
                        gidx = 16 * i + k
                        if k == 0 and i >= 1:
                            s.wait_ge(wbx.h, 16 * i)
                        if k == 8:
                            if i >= 1:
                                s.wait_ge(wbh.h, 16 * i)
                            s.wait_ge(gmm.h, i)
                        s.wait_ge(tpx.h, gidx + 1)
                        dstt = xst if k < 8 else hst
                        kk = k % 8
                        s.copy(dstt[:, kk * 128:(kk + 1) * 128],
                               psT[gidx % 2][:, :]).then_inc(evx.h, evx.inc())
                    s.wait_ge(top.h, 2 * i + 1)
                    s.activation(sctile[:, 0:8], sctile[:, 0:8], AF.Exp,
                                 bias=sctile[:, 9:10], scale=1.0).then_inc(
                        acr.h, acr.inc())

            @block.tensor
            def _(t):
                for i in range(NT128):
                    j = i % 2
                    t.wait_ge(ldx[j].h, 16 * (i // 2 + 1))
                    for k in range(8):
                        gidx = 16 * i + k
                        if gidx >= 2:
                            t.wait_ge(evx.h, gidx - 1)
                        t.transpose(psT[gidx % 2][:, :],
                                    xts[j][:, k * 128:(k + 1) * 128],
                                    ids[:, :]).then_inc(tpx.h, tpx.inc())
                    t.wait_ge(dap.h, i + 1)
                    for k in range(8):
                        gidx = 16 * i + 8 + k
                        t.wait_ge(evx.h, gidx - 1)
                        t.transpose(psT[gidx % 2][:, :],
                                    h1t[:, k * 128:(k + 1) * 128],
                                    ids[:, :]).then_inc(tpx.h, tpx.inc())
                    if i >= 2:
                        t.wait_ge(sev.h, i - 1)
                    for k in range(KC):
                        t.wait_ge(evx.h, 16 * i + 8 + k + 1)
                        inst = t.matmul(psG[i % 2][:, :],
                                        gateT[:, k * 8:(k + 1) * 8],
                                        hst[:, k * 128:(k + 1) * 128],
                                        start=(k == 0), stop=(k == KC - 1))
                    inst.then_inc(gmm.h, gmm.inc())
                    t.wait_ge(gev.h, i + 1)
                    if i >= 1:
                        t.wait_ge(top.h, 2 * (i - 1) + 1)
                    t.transpose(psS[:, :], gevr[:, :],
                                ids[0:8, 0:8]).then_inc(smm.h, smm.inc())
                    t.wait_ge(top.h, 2 * i + 2)
                    t.transpose(psG[i % 2][:, :], msc[:, :],
                                ids[:, :]).then_inc(smm.h, smm.inc())

            @block.gpsimd
            def _(g):
                for i in range(NT128):
                    g.wait_ge(evx.h, 16 * i + 8)
                    if i >= 1:
                        g.wait_ge(wbx.h, 16 * i)
                    g.dma_start(x0T3[:, :, i * 128:(i + 1) * 128],
                                xst[:, :].rearrange("p (c t) -> p c t", c=KC)
                                ).then_inc(wbx.h, wbx.inc(16))
                    g.wait_ge(evx.h, 16 * i + 16)
                    if i >= 1:
                        g.wait_ge(wbh.h, 16 * i)
                    g.dma_start(hT3[:, :, i * 128:(i + 1) * 128],
                                hst[:, :].rearrange("p (c t) -> p c t", c=KC)
                                ).then_inc(wbh.h, wbh.inc(16))
                    g.wait_ge(sev.h, i + 1)
                    if i >= 1:
                        g.wait_ge(wbm.h, 16 * i)
                    g.dma_start(sc["masksd"][:, i * 128:(i + 1) * 128],
                                mst[i % 2][:, :]).then_inc(wbm.h, wbm.inc(16))
                g.wait_ge(wbx.h, wbx.n)
                g.wait_ge(wbh.h, wbh.n)
                g.wait_ge(wbm.h, wbm.n)


def emit_moe(b, io, sc, per, inv):
    nc = b.nc
    ones_r = per["ones_r"]
    hT3 = fm(sc["hT"])
    facc3 = fm(sc["facc"])
    with ExitStack() as ph:
        P = ph.enter_context
        A = P(nc.sbuf_tensor(f"m{inv}_A", [128, KC * HID], BF16))
        Bm = P(nc.sbuf_tensor(f"m{inv}_B", [128, KC * HID], BF16))
        C = P(nc.sbuf_tensor(f"m{inv}_C", [128, NHC * D], BF16))
        ht = P(nc.sbuf_tensor(f"m{inv}_h", [128, KC * TT], BF16))
        xe = P(nc.sbuf_tensor(f"m{inv}_xe", [128, KC * TT], BF16))
        ag = P(nc.sbuf_tensor(f"m{inv}_ag", [128, NHC * TT], BF16))
        outT = P(nc.sbuf_tensor(f"m{inv}_o", [128, KC * TT], F32))
        mrow = [P(nc.sbuf_tensor(f"m{inv}_mr{j}", [1, TT], F32)) for j in range(2)]
        jk = P(nc.sbuf_tensor(f"m{inv}_jk", [128, 2], F32))
        psA = [P(nc.psum_tensor(f"m{inv}_psA{j}", [128, TT], F32)) for j in range(3)]
        psC = [P(nc.psum_tensor(f"m{inv}_psC{j}", [128, TT], F32)) for j in range(2)]
        psM = P(nc.psum_tensor(f"m{inv}_psM", [128, TT], F32))

        A3 = A[:, :].rearrange("p (c n) -> p c n", c=KC)
        B3 = Bm[:, :].rearrange("p (c n) -> p c n", c=KC)
        C3 = C[:, :].rearrange("p (c n) -> p c n", c=NHC)
        ht3 = ht[:, :].rearrange("p (c t) -> p c t", c=KC)
        xe3 = xe[:, :].rearrange("p (c t) -> p c t", c=KC)
        ag3 = ag[:, :].rearrange("p (c t) -> p c t", c=NHC)
        oT3 = outT[:, :].rearrange("p (c t) -> p c t", c=KC)

        wldA, wldB, wldC = b.sem(), b.sem(), b.sem()
        wfA, wfB, wfC = b.sem(), b.sem(), b.sem()
        hld = b.sem()
        mld = [b.sem() for _ in range(2)]
        bcm = b.sem()
        xeb = b.sem()
        mmA = b.sem()
        evA = b.sem()
        evB = b.sem()
        mmC = b.sem()
        evC = b.sem()
        ord_ = b.sem()
        oacc = b.sem()
        phw3 = b.sem()

        w1r = io["w1"].rearrange("e (c p) n -> e p c n", p=128)
        w3r = io["w3"].rearrange("e (c p) n -> e p c n", p=128)
        w2r = io["w2"].rearrange("e (c p) n -> e p c n", p=128)

        with nc.Block(f"moe{inv}") as block:
            @block.gpsimd
            def _(g):
                g.dma_start(A3, w1r[0]).then_inc(wldA.h, wldA.inc(16))
                g.dma_start(B3, w3r[0]).then_inc(wldB.h, wldB.inc(16))
                g.dma_start(C3, w2r[0]).then_inc(wldC.h, wldC.inc(16))
                for e in range(E):
                    for ti in range(NTT):
                        q = e * NTT + ti
                        if e + 1 < E and ti == NTT - 2:
                            g.wait_ge(wfA.h, e + 1)
                            g.dma_start(A3, w1r[e + 1]).then_inc(
                                wldA.h, wldA.inc(16))
                        if e + 1 < E and ti == NTT - 1:
                            g.wait_ge(wfB.h, e + 1)
                            g.dma_start(B3, w3r[e + 1]).then_inc(
                                wldB.h, wldB.inc(16))
                        g.wait_ge(ord_.h, q + 1)
                        if q >= 1:
                            g.wait_ge(oacc.h, 16 * q)
                        g.dma_start(
                            facc3[:, :, ti * TT:(ti + 1) * TT], oT3,
                            accum_op=(ALU.bypass if e == 0 else ALU.add)
                        ).then_inc(oacc.h, oacc.inc(16))
                        if e + 1 < E and ti == NTT - 1:
                            g.wait_ge(wfC.h, e + 1)
                            g.dma_start(C3, w2r[e + 1]).then_inc(
                                wldC.h, wldC.inc(16))
                g.wait_ge(oacc.h, oacc.n)

            @block.sync
            def _(s):
                for e in range(E):
                    for ti in range(NTT):
                        q = e * NTT + ti
                        if q >= 1:
                            s.wait_ge(xeb.h, q)
                        s.dma_start(ht3, hT3[:, :, ti * TT:(ti + 1) * TT]
                                    ).then_inc(hld.h, hld.inc(16))
                        j = q % 2
                        if q >= 2:
                            s.wait_ge(xeb.h, q - 1)
                        s.dma_start(mrow[j][:, :],
                                    sc["masksd"][e:e + 1, ti * TT:(ti + 1) * TT]
                                    ).then_inc(mld[j].h, mld[j].inc(16))

            @block.tensor
            def _(t):
                for e in range(E):
                    for ti in range(NTT):
                        q = e * NTT + ti
                        j = q % 2
                        t.wait_ge(mld[j].h, 16 * (q // 2 + 1))
                        if q >= 1:
                            t.wait_ge(xeb.h, q)
                        t.matmul(psM[:, :], ones_r[:, :], mrow[j][:, :],
                                 start=True, stop=True).then_inc(bcm.h, bcm.inc())
                        if ti == 0:
                            t.wait_ge(wldA.h, 16 * (e + 1))
                        t.wait_ge(xeb.h, q + 1)
                        for m in range(NHC):
                            if m >= 3:
                                t.wait_ge(evA.h, 22 * q + m - 2)
                            elif q >= 1:
                                t.wait_ge(evB.h, 22 * (q - 1) + 20 + m)
                            for k in range(KC):
                                inst = t.matmul(
                                    psA[(44 * q + m) % 3][:, :],
                                    A3[:, k, m * 128:(m + 1) * 128],
                                    xe3[:, k, :],
                                    start=(k == 0), stop=(k == KC - 1))
                            inst.then_inc(mmA.h, mmA.inc())
                        if ti == NTT - 1:
                            t.nop().then_inc(wfA.h, wfA.inc())
                        if ti == 0:
                            t.wait_ge(wldB.h, 16 * (e + 1))
                        for m in range(NHC):
                            if m >= 3:
                                t.wait_ge(evB.h, 22 * q + m - 2)
                            else:
                                t.wait_ge(evA.h, 22 * q + 20 + m)
                            for k in range(KC):
                                inst = t.matmul(
                                    psA[(44 * q + NHC + m) % 3][:, :],
                                    B3[:, k, m * 128:(m + 1) * 128],
                                    xe3[:, k, :],
                                    start=(k == 0), stop=(k == KC - 1))
                            inst.then_inc(mmA.h, mmA.inc())
                        if ti == NTT - 1:
                            t.nop().then_inc(wfB.h, wfB.inc())
                        t.nop().then_inc(phw3.h, phw3.inc())
                        if ti == 0:
                            t.wait_ge(wldC.h, 16 * (e + 1))
                        t.wait_ge(evB.h, 22 * (q + 1))
                        for mo in range(KC):
                            ci = 8 * q + mo
                            if ci >= 2:
                                t.wait_ge(evC.h, ci - 1)
                            for k2 in range(NHC):
                                inst = t.matmul(
                                    psC[ci % 2][:, :],
                                    C3[:, k2, mo * 128:(mo + 1) * 128],
                                    ag3[:, k2, :],
                                    start=(k2 == 0), stop=(k2 == NHC - 1))
                            inst.then_inc(mmC.h, mmC.inc())
                        if ti == NTT - 1:
                            t.nop().then_inc(wfC.h, wfC.inc())

            @block.vector
            def _(v):
                v.memset(jk[:, :], 0.0)
                for e in range(E):
                    for ti in range(NTT):
                        q = e * NTT + ti
                        v.wait_ge(hld.h, 16 * (q + 1))
                        v.wait_ge(bcm.h, q + 1)
                        if q >= 1:
                            v.wait_ge(phw3.h, q)
                        for k in range(KC):
                            inst = v.tensor_mul(xe3[:, k, :], ht3[:, k, :],
                                                psM[:, :])
                        inst.then_inc(xeb.h, xeb.inc())
                        for m in range(NHC):
                            gi = 44 * q + NHC + m
                            v.wait_ge(mmA.h, gi + 1)
                            v.wait_ge(evA.h, 22 * q + m + 1)
                            v.tensor_mul(ag3[:, m, :], ag3[:, m, :],
                                         psA[gi % 3][:, :]).then_inc(
                                evB.h, evB.inc())
                        for mo in range(KC):
                            ci = 8 * q + mo
                            v.wait_ge(mmC.h, ci + 1)
                            if q >= 1 and mo == 0:
                                v.wait_ge(oacc.h, 16 * q)
                            inst = v.tensor_copy(oT3[:, mo, :],
                                                 psC[ci % 2][:, :])
                            inst.then_inc(evC.h, evC.inc())
                        v.tensor_scalar_mul(jk[:, 0:1], jk[:, 1:2], 1.0
                                            ).then_inc(ord_.h, ord_.inc())

            @block.scalar
            def _(s):
                for e in range(E):
                    for ti in range(NTT):
                        q = e * NTT + ti
                        for m in range(NHC):
                            gi = 44 * q + m
                            s.wait_ge(mmA.h, gi + 1)
                            s.activation(ag3[:, m, :], psA[gi % 3][:, :],
                                         AF.Silu).then_inc(evA.h, evA.inc())


def emit_combine_pot(b, io, sc, per):
    nc = b.nc
    ones_c, ones_r, gv, bv = per["ones_c"], per["ones_r"], per["gv"], per["bv"]
    ids = per["ids"]
    x0T3, facc3, x1T3 = fm(sc["x0T"]), fm(sc["facc"]), fm(sc["x1T"])
    with ExitStack() as ph:
        P = ph.enter_context
        x0t = [P(nc.sbuf_tensor(f"c_x0{j}", [128, KC * TT], F32)) for j in range(2)]
        fat = [P(nc.sbuf_tensor(f"c_fa{j}", [128, KC * TT], F32)) for j in range(2)]
        x1t = P(nc.sbuf_tensor("c_x1", [128, KC * TT], F32))
        sqt = P(nc.sbuf_tensor("c_sqt", [128, KC * TT], F32))
        wp = P(nc.sbuf_tensor("c_wp", [128, KC * 2], F32))
        wt = P(nc.sbuf_tensor("c_wt", [128, KC * 2], F32))
        mv3 = P(nc.sbuf_tensor("c_mv", [128, KC * 3], F32))
        bp = P(nc.sbuf_tensor("c_bp", [1, 2], F32))
        crow = P(nc.sbuf_tensor("c_cr", [1, 8], F32))
        s1tok = P(nc.sbuf_tensor("c_s1", [128, NT128], F32))
        s2tok = P(nc.sbuf_tensor("c_s2", [128, NT128], F32))
        praw = P(nc.sbuf_tensor("c_pr", [128, 2 * NT128], F32))
        stt = P(nc.sbuf_tensor("c_stt", [128, 8 * NT128], F32))
        gam = P(nc.sbuf_tensor("c_gam", [128, NT128], F32))
        colb = P(nc.sbuf_tensor("c_col", [128, 8], F32))
        tpo = P(nc.sbuf_tensor("c_tpo", [32, 128], F32))
        jk = P(nc.sbuf_tensor("c_jk", [128, 2], F32))
        ell32 = P(nc.sbuf_tensor("c_ell", [32, 128], F32))
        cum32 = P(nc.sbuf_tensor("c_cum", [32, 128], F32))
        one32 = P(nc.sbuf_tensor("c_one", [32, 128], F32))
        psSq = [P(nc.psum_tensor(f"c_pQ{j}", [128, 512], F32)) for j in range(2)]
        psP = [P(nc.psum_tensor(f"c_pP{j}", [128, 512], F32)) for j in range(2)]
        psT2 = P(nc.psum_tensor("c_pT", [128, 512], F32))
        psX = P(nc.psum_tensor("c_pX", [128, 512], F32))

        ldx = [b.sem() for _ in range(2)]
        ldf = [b.sem() for _ in range(2)]
        ldw = b.sem()
        dvx = b.sem()
        smm = b.sem()
        sev = b.sem()
        wb1 = b.sem()
        fin = b.sem()
        wbv = b.sem()

        m1v = lambda: stt[:, 0:32]
        varv = lambda: stt[:, 32:64]
        rstd1v = lambda: stt[:, 64:96]
        rstd12v = lambda: stt[:, 96:128]
        tmpv = lambda: stt[:, 128:160]
        potVv = lambda: stt[:, 160:192]
        abv = lambda: stt[:, 192:224]

        with nc.Block("combpot") as block:
            @block.sync
            def _(s):
                s.dma_start(wp[:, :].rearrange("p (c j) -> p c j", c=KC),
                            io["w_pot"].rearrange("(c p) j -> p c j", p=128),
                            allow_slow_non_contiguous=True
                            ).then_inc(ldw.h, ldw.inc(16))
                s.dma_start(bp[:, :], io["b_pot"].rearrange("(r j) -> r j", r=1)
                            ).then_inc(ldw.h, ldw.inc(16))
                for ti in range(NTT):
                    j = ti % 2
                    if ti >= 2:
                        s.wait_ge(dvx.h, ti - 1)
                    s.dma_start(x0t[j][:, :].rearrange("p (c t) -> p c t", c=KC),
                                x0T3[:, :, ti * TT:(ti + 1) * TT]
                                ).then_inc(ldx[j].h, ldx[j].inc(16))
                    s.dma_start(fat[j][:, :].rearrange("p (c t) -> p c t", c=KC),
                                facc3[:, :, ti * TT:(ti + 1) * TT]
                                ).then_inc(ldf[j].h, ldf[j].inc(16))

            @block.gpsimd
            def _(g):
                for ti in range(NTT):
                    g.wait_ge(dvx.h, ti + 1)
                    if ti >= 1:
                        g.wait_ge(wb1.h, 16 * ti)
                    g.dma_start(x1T3[:, :, ti * TT:(ti + 1) * TT],
                                x1t[:, :].rearrange("p (c t) -> p c t", c=KC)
                                ).then_inc(wb1.h, wb1.inc(16))
                for r, row in enumerate([V_M1, V_RSTD1, V_POTV, V_POTG]):
                    g.wait_ge(fin.h, 15 + 2 * r)
                    g.dma_start(
                        sc["vecs"][row:row + 1, :].rearrange(
                            "r (c t) -> (r c) t", c=32),
                        tpo[:, :]).then_inc(wbv.h, wbv.inc(16))
                g.wait_ge(fin.h, 22)
                g.dma_start(
                    sc["vecs"][V_CUM:V_CUM + 1, :].rearrange(
                        "r (c t) -> (r c) t", c=32),
                    cum32[:, :]).then_inc(wbv.h, wbv.inc(16))
                g.wait_ge(wbv.h, wbv.n)

            @block.vector
            def _(v):
                sp = mk_sp(v, jk)
                v.memset(jk[:, :], 0.0)
                v.wait_ge(ldw.h, 32)
                wp3 = wp[:, :].rearrange("p (c j) -> p c j", c=KC)
                wt3 = wt[:, :].rearrange("p (c j) -> p c j", c=KC)
                v.memset(one32[:, :], 1.0)
                v.tensor_mul(wt3[:, :, 0], wp3[:, :, 0], gv[:, 0:KC])
                v.tensor_mul(wt3[:, :, 1], wp3[:, :, 1], gv[:, 0:KC])
                v.memset(mv3[:, 0:3 * KC:3], 1.0)
                v.tensor_copy(mv3[:, 1:3 * KC:3], wt[:, 0:2 * KC:2])
                v.tensor_copy(mv3[:, 2:3 * KC:3], wt[:, 1:2 * KC:2]
                              ).then_inc(fin.h, 1)
                for ti in range(NTT):
                    j = ti % 2
                    v.wait_ge(ldx[j].h, 16 * (ti // 2 + 1))
                    v.wait_ge(ldf[j].h, 16 * (ti // 2 + 1))
                    if ti >= 1:
                        v.wait_ge(wb1.h, 16 * ti)
                        v.wait_ge(smm.h, 4 * ti)
                    x03 = x0t[j][:, :].rearrange("p (c t) -> p c t", c=KC)
                    fa3 = fat[j][:, :].rearrange("p (c t) -> p c t", c=KC)
                    x13 = x1t[:, :].rearrange("p (c t) -> p c t", c=KC)
                    sq3 = sqt[:, :].rearrange("p (c t) -> p c t", c=KC)
                    for k in range(KC):
                        v.scalar_tensor_tensor(
                            x13[:, k, :], fa3[:, k, :], 0.5, x03[:, k, :],
                            ALU.mult, ALU.add)
                    for k in range(KC):
                        inst = v.tensor_mul(sq3[:, k, :], x13[:, k, :],
                                            x13[:, k, :])
                    inst.then_inc(dvx.h, dvx.inc())
                    for g in range(4):
                        gi = 4 * ti + g
                        v.wait_ge(smm.h, gi + 1)
                        v.tensor_copy(s1tok[:, gi:gi + 1], psP[gi % 2][:, 0:1])
                        v.tensor_copy(s2tok[:, gi:gi + 1], psSq[gi % 2][:, 0:1])
                        v.tensor_copy(praw[:, 2 * gi:2 * gi + 2],
                                      psP[gi % 2][:, 1:3]).then_inc(
                            sev.h, sev.inc())
                v.tensor_scalar_mul(m1v(), s1tok[:, :], 1.0 / D)
                v.tensor_scalar_mul(tmpv(), s2tok[:, :], 1.0 / D)
                v.tensor_mul(varv(), m1v(), m1v())
                sp()
                v.tensor_sub(varv(), tmpv(), varv()).then_inc(fin.h, 1)
                v.wait_ge(fin.h, 3)
                v.tensor_scalar(tmpv(), varv(), EPS, None, ALU.add)
                sp()
                v.reciprocal(tmpv(), tmpv())
                sp()
                v.tensor_mul(tmpv(), varv(), tmpv()).then_inc(fin.h, 1)
                v.wait_ge(fin.h, 5)
                v.tensor_mul(rstd12v(), rstd1v(), tmpv())
                v.wait_ge(fin.h, 7)
                v.tensor_copy(crow[0:1, 0:2], psX[0:1, 0:2])
                v.tensor_copy(crow[0:1, 2:4], psX[0:1, 2:4])
                sp()
                v.tensor_add(crow[0:1, 2:4], crow[0:1, 2:4], bp[0:1, 0:2]
                             ).then_inc(fin.h, 1)
                v.wait_ge(fin.h, 9)
                v.tensor_copy(colb[:, 0:4], psX[:, 4:8])
                sp()
                sp()
                prV = praw[:, 0:2 * NT128:2]
                prG = praw[:, 1:2 * NT128:2]
                v.tensor_scalar_mul(potVv(), m1v(), colb[:, 0:1])
                v.tensor_scalar_mul(tmpv(), m1v(), colb[:, 1:2])
                v.tensor_sub(potVv(), prV, potVv())
                v.tensor_sub(tmpv(), prG, tmpv())
                v.tensor_mul(potVv(), potVv(), rstd12v())
                v.tensor_mul(tmpv(), tmpv(), rstd12v())
                v.tensor_scalar_add(potVv(), potVv(), colb[:, 2:3])
                v.tensor_scalar_add(tmpv(), tmpv(), colb[:, 3:4]
                                    ).then_inc(fin.h, 1)
                v.wait_ge(fin.h, 12)
                v.tensor_scalar_max(gam[:, :], tmpv(), 0.0)
                sp()
                v.tensor_add(gam[:, :], gam[:, :], abv())
                sp()
                v.tensor_scalar_add(gam[:, :], gam[:, :], GFLOOR
                                    ).then_inc(fin.h, 1)
                for r in range(4):
                    v.wait_ge(fin.h, 14 + 2 * r)
                    v.wait_ge(wbv.h, 16 * r)
                    v.tensor_copy(tpo[:, :], psT2[0:32, 0:128]).then_inc(
                        fin.h, 1)
                sp()
                v.tensor_scalar_mul(ell32[:, :], tpo[:, :], -DTC)
                v.tensor_tensor_scan(cum32[:, :], one32[:, :], ell32[:, :],
                                     0.0, ALU.mult, ALU.add).then_inc(
                    fin.h, 1)

            @block.scalar
            def _(s):
                s.wait_ge(fin.h, 2)
                s.activation(rstd1v(), varv(), AF.Ln, bias=EPS, scale=1.0)
                s.activation(rstd1v(), rstd1v(), AF.Exp, bias=0.0, scale=-0.5
                             ).then_inc(fin.h, 1)
                s.wait_ge(fin.h, 4)
                s.activation(tmpv(), tmpv(), AF.Ln, bias=EPS, scale=1.0)
                s.activation(tmpv(), tmpv(), AF.Exp, bias=0.0, scale=-0.5
                             ).then_inc(fin.h, 1)
                s.wait_ge(fin.h, 10)
                s.activation(abv(), tmpv(), AF.Abs).then_inc(fin.h, 1)
                s.activation(abv(), abv(), AF.Exp, bias=0.0, scale=-1.0)
                s.activation(abv(), abv(), AF.Ln, bias=1.0, scale=1.0
                             ).then_inc(fin.h, 1)

            @block.tensor
            def _(t):
                t.wait_ge(fin.h, 1)
                for ti in range(NTT):
                    t.wait_ge(dvx.h, ti + 1)
                    x13 = x1t[:, :].rearrange("p (c t) -> p c t", c=KC)
                    sq3 = sqt[:, :].rearrange("p (c t) -> p c t", c=KC)
                    wt3 = wt[:, :].rearrange("p (c j) -> p c j", c=KC)
                    for g in range(4):
                        gi = 4 * ti + g
                        gs = slice(g * 128, (g + 1) * 128)
                        if gi >= 2:
                            t.wait_ge(sev.h, gi - 1)
                        for k in range(KC):
                            t.matmul(psP[gi % 2][:, 0:3], x13[:, k, gs],
                                     mv3[:, 3 * k:3 * k + 3], start=(k == 0),
                                     stop=(k == KC - 1))
                        for k in range(KC):
                            inst = t.matmul(psSq[gi % 2][:, 0:1], sq3[:, k, gs],
                                            ones_c[:, :], start=(k == 0),
                                            stop=(k == KC - 1))
                        inst.then_inc(smm.h, smm.inc())
                t.wait_ge(fin.h, 5)
                wt3 = wt[:, :].rearrange("p (c j) -> p c j", c=KC)
                wp3 = wp[:, :].rearrange("p (c j) -> p c j", c=KC)
                for k in range(KC):
                    t.matmul(psX[0:1, 0:2], ones_c[:, :], wt3[:, k, :],
                             start=(k == 0), stop=(k == KC - 1))
                for k in range(KC):
                    inst = t.matmul(psX[0:1, 2:4], bv[:, k:k + 1], wp3[:, k, :],
                                    start=(k == 0), stop=(k == KC - 1))
                inst.then_inc(fin.h, 2)
                t.wait_ge(fin.h, 8)
                for jj in range(4):
                    inst = t.matmul(psX[:, 4 + jj:5 + jj], ones_r[:, :],
                                    crow[0:1, jj:jj + 1], start=True, stop=True)
                inst.then_inc(fin.h, 1)
                t.wait_ge(fin.h, 13)
                srcs = [m1v(), rstd1v(), potVv(), gam[:, :]]
                for r, sr in enumerate(srcs):
                    if r >= 1:
                        t.wait_ge(fin.h, 13 + 2 * r)
                    t.transpose(psT2[0:32, 0:128], sr, ids[:, :]).then_inc(
                        fin.h, 1)


def emit_bk_scan(b, io, sc, per):
    nc = b.nc
    with ExitStack() as ph:
        P = ph.enter_context
        dre = P(nc.sbuf_tensor("k_dre", [32, 128], F32))
        dim = P(nc.sbuf_tensor("k_dim", [32, 128], F32))
        sfac = P(nc.sbuf_tensor("k_sf", [32, 128], F32))
        drs = P(nc.sbuf_tensor("k_drs", [32, 128], F32))
        Rf = P(nc.sbuf_tensor("k_Rf", [32, 128 * 8], F32))
        Rb = P(nc.sbuf_tensor("k_Rb", [32, 128 * 8], F32))
        sc4 = P(nc.sbuf_tensor("k_sc4", [32, 8], F32))
        sc4b = P(nc.sbuf_tensor("k_sc4b", [32, 8], F32))
        lrow = P(nc.sbuf_tensor("k_lrow", [1, 512], F32))
        srow = P(nc.sbuf_tensor("k_srow", [1, 512], F32))
        csg = P(nc.sbuf_tensor("k_csg", [1, 512], F32))
        v8 = [P(nc.sbuf_tensor(f"k_v8{j}", [1, 8], F32)) for j in range(4)]
        pscr = [P(nc.sbuf_tensor(f"k_pscr{j}", [1, 8], F32)) for j in range(2)]
        vbrow = P(nc.sbuf_tensor("k_vb", [1, 256], F32))
        vb32 = P(nc.sbuf_tensor("k_vb32", [32, 8], F32))
        ar = P(nc.sbuf_tensor("k_ar", [32, 128], F32))
        ai = P(nc.sbuf_tensor("k_ai", [32, 128], F32))
        br = P(nc.sbuf_tensor("k_br", [32, 128], F32))
        bi = P(nc.sbuf_tensor("k_bi", [32, 128], F32))
        t1 = P(nc.sbuf_tensor("k_t1", [32, 128], F32))
        t2 = P(nc.sbuf_tensor("k_t2", [32, 128], F32))
        t3 = P(nc.sbuf_tensor("k_t3", [32, 128], F32))
        gout = P(nc.sbuf_tensor("k_go", [32, 256], F32))

        ld = b.sem()
        fin = b.sem()
        wbv = b.sem()

        vrow = lambda r: sc["vecs"][r:r + 1, :].rearrange(
            "r (c t) -> (r c) t", c=32)

        def lvl1_gen(v, R, rev, s4):
            R8 = R[:, :].rearrange("p (t x) -> p t x", t=128)
            slot = (lambda ts: 127 - ts) if rev else (lambda ts: ts)
            p0 = 127 if rev else 0
            s0 = slot(0)
            yield v.tensor_mul(R8[:, s0, 0:1], dre[:, p0:p0 + 1],
                               sfac[:, p0:p0 + 1])
            yield v.tensor_mul(R8[:, s0, 1:2], dim[:, p0:p0 + 1],
                               sfac[:, p0:p0 + 1])
            yield v.tensor_scalar_mul(R8[:, s0, 2:3], sfac[:, p0:p0 + 1], -1.0)
            yield v.memset(R8[:, s0, 3:4], 0.0)
            yield v.tensor_copy(R8[:, s0, 4:5], sfac[:, p0:p0 + 1])
            yield v.memset(R8[:, s0, 5:8], 0.0)
            for ts in range(1, 128):
                pos = 127 - ts if rev else ts
                prev = R8[:, slot(ts - 1), :]
                cur = R8[:, slot(ts), :]
                dcol = dre[:, pos:pos + 1]
                icol = dim[:, pos:pos + 1]
                scol = sfac[:, pos:pos + 1]
                yield v.tensor_scalar_mul(s4[:, 0:4], prev[:, 0:4], dcol)
                yield v.tensor_scalar_mul(s4[:, 4:8], prev[:, 0:4], icol)
                yield v.tensor_sub(cur[:, 0:4:2], s4[:, 0:4:2], s4[:, 5:8:2])
                yield v.tensor_sub(cur[:, 0:4:2], cur[:, 0:4:2], prev[:, 4:8:2])
                yield v.tensor_scalar_mul(cur[:, 0:4:2], cur[:, 0:4:2], scol)
                yield v.tensor_add(cur[:, 1:4:2], s4[:, 1:4:2], s4[:, 4:8:2])
                yield v.tensor_sub(cur[:, 1:4:2], cur[:, 1:4:2], prev[:, 5:8:2])
                yield v.tensor_scalar_mul(cur[:, 1:4:2], cur[:, 1:4:2], scol)
                yield v.tensor_scalar_mul(cur[:, 4:8], prev[:, 0:4], scol)

        def lvl2_gen(v, half, va, vb_, ps):
            rev = (half == 1)
            base = 256 * half
            yield v.memset(va[:, :], 0.0)
            yield v.memset(va[:, 0:1], 1.0)
            yield v.memset(va[:, 5:6], 1.0)
            first = 31 if rev else 0
            yield v.memset(
                vbrow[:, 8 * first + 4 * half:8 * first + 4 * half + 4], 0.0)
            yield v.memset(
                vbrow[:, 8 * first + 4 * half:8 * first + 4 * half + 1], 1.0)
            order = list(range(31, 0, -1)) if rev else list(range(31))
            cur, nxt = va, vb_
            for c in order:
                nxtc = c - 1 if rev else c + 1
                g0 = base + 8 * c
                yield v.tensor_mul(ps[:, 0:4], srow[0:1, g0:g0 + 4],
                                   cur[:, 0:4])
                yield v.reduce_sum(nxt[:, 0:1], ps[:, 0:4], AX.X)
                yield v.tensor_mul(ps[:, 0:4], lrow[0:1, g0:g0 + 4],
                                   cur[:, 4:8])
                yield v.reduce_sum(nxt[:, 1:2], ps[:, 0:4], AX.X)
                yield v.tensor_mul(ps[:, 0:4], srow[0:1, g0 + 4:g0 + 8],
                                   cur[:, 0:4])
                yield v.reduce_sum(nxt[:, 2:3], ps[:, 0:4], AX.X)
                yield v.tensor_mul(ps[:, 0:4], lrow[0:1, g0 + 4:g0 + 8],
                                   cur[:, 4:8])
                yield v.reduce_sum(nxt[:, 3:4], ps[:, 0:4], AX.X)
                yield v.tensor_scalar_mul(ps[:, 4:8], nxt[:, 0:4], -1.0)
                yield v.tensor_max(ps[:, 0:4], nxt[:, 0:4], ps[:, 4:8])
                yield v.reduce_sum(ps[:, 4:5], ps[:, 0:4], AX.X)
                yield v.reciprocal(ps[:, 5:6], ps[:, 4:5])
                yield v.tensor_scalar_mul(nxt[:, 0:4], nxt[:, 0:4],
                                          ps[0:1, 5:6])
                yield v.tensor_copy(nxt[:, 4:5], nxt[:, 1:2])
                yield v.tensor_copy(nxt[:, 5:6], nxt[:, 0:1])
                yield v.tensor_copy(nxt[:, 6:7], nxt[:, 3:4])
                yield v.tensor_copy(nxt[:, 7:8], nxt[:, 2:3])
                yield v.tensor_copy(
                    vbrow[:, 8 * nxtc + 4 * half:8 * nxtc + 4 * half + 4],
                    nxt[:, 0:4])
                cur, nxt = nxt, cur

        def interleave(*gens):
            gens = list(gens)
            last = None
            while gens:
                nxt_gens = []
                for g in gens:
                    try:
                        r = next(g)
                        if r is not None:
                            last = r
                        nxt_gens.append(g)
                    except StopIteration:
                        pass
                gens = nxt_gens
            return last

        def apply_dir(v, R, half):
            R8 = R[:, :].rearrange("p (t x) -> p t x", t=128)
            o = 4 * half
            outr = ar if half == 0 else br
            outi = ai if half == 0 else bi
            vc = lambda k: vb32[:, o + k:o + k + 1]
            v.tensor_scalar_mul(t1[:, :], R8[:, :, 0], vc(0))
            v.tensor_scalar_mul(t2[:, :], R8[:, :, 1], vc(1))
            v.tensor_sub(t1[:, :], t1[:, :], t2[:, :])
            v.tensor_scalar_mul(t2[:, :], R8[:, :, 2], vc(2))
            v.tensor_add(t1[:, :], t1[:, :], t2[:, :])
            v.tensor_scalar_mul(t2[:, :], R8[:, :, 3], vc(3))
            v.tensor_sub(t1[:, :], t1[:, :], t2[:, :])
            v.tensor_scalar_mul(t2[:, :], R8[:, :, 0], vc(1))
            v.tensor_scalar_mul(t3[:, :], R8[:, :, 1], vc(0))
            v.tensor_add(t2[:, :], t2[:, :], t3[:, :])
            v.tensor_scalar_mul(t3[:, :], R8[:, :, 2], vc(3))
            v.tensor_add(t2[:, :], t2[:, :], t3[:, :])
            v.tensor_scalar_mul(t3[:, :], R8[:, :, 3], vc(2))
            v.tensor_add(t2[:, :], t2[:, :], t3[:, :])
            v.tensor_scalar_mul(outr[:, :], R8[:, :, 4], vc(0))
            v.tensor_scalar_mul(t3[:, :], R8[:, :, 5], vc(1))
            v.tensor_sub(outr[:, :], outr[:, :], t3[:, :])
            v.tensor_scalar_mul(t3[:, :], R8[:, :, 6], vc(2))
            v.tensor_add(outr[:, :], outr[:, :], t3[:, :])
            v.tensor_scalar_mul(t3[:, :], R8[:, :, 7], vc(3))
            v.tensor_sub(outr[:, :], outr[:, :], t3[:, :])
            v.tensor_scalar_mul(outi[:, :], R8[:, :, 4], vc(1))
            v.tensor_scalar_mul(t3[:, :], R8[:, :, 5], vc(0))
            v.tensor_add(outi[:, :], outi[:, :], t3[:, :])
            v.tensor_scalar_mul(t3[:, :], R8[:, :, 6], vc(3))
            v.tensor_add(outi[:, :], outi[:, :], t3[:, :])
            v.tensor_scalar_mul(t3[:, :], R8[:, :, 7], vc(2))
            v.tensor_add(outi[:, :], outi[:, :], t3[:, :])
            v.tensor_mul(t3[:, :], outr[:, :], outr[:, :])
            v.tensor_mul(drs[:, :], outi[:, :], outi[:, :])
            v.tensor_add(t3[:, :], t3[:, :], drs[:, :])
            v.reciprocal(t3[:, :], t3[:, :])
            v.tensor_mul(drs[:, :], t1[:, :], outr[:, :])
            v.tensor_mul(sfac[:, :], t2[:, :], outi[:, :])
            v.tensor_add(drs[:, :], drs[:, :], sfac[:, :])
            v.tensor_mul(sfac[:, :], t2[:, :], outr[:, :])
            v.tensor_mul(t1[:, :], t1[:, :], outi[:, :])
            v.tensor_sub(sfac[:, :], sfac[:, :], t1[:, :])
            v.tensor_mul(outr[:, :], drs[:, :], t3[:, :])
            v.tensor_mul(outi[:, :], sfac[:, :], t3[:, :])

        with nc.Block("bkscan") as block:
            @block.sync
            def _(s):
                s.dma_start(dre[:, :], vrow(V_POTV)).then_inc(ld.h, ld.inc(16))
                s.dma_start(dim[:, :], vrow(V_POTG)).then_inc(ld.h, ld.inc(16))
                s.dma_start(csg[:, :], io["csign"]).then_inc(ld.h, ld.inc(16))

            @block.gpsimd
            def _(g):
                g.wait_ge(fin.h, 3)
                g.dma_start(sc["vecs"][24:25, 0:256].rearrange(
                    "r (p x) -> (r p) x", p=32),
                    Rf[:, :].rearrange("p (t x) -> p t x", t=128)[:, 127, :]
                ).then_inc(wbv.h, wbv.inc(16))
                g.wait_ge(wbv.h, 16)
                g.dma_start(sc["vecs"][25:26, 0:256].rearrange(
                    "r (p x) -> (r p) x", p=32),
                    Rb[:, :].rearrange("p (t x) -> p t x", t=128)[:, 0, :]
                ).then_inc(wbv.h, wbv.inc(16))
                g.wait_ge(wbv.h, 32)
                g.dma_start(lrow[0:1, 0:256], sc["vecs"][24:25, 0:256]
                            ).then_inc(wbv.h, wbv.inc(16))
                g.wait_ge(wbv.h, 48)
                g.dma_start(lrow[0:1, 256:512], sc["vecs"][25:26, 0:256]
                            ).then_inc(wbv.h, wbv.inc(16))
                g.wait_ge(wbv.h, 64)
                g.nop().then_inc(fin.h, 1)
                g.wait_ge(fin.h, 5)
                g.dma_start(sc["vecs"][24:25, 0:256], vbrow[0:1, 0:256]
                            ).then_inc(wbv.h, wbv.inc(16))
                g.wait_ge(wbv.h, 80)
                g.dma_start(vb32[:, :], sc["vecs"][24:25, 0:256].rearrange(
                    "r (c x) -> (r c) x", c=32)).then_inc(wbv.h, wbv.inc(16))
                g.wait_ge(wbv.h, 96)
                g.nop().then_inc(fin.h, 1)
                g.wait_ge(fin.h, 7)
                g.dma_start(vrow(V_BKRE), gout[:, 0:128]).then_inc(
                    wbv.h, wbv.inc(16))
                g.wait_ge(wbv.h, 112)
                g.dma_start(vrow(V_BKIM), gout[:, 128:256]).then_inc(
                    wbv.h, wbv.inc(16))
                g.wait_ge(wbv.h, wbv.n)

            @block.vector
            def _(v):
                v.wait_ge(ld.h, 48)
                v.tensor_scalar_add(dre[:, :], dre[:, :], 2.0)
                v.tensor_scalar_mul(dim[:, :], dim[:, :], -1.0)
                v.tensor_mul(t1[:, :], dre[:, :], dre[:, :])
                v.tensor_mul(t2[:, :], dim[:, :], dim[:, :])
                v.tensor_add(t1[:, :], t1[:, :], t2[:, :])
                v.tensor_scalar_add(t1[:, :], t1[:, :], 1.0
                                    ).then_inc(fin.h, 1)
                v.wait_ge(fin.h, 2)
                interleave(lvl1_gen(v, Rf, False, sc4),
                           lvl1_gen(v, Rb, True, sc4b)
                           ).then_inc(fin.h, 1)
                v.wait_ge(fin.h, 4)
                v.tensor_mul(srow[:, :], lrow[:, :], csg[:, :])
                interleave(lvl2_gen(v, 0, v8[0], v8[1], pscr[0]),
                           lvl2_gen(v, 1, v8[2], v8[3], pscr[1])
                           ).then_inc(fin.h, 1)
                v.wait_ge(fin.h, 6)
                apply_dir(v, Rf, 0)
                apply_dir(v, Rb, 1)
                v.tensor_add(t1[:, :], ar[:, :], br[:, :])
                v.tensor_sub(t1[:, :], t1[:, :], dre[:, :])
                v.tensor_add(t2[:, :], ai[:, :], bi[:, :])
                v.tensor_sub(t2[:, :], t2[:, :], dim[:, :])
                v.tensor_mul(t3[:, :], t1[:, :], t1[:, :])
                v.tensor_mul(drs[:, :], t2[:, :], t2[:, :])
                v.tensor_add(t3[:, :], t3[:, :], drs[:, :])
                v.reciprocal(t3[:, :], t3[:, :])
                v.tensor_mul(gout[:, 0:128], t1[:, :], t3[:, :])
                v.tensor_mul(gout[:, 128:256], t2[:, :], t3[:, :])
                v.tensor_scalar_mul(gout[:, 128:256], gout[:, 128:256], -1.0
                                    ).then_inc(fin.h, 1)

            @block.scalar
            def _(s):
                s.wait_ge(fin.h, 1)
                s.activation(t1[:, :], t1[:, :], AF.Ln)
                s.activation(sfac[:, :], t1[:, :], AF.Exp, bias=0.0,
                             scale=-0.5).then_inc(fin.h, 1)


def emit_attn_core(b, io, sc, per):
    nc = b.nc
    x1T3, y1T3, y2T3 = fm(sc["x1T"]), fm(sc["y1T"]), fm(sc["y2T"])
    with ExitStack() as outer:
        O_ = outer.enter_context
        qkvT = O_(nc.sbuf_tensor("a_qkv", [128, 12 * T], BF16))
        qkvT3 = qkvT[:, :].rearrange("p (c t) -> p c t", c=12)

        _emit_p5a(b, io, sc, per, x1T3, y1T3)
        b.new_phase()
        _emit_p5b(b, io, sc, per, y1T3, qkvT3)
        b.new_phase()
        _emit_p6(b, io, sc, per, qkvT3, y1T3, y2T3)


def _emit_p5a(b, io, sc, per, x1T3, y1T3):
    nc = b.nc
    ones_r, ones_c, bbkc, ids = (per["ones_r"], per["ones_c"], per["bbkc"],
                                 per["ids"])
    with ExitStack() as ph:
        P = ph.enter_context
        x1t = [P(nc.sbuf_tensor(f"a_x1{j}", [128, KC * TT], F32))
               for j in range(2)]
        y1t = P(nc.sbuf_tensor("a_y1", [128, KC * TT], F32))
        sqt = P(nc.sbuf_tensor("a_sq", [128, KC * TT], F32))
        rowm = [P(nc.sbuf_tensor(f"a_rm{j}", [1, TT], F32)) for j in range(2)]
        rowr = [P(nc.sbuf_tensor(f"a_rr{j}", [1, TT], F32)) for j in range(2)]
        bkT = [P(nc.sbuf_tensor(f"a_bk{j}", [2, TT], F32)) for j in range(2)]
        wbk = P(nc.sbuf_tensor("a_wbk", [2, D], F32))
        s1tok = P(nc.sbuf_tensor("a_s1", [128, NT128], F32))
        s2tok = P(nc.sbuf_tensor("a_s2", [128, NT128], F32))
        stt = P(nc.sbuf_tensor("a_stt", [128, 4 * NT128], F32))
        tpo = P(nc.sbuf_tensor("a_tpo", [32, 128], F32))
        jk = P(nc.sbuf_tensor("a_jk", [128, 2], F32))
        psB1 = P(nc.psum_tensor("a_pB1", [128, 512], F32))
        psB2 = P(nc.psum_tensor("a_pB2", [128, 512], F32))
        psY = [P(nc.psum_tensor(f"a_pY{j}", [128, 512], F32)) for j in range(2)]
        psSum = [P(nc.psum_tensor(f"a_pS{j}", [128, 512], F32))
                 for j in range(2)]
        psT2 = P(nc.psum_tensor("a_pT", [128, 512], F32))

        ldx = [b.sem() for _ in range(2)]
        ldr = b.sem()
        ldr2 = [b.sem() for _ in range(2)]
        bcd = b.sem()
        ykd = b.sem()
        dvy = b.sem()
        sqd = b.sem()
        smm = b.sem()
        sev = b.sem()
        wb1 = b.sem()
        fin = b.sem()
        wbv = b.sem()

        m1v = lambda: stt[:, 0:32]
        varv = lambda: stt[:, 32:64]
        rstdv = lambda: stt[:, 64:96]
        tmpv = lambda: stt[:, 96:128]

        with nc.Block("p5a") as block:
            @block.sync
            def _(s):
                s.dma_start(wbk[:, :], io["w_bk"]).then_inc(ldr.h, ldr.inc(16))
                for ti in range(NTT):
                    j = ti % 2
                    ts_ = slice(ti * TT, (ti + 1) * TT)
                    if ti >= 2:
                        s.wait_ge(dvy.h, 8 * (ti - 1))
                        s.wait_ge(bcd.h, ti - 1)
                    s.dma_start(
                        x1t[j][:, :].rearrange("p (c t) -> p c t", c=KC),
                        x1T3[:, :, ts_]).then_inc(ldx[j].h, ldx[j].inc(16))
                    s.dma_start(rowm[j][:, :], sc["vecs"][V_M1:V_M1 + 1, ts_]
                                ).then_inc(ldr2[j].h, ldr2[j].inc(16))
                    s.dma_start(rowr[j][:, :],
                                sc["vecs"][V_RSTD1:V_RSTD1 + 1, ts_]
                                ).then_inc(ldr2[j].h, ldr2[j].inc(16))
                    s.dma_start(bkT[j][0:1, :],
                                sc["vecs"][V_BKRE:V_BKRE + 1, ts_]
                                ).then_inc(ldr2[j].h, ldr2[j].inc(16))
                    s.dma_start(bkT[j][1:2, :],
                                sc["vecs"][V_BKIM:V_BKIM + 1, ts_]
                                ).then_inc(ldr2[j].h, ldr2[j].inc(16))

            @block.tensor
            def _(t):
                t.wait_ge(ldr.h, 16)
                for ti in range(NTT):
                    j = ti % 2
                    t.wait_ge(ldr2[j].h, 64 * (ti // 2 + 1))
                    if ti >= 1:
                        t.wait_ge(dvy.h, 8 * ti)
                    t.matmul(psB1[:, :], ones_r[:, :], rowm[j][0:1, :],
                             start=True, stop=True)
                    t.matmul(psB2[:, :], ones_r[:, :], rowr[j][0:1, :],
                             start=True, stop=True).then_inc(bcd.h, bcd.inc())
                    for k in range(KC):
                        gi = 8 * ti + k
                        if gi >= 2:
                            t.wait_ge(dvy.h, gi - 1)
                        t.matmul(psY[gi % 2][:, :],
                                 wbk[:, k * 128:(k + 1) * 128],
                                 bkT[j][:, :], start=True, stop=True
                                 ).then_inc(ykd.h, ykd.inc())
                    t.wait_ge(dvy.h, 8 * (ti + 1))
                    t.wait_ge(sqd.h, ti + 1)
                    y13 = y1t[:, :].rearrange("p (c t) -> p c t", c=KC)
                    sq3 = sqt[:, :].rearrange("p (c t) -> p c t", c=KC)
                    for g in range(4):
                        gi4 = 4 * ti + g
                        gs = slice(g * 128, (g + 1) * 128)
                        if gi4 >= 2:
                            t.wait_ge(sev.h, gi4 - 1)
                        for k in range(KC):
                            t.matmul(psSum[gi4 % 2][:, 0:1], y13[:, k, gs],
                                     ones_c[:, :], start=(k == 0),
                                     stop=(k == KC - 1))
                        for k in range(KC):
                            inst = t.matmul(psSum[gi4 % 2][:, 1:2],
                                            sq3[:, k, gs], ones_c[:, :],
                                            start=(k == 0), stop=(k == KC - 1))
                        inst.then_inc(smm.h, smm.inc())
                t.wait_ge(fin.h, 3)
                t.transpose(psT2[0:32, 0:128], m1v(), ids[:, :]).then_inc(
                    fin.h, 1)
                t.wait_ge(fin.h, 5)
                t.transpose(psT2[0:32, 0:128], rstdv(), ids[:, :]).then_inc(
                    fin.h, 1)

            @block.vector
            def _(v):
                v.memset(jk[:, :], 0.0)
                for ti in range(NTT):
                    j = ti % 2
                    v.wait_ge(ldx[j].h, 16 * (ti // 2 + 1))
                    v.wait_ge(bcd.h, ti + 1)
                    if ti >= 1:
                        v.wait_ge(wb1.h, 16 * ti)
                        v.wait_ge(smm.h, 4 * ti)
                        v.wait_ge(sqd.h, ti)
                    x13 = x1t[j][:, :].rearrange("p (c t) -> p c t", c=KC)
                    y13 = y1t[:, :].rearrange("p (c t) -> p c t", c=KC)
                    sq3 = sqt[:, :].rearrange("p (c t) -> p c t", c=KC)
                    for k in range(KC):
                        gi = 8 * ti + k
                        v.wait_ge(ykd.h, gi + 1)
                        v.tensor_sub(y13[:, k, :], x13[:, k, :], psB1[:, :])
                        v.tensor_mul(y13[:, k, :], y13[:, k, :], psB2[:, :])
                        v.scalar_tensor_tensor(
                            y13[:, k, :], psY[gi % 2][:, :], bbkc[:, k:k + 1],
                            y13[:, k, :], ALU.add, ALU.add
                        ).then_inc(dvy.h, dvy.inc())
                    for g in range(4):
                        gi4 = 4 * ti + g
                        v.wait_ge(smm.h, gi4 + 1)
                        v.tensor_copy(s1tok[:, gi4:gi4 + 1],
                                      psSum[gi4 % 2][:, 0:1])
                        v.tensor_copy(s2tok[:, gi4:gi4 + 1],
                                      psSum[gi4 % 2][:, 1:2]).then_inc(
                            sev.h, sev.inc())
                v.tensor_scalar_mul(m1v(), s1tok[:, :], 1.0 / D)
                v.tensor_scalar_mul(tmpv(), s2tok[:, :], 1.0 / D)
                v.tensor_mul(varv(), m1v(), m1v())
                v.tensor_scalar_mul(jk[:, 0:1], jk[:, 1:2], 1.0)
                v.tensor_sub(varv(), tmpv(), varv()).then_inc(fin.h, 1)
                v.wait_ge(fin.h, 2)
                v.nop().then_inc(fin.h, 1)
                v.wait_ge(fin.h, 4)
                v.tensor_copy(tpo[:, :], psT2[0:32, 0:128]).then_inc(
                    fin.h, 1)
                v.wait_ge(fin.h, 6)
                v.wait_ge(wbv.h, 16)
                v.tensor_copy(tpo[:, :], psT2[0:32, 0:128]).then_inc(
                    fin.h, 1)

            @block.scalar
            def _(s):
                y13 = y1t[:, :].rearrange("p (c t) -> p c t", c=KC)
                sq3 = sqt[:, :].rearrange("p (c t) -> p c t", c=KC)
                for ti in range(NTT):
                    s.wait_ge(dvy.h, 8 * (ti + 1))
                    if ti >= 1:
                        s.wait_ge(smm.h, 4 * ti)
                    for k in range(KC):
                        inst = s.activation(sq3[:, k, :], y13[:, k, :],
                                            AF.Square)
                    inst.then_inc(sqd.h, sqd.inc())
                s.wait_ge(fin.h, 1)
                s.activation(rstdv(), varv(), AF.Ln, bias=EPS, scale=1.0)
                s.activation(rstdv(), rstdv(), AF.Exp, bias=0.0, scale=-0.5
                             ).then_inc(fin.h, 1)

            @block.gpsimd
            def _(g):
                for ti in range(NTT):
                    g.wait_ge(dvy.h, 8 * (ti + 1))
                    if ti >= 1:
                        g.wait_ge(wb1.h, 16 * ti)
                    g.dma_start(y1T3[:, :, ti * TT:(ti + 1) * TT],
                                y1t[:, :].rearrange("p (c t) -> p c t", c=KC)
                                ).then_inc(wb1.h, wb1.inc(16))
                for r, row in enumerate([V_MA, V_RSTDA]):
                    g.wait_ge(fin.h, 5 + 2 * r)
                    g.dma_start(sc["vecs"][row:row + 1, :].rearrange(
                        "r (c t) -> (r c) t", c=32),
                        tpo[:, :]).then_inc(wbv.h, wbv.inc(16))
                g.wait_ge(wbv.h, wbv.n)


def _emit_p5b(b, io, sc, per, y1T3, qkvT3):
    nc = b.nc
    ones_r, gv, bv, bqkvc = (per["ones_r"], per["gv"], per["bv"],
                             per["bqkvc"])
    with ExitStack() as ph:
        P = ph.enter_context
        y1b = [P(nc.sbuf_tensor(f"b_y1{j}", [128, KC * TT], F32))
               for j in range(2)]
        xnt = P(nc.sbuf_tensor("b_xn", [128, KC * TT], BF16))
        wq = P(nc.sbuf_tensor("b_wq", [128, KC * 1536], BF16))
        rowm = [P(nc.sbuf_tensor(f"b_rm{j}", [1, TT], F32)) for j in range(2)]
        rowr = [P(nc.sbuf_tensor(f"b_rr{j}", [1, TT], F32)) for j in range(2)]
        psB1 = P(nc.psum_tensor("b_pB1", [128, 512], F32))
        psB2 = P(nc.psum_tensor("b_pB2", [128, 512], F32))
        psQ = [P(nc.psum_tensor(f"b_pQ{j}", [128, 512], F32)) for j in range(2)]
        wq3 = wq[:, :].rearrange("p (c n) -> p c n", c=KC)

        ldx = [b.sem() for _ in range(2)]
        ldr2 = [b.sem() for _ in range(2)]
        ldw = b.sem()
        bcd = b.sem()
        dvx = b.sem()
        qmm = b.sem()
        qev = b.sem()

        with nc.Block("p5b") as block:
            @block.sync
            def _(s):
                for ti in range(NTT):
                    j = ti % 2
                    ts_ = slice(ti * TT, (ti + 1) * TT)
                    if ti >= 2:
                        s.wait_ge(dvx.h, ti - 1)
                        s.wait_ge(bcd.h, ti - 1)
                    s.dma_start(
                        y1b[j][:, :].rearrange("p (c t) -> p c t", c=KC),
                        y1T3[:, :, ts_]).then_inc(ldx[j].h, ldx[j].inc(16))
                    s.dma_start(rowm[j][:, :], sc["vecs"][V_MA:V_MA + 1, ts_]
                                ).then_inc(ldr2[j].h, ldr2[j].inc(16))
                    s.dma_start(rowr[j][:, :],
                                sc["vecs"][V_RSTDA:V_RSTDA + 1, ts_]
                                ).then_inc(ldr2[j].h, ldr2[j].inc(16))

            @block.gpsimd
            def _(g):
                g.dma_start(wq3, io["w_qkv"].rearrange(
                    "(c p) n -> p c n", p=128)).then_inc(ldw.h, ldw.inc(16))

            @block.tensor
            def _(t):
                t.wait_ge(ldw.h, 16)
                for ti in range(NTT):
                    j = ti % 2
                    ts_ = slice(ti * TT, (ti + 1) * TT)
                    t.wait_ge(ldr2[j].h, 32 * (ti // 2 + 1))
                    if ti >= 1:
                        t.wait_ge(dvx.h, ti)
                    t.matmul(psB1[:, :], ones_r[:, :], rowm[j][0:1, :],
                             start=True, stop=True)
                    t.matmul(psB2[:, :], ones_r[:, :], rowr[j][0:1, :],
                             start=True, stop=True).then_inc(bcd.h, bcd.inc())
                    t.wait_ge(dvx.h, ti + 1)
                    xn3 = xnt[:, :].rearrange("p (c t) -> p c t", c=KC)
                    for n in range(12):
                        gi = 12 * ti + n
                        if gi >= 2:
                            t.wait_ge(qev.h, gi - 1)
                        for k in range(KC):
                            inst = t.matmul(
                                psQ[gi % 2][:, :],
                                wq3[:, k, n * 128:(n + 1) * 128],
                                xn3[:, k, :], start=(k == 0),
                                stop=(k == KC - 1))
                        inst.then_inc(qmm.h, qmm.inc())

            @block.vector
            def _(v):
                for ti in range(NTT):
                    j = ti % 2
                    v.wait_ge(ldx[j].h, 16 * (ti // 2 + 1))
                    v.wait_ge(bcd.h, ti + 1)
                    if ti >= 1:
                        v.wait_ge(qmm.h, 12 * ti)
                    y13 = y1b[j][:, :].rearrange("p (c t) -> p c t", c=KC)
                    xn3 = xnt[:, :].rearrange("p (c t) -> p c t", c=KC)
                    for k in range(KC):
                        v.tensor_sub(y13[:, k, :], y13[:, k, :], psB1[:, :])
                        v.tensor_mul(y13[:, k, :], y13[:, k, :], psB2[:, :])
                        inst = v.tensor_scalar(
                            xn3[:, k, :], y13[:, k, :],
                            gv[:, KC + k:KC + k + 1], bv[:, KC + k:KC + k + 1],
                            ALU.mult, ALU.add)
                    inst.then_inc(dvx.h, dvx.inc())

            @block.scalar
            def _(s):
                for ti in range(NTT):
                    ts_ = slice(ti * TT, (ti + 1) * TT)
                    for n in range(12):
                        gi = 12 * ti + n
                        s.wait_ge(qmm.h, gi + 1)
                        s.activation(
                            qkvT3[:, n, ts_], psQ[gi % 2][:, :], AF.Identity,
                            bias=bqkvc[:, n:n + 1], scale=1.0
                        ).then_inc(qev.h, qev.inc())


def _emit_p6(b, io, sc, per, qkvT3, y1T3, y2T3):
    nc = b.nc
    ones_r, ids, idsb, trileta, boutc = (per["ones_r"], per["ids"],
                                         per["idsb"], per["trileta"],
                                         per["boutc"])
    with ExitStack() as ph:
        P = ph.enter_context
        cumr = P(nc.sbuf_tensor("h_cum", [1, T], F32))
        epr = P(nc.sbuf_tensor("h_ep", [1, T], F32))
        scr = P(nc.sbuf_tensor("h_sc", [1, 32], F32))
        one1 = P(nc.sbuf_tensor("h_one1", [1, 1], F32))
        Dm = P(nc.sbuf_tensor("h_Dm", [128, 128], F32))
        kcol = P(nc.sbuf_tensor("h_kc", [128, 1], F32))
        epcol = P(nc.sbuf_tensor("h_epc", [128, 1], F32))
        sccol = P(nc.sbuf_tensor("h_scc", [128, 1], F32))
        ktm = P(nc.sbuf_tensor("h_ktm", [128, 128], BF16))
        vtm = P(nc.sbuf_tensor("h_vtm", [128, 128], BF16))
        qtm = P(nc.sbuf_tensor("h_qtm", [128, 128], BF16))
        qfm = P(nc.sbuf_tensor("h_qfm", [128, 128], BF16))
        Pt = [P(nc.sbuf_tensor(f"h_Pt{j}", [128, 128], BF16)) for j in range(2)]
        Wst = P(nc.sbuf_tensor("h_W", [128, 4 * 64], F32))
        Wb = P(nc.sbuf_tensor("h_Wb", [128, 64], BF16))
        jk = P(nc.sbuf_tensor("h_jk", [128, 2], F32))
        oc = [P(nc.sbuf_tensor(f"h_oc{j}", [64, 8 * 128], BF16))
              for j in range(2)]
        wo = P(nc.sbuf_tensor("h_wo", [64, 8 * D], BF16))
        y1s = [P(nc.sbuf_tensor(f"h_y1{j}", [128, KC * 128], F32))
               for j in range(2)]
        y2s = [P(nc.sbuf_tensor(f"h_y2{j}", [128, KC * 128], F32))
               for j in range(2)]
        pB1 = P(nc.psum_tensor("h_B1", [128, 1024], BF16))
        pB2 = P(nc.psum_tensor("h_B2", [128, 1024], BF16))
        pB3 = P(nc.psum_tensor("h_B3", [128, 1024], BF16))
        pF1 = P(nc.psum_tensor("h_F1", [128, 512], F32))
        pF2 = P(nc.psum_tensor("h_F2", [128, 512], F32))
        pDX = P(nc.psum_tensor("h_DX", [128, 512], F32))
        pW = [P(nc.psum_tensor(f"h_pW{j}", [128, 512], F32)) for j in range(2)]

        ldr = b.sem()
        ldw = b.sem()
        ldy = [b.sem() for _ in range(2)]
        acr = b.sem()
        pe6 = b.sem()
        dv6 = b.sem()
        wb2 = b.sem()
        Wst3 = Wst[:, :].rearrange("p (c x) -> p c x", c=4)
        wo3 = wo[:, :].rearrange("p (c n) -> p c n", c=8)

        PEC = 45

        def pe_cnt(j, c2, e):
            return PEC * j + (1 if e == 0 else 1 + 9 * c2 + e)

        def pe_wout(j, k):
            return PEC * j + 37 + k + 1

        def dv_cnt(j, c2, e):
            return 1 + PEC * j + (1 if e == 0 else 1 + 9 * c2 + e)

        def dv_wout(j, k):
            return 1 + PEC * j + 37 + k + 1

        def prev_pair(j, c2):
            return (j, c2 - 1) if c2 > 0 else (j - 1, 3)

        with nc.Block("p6") as block:
            @block.sync
            def _(s):
                s.dma_start(cumr[:, :], sc["vecs"][V_CUM:V_CUM + 1, :]
                            ).then_inc(ldr.h, ldr.inc(16))
                s.dma_start(scr[:, :], sc["vecs"][V_CUM:V_CUM + 1, 127:T:128],
                            allow_slow_non_contiguous=True
                            ).then_inc(ldr.h, ldr.inc(16))
                for j2 in range(NHCH):
                    sl = j2 % 2
                    if j2 >= 2:
                        s.wait_ge(dv6.h, dv_wout(j2 - 2, 7))
                    s.dma_start(
                        y1s[sl][:, :].rearrange("p (c t) -> p c t", c=KC),
                        y1T3[:, :, j2 * 128:(j2 + 1) * 128]
                    ).then_inc(ldy[sl].h, ldy[sl].inc(16))

            @block.gpsimd
            def _(g):
                g.dma_start(wo3, io["w_out"].rearrange("(c p) n -> p c n",
                                                       p=64)
                            ).then_inc(ldw.h, ldw.inc(16))
                for j2 in range(NHCH):
                    sl = j2 % 2
                    g.wait_ge(dv6.h, dv_wout(j2, 7))
                    if j2 >= 2:
                        g.wait_ge(wb2.h, 16 * (j2 - 1))
                    g.dma_start(y2T3[:, :, j2 * 128:(j2 + 1) * 128],
                                y2s[sl][:, :].rearrange("p (c t) -> p c t",
                                                        c=KC)
                                ).then_inc(wb2.h, wb2.inc(16))
                g.wait_ge(wb2.h, wb2.n)

            @block.scalar
            def _(s):
                s.wait_ge(ldr.h, 32)
                s.activation(epr[:, :], cumr[:, :], AF.Exp)
                s.activation(cumr[:, :], cumr[:, :], AF.Exp, bias=0.0,
                             scale=-1.0)
                s.activation(scr[:, :], scr[:, :], AF.Exp).then_inc(
                    acr.h, acr.inc())

            enr = cumr

            @block.tensor
            def _(t):
                t.wait_ge(acr.h, 1)
                t.wait_ge(dv6.h, 1)
                t.wait_ge(ldw.h, 16)
                for j in range(NHCH):
                    js = slice(j * HCH, (j + 1) * HCH)
                    if j >= 1:
                        t.wait_ge(dv6.h, dv_cnt(j - 1, 0, 0))
                    t.matmul(pDX[:, 0:128], enr[0:1, js], epr[0:1, js],
                             start=True, stop=True)
                    t.matmul(pDX[:, 128:129], enr[0:1, js], scr[0:1, j:j + 1],
                             start=True, stop=True)
                    t.matmul(pDX[:, 129:130], ones_r[:, :], scr[0:1, j:j + 1],
                             start=True, stop=True)
                    t.matmul(pDX[:, 130:131], epr[0:1, js], one1[0:1, 0:1],
                             start=True, stop=True).then_inc(pe6.h, 1)
                    for c2 in range(4):
                        pj, pc = prev_pair(j, c2)
                        kview = qkvT3[:, 4 + c2, js]
                        vview = qkvT3[:, 8 + c2, js]
                        qview = qkvT3[:, c2, js]
                        if pj >= 0:
                            t.wait_ge(dv6.h, dv_cnt(pj, pc, 4))
                        t.transpose(pB1[:, 0:128], kview, idsb[:, :]
                                    ).then_inc(pe6.h, 1)
                        if pj >= 0:
                            t.wait_ge(dv6.h, dv_cnt(pj, pc, 2))
                        t.transpose(pB2[:, 0:128], vview, idsb[:, :]
                                    ).then_inc(pe6.h, 1)
                        if pj >= 0:
                            t.wait_ge(dv6.h, dv_cnt(pj, pc, 3))
                        t.transpose(pB3[:, 0:128], qview, idsb[:, :]
                                    ).then_inc(pe6.h, 1)
                        t.wait_ge(dv6.h, dv_cnt(j, c2, 3))
                        t.transpose(pB1[:, 0:128], qtm[:, :], idsb[:, :]
                                    ).then_inc(pe6.h, 1)
                        if pj >= 0:
                            t.wait_ge(dv6.h, dv_cnt(pj, pc, 9))
                        t.matmul(pF1[:, 0:128], kview[0:64, :], qview[0:64, :],
                                 start=True, stop=True).then_inc(pe6.h, 1)
                        if pj >= 0:
                            t.wait_ge(dv6.h, dv_cnt(pj, pc, 8))
                        t.matmul(pF2[:, 0:128], kview[64:128, :],
                                 qview[64:128, :], start=True, stop=True
                                 ).then_inc(pe6.h, 1)
                        t.wait_ge(dv6.h, dv_cnt(j, c2, 5))
                        t.matmul(pF1[:, 0:128], ktm[:, :], vtm[:, :],
                                 start=True, stop=True).then_inc(pe6.h, 1)
                        t.wait_ge(dv6.h, dv_cnt(j, c2, 7))
                        t.matmul(pF2[0:64, 0:128], Wb[0:64, :], qfm[0:64, :],
                                 start=True, stop=False)
                        t.matmul(pF2[0:64, 0:128], vtm[:, 0:64], Pt[0][:, :],
                                 start=False, stop=True).then_inc(pe6.h, 1)
                        t.matmul(pF1[0:64, 0:128], Wb[64:128, :],
                                 qfm[64:128, :], start=True, stop=False)
                        t.matmul(pF1[0:64, 0:128], vtm[:, 64:128], Pt[1][:, :],
                                 start=False, stop=True).then_inc(pe6.h, 1)
                    t.wait_ge(dv6.h, dv_cnt(j, 3, 9))
                    oc3 = oc[j % 2][:, :].rearrange("p (c t) -> p c t", c=8)
                    for k in range(KC):
                        if pe_wout(j, k) > 2:
                            pass
                        if k >= 2 or j >= 1:
                            kk_prev = (j, k - 2) if k >= 2 else (j - 1, k + 6)
                            t.wait_ge(dv6.h, dv_wout(*kk_prev))
                        for kk in range(8):
                            inst = t.matmul(
                                pW[k % 2][:, 0:128],
                                wo3[:, kk, k * 128:(k + 1) * 128],
                                oc3[:, kk, :], start=(kk == 0), stop=(kk == 7))
                        inst.then_inc(pe6.h, 1)

            @block.vector
            def _(v):
                v.memset(Wst[:, :], 0.0)
                v.memset(one1[:, :], 1.0)
                v.memset(jk[:, :], 0.0)
                v.tensor_scalar_mul(jk[:, 0:1], jk[:, 1:2], 1.0)
                v.tensor_scalar_mul(jk[:, 0:1], jk[:, 1:2], 1.0
                                    ).then_inc(dv6.h, 1)
                for j in range(NHCH):
                    js = slice(j * HCH, (j + 1) * HCH)
                    sl = j % 2
                    v.wait_ge(pe6.h, pe_cnt(j, 0, 0))
                    v.tensor_mul(Dm[:, :], pDX[:, 0:128], trileta[:, :])
                    v.tensor_copy(kcol[:, :], pDX[:, 128:129])
                    v.tensor_copy(sccol[:, :], pDX[:, 129:130])
                    v.tensor_copy(epcol[:, :], pDX[:, 130:131]).then_inc(
                        dv6.h, 1)
                    for c2 in range(4):
                        v.wait_ge(pe6.h, pe_cnt(j, c2, 1))
                        v.tensor_scalar_mul(ktm[:, :], pB1[:, 0:128],
                                            kcol[:, 0:1]).then_inc(dv6.h, 1)
                        v.wait_ge(pe6.h, pe_cnt(j, c2, 2))
                        v.tensor_copy(vtm[:, :], pB2[:, 0:128]).then_inc(
                            dv6.h, 1)
                        v.wait_ge(pe6.h, pe_cnt(j, c2, 3))
                        v.tensor_scalar_mul(qtm[:, :], pB3[:, 0:128],
                                            epcol[:, 0:1]).then_inc(dv6.h, 1)
                        v.wait_ge(pe6.h, pe_cnt(j, c2, 4))
                        v.tensor_copy(qfm[:, :], pB1[:, 0:128]).then_inc(
                            dv6.h, 1)
                        v.wait_ge(pe6.h, pe_cnt(j, c2, 5))
                        v.tensor_mul(Pt[0][:, :], pF1[:, 0:128], Dm[:, :]
                                     ).then_inc(dv6.h, 1)
                        v.wait_ge(pe6.h, pe_cnt(j, c2, 6))
                        v.tensor_mul(Pt[1][:, :], pF2[:, 0:128], Dm[:, :]
                                     ).then_inc(dv6.h, 1)
                        v.wait_ge(pe6.h, pe_cnt(j, c2, 7))
                        v.tensor_copy(Wb[:, :], Wst3[:, c2, :])
                        for hh in range(2):
                            bs = slice(hh * 64, hh * 64 + 64)
                            v.tensor_scalar_mul(Wst3[:, c2, :][bs, :],
                                                Wst3[:, c2, :][bs, :],
                                                sccol[bs, 0:1])
                            v.scalar_tensor_tensor(
                                Wst3[:, c2, :][bs, :],
                                pF1[bs, 0:128][:, slice(hh * 64, hh * 64 + 64)],
                                ETA, Wst3[:, c2, :][bs, :],
                                ALU.mult, ALU.add)
                        v.tensor_scalar_mul(jk[:, 0:1], jk[:, 1:2], 1.0
                                            ).then_inc(dv6.h, 1)
                        oc3 = oc[sl][:, :].rearrange("p (c t) -> p c t", c=8)
                        v.wait_ge(pe6.h, pe_cnt(j, c2, 8))
                        if j >= 2 and c2 == 0:
                            v.wait_ge(pe6.h, pe_wout(j - 2, 7))
                        v.tensor_copy(oc3[:, 2 * c2, :], pF2[0:64, 0:128]
                                      ).then_inc(dv6.h, 1)
                        v.wait_ge(pe6.h, pe_cnt(j, c2, 9))
                        v.tensor_copy(oc3[:, 2 * c2 + 1, :], pF1[0:64, 0:128]
                                      ).then_inc(dv6.h, 1)
                    v.wait_ge(ldy[sl].h, 16 * (j // 2 + 1))
                    if j >= 2:
                        v.wait_ge(wb2.h, 16 * (j - 1))
                    y13 = y1s[sl][:, :].rearrange("p (c t) -> p c t", c=KC)
                    y23 = y2s[sl][:, :].rearrange("p (c t) -> p c t", c=KC)
                    for k in range(KC):
                        v.wait_ge(pe6.h, pe_wout(j, k))
                        v.scalar_tensor_tensor(
                            y23[:, k, :], pW[k % 2][:, 0:128],
                            boutc[:, k:k + 1], y13[:, k, :],
                            ALU.add, ALU.add).then_inc(dv6.h, 1)


def emit_y2stats(b, io, sc, per):
    nc = b.nc
    ones_c, ids = per["ones_c"], per["ids"]
    y2T3 = fm(sc["y2T"])
    with ExitStack() as ph:
        P = ph.enter_context
        y2t = [P(nc.sbuf_tensor(f"w_y2{j}", [128, KC * TT], F32))
               for j in range(2)]
        sqt = P(nc.sbuf_tensor("w_sq", [128, KC * TT], F32))
        s1tok = P(nc.sbuf_tensor("w_s1", [128, NT128], F32))
        s2tok = P(nc.sbuf_tensor("w_s2", [128, NT128], F32))
        stt = P(nc.sbuf_tensor("w_stt", [128, 4 * NT128], F32))
        tpo = P(nc.sbuf_tensor("w_tpo", [32, 128], F32))
        jk = P(nc.sbuf_tensor("w_jk", [128, 2], F32))
        psSum = [P(nc.psum_tensor(f"w_pS{j}", [128, 512], F32))
                 for j in range(2)]
        psT2 = P(nc.psum_tensor("w_pT", [128, 512], F32))

        ldx = [b.sem() for _ in range(2)]
        dvq = b.sem()
        smm = b.sem()
        sev = b.sem()
        fin = b.sem()
        wbv = b.sem()

        m1v = lambda: stt[:, 0:32]
        varv = lambda: stt[:, 32:64]
        rstdv = lambda: stt[:, 64:96]
        tmpv = lambda: stt[:, 96:128]

        with nc.Block("p7b") as block:
            @block.sync
            def _(s):
                for ti in range(NTT):
                    j = ti % 2
                    if ti >= 2:
                        s.wait_ge(smm.h, 4 * (ti - 1))
                    s.dma_start(
                        y2t[j][:, :].rearrange("p (c t) -> p c t", c=KC),
                        y2T3[:, :, ti * TT:(ti + 1) * TT]
                    ).then_inc(ldx[j].h, ldx[j].inc(16))

            @block.vector
            def _(v):
                v.memset(jk[:, :], 0.0)
                for ti in range(NTT):
                    for g in range(4):
                        gi4 = 4 * ti + g
                        v.wait_ge(smm.h, gi4 + 1)
                        v.tensor_copy(s1tok[:, gi4:gi4 + 1],
                                      psSum[gi4 % 2][:, 0:1])
                        v.tensor_copy(s2tok[:, gi4:gi4 + 1],
                                      psSum[gi4 % 2][:, 1:2]).then_inc(
                            sev.h, sev.inc())
                v.tensor_scalar_mul(m1v(), s1tok[:, :], 1.0 / D)
                v.tensor_scalar_mul(tmpv(), s2tok[:, :], 1.0 / D)
                v.tensor_mul(varv(), m1v(), m1v())
                v.tensor_scalar_mul(jk[:, 0:1], jk[:, 1:2], 1.0)
                v.tensor_sub(varv(), tmpv(), varv()).then_inc(fin.h, 1)
                v.wait_ge(fin.h, 2)
                v.nop().then_inc(fin.h, 1)
                v.wait_ge(fin.h, 4)
                v.tensor_copy(tpo[:, :], psT2[0:32, 0:128]).then_inc(
                    fin.h, 1)
                v.wait_ge(fin.h, 6)
                v.wait_ge(wbv.h, 16)
                v.tensor_copy(tpo[:, :], psT2[0:32, 0:128]).then_inc(
                    fin.h, 1)

            @block.tensor
            def _(t):
                for ti in range(NTT):
                    j = ti % 2
                    t.wait_ge(dvq.h, ti + 1)
                    y23 = y2t[j][:, :].rearrange("p (c t) -> p c t", c=KC)
                    sq3 = sqt[:, :].rearrange("p (c t) -> p c t", c=KC)
                    for g in range(4):
                        gi4 = 4 * ti + g
                        gs = slice(g * 128, (g + 1) * 128)
                        if gi4 >= 2:
                            t.wait_ge(sev.h, gi4 - 1)
                        for k in range(KC):
                            t.matmul(psSum[gi4 % 2][:, 0:1], y23[:, k, gs],
                                     ones_c[:, :], start=(k == 0),
                                     stop=(k == KC - 1))
                        for k in range(KC):
                            inst = t.matmul(psSum[gi4 % 2][:, 1:2],
                                            sq3[:, k, gs], ones_c[:, :],
                                            start=(k == 0), stop=(k == KC - 1))
                        inst.then_inc(smm.h, smm.inc())
                t.wait_ge(fin.h, 3)
                t.transpose(psT2[0:32, 0:128], m1v(), ids[:, :]).then_inc(
                    fin.h, 1)
                t.wait_ge(fin.h, 5)
                t.transpose(psT2[0:32, 0:128], rstdv(), ids[:, :]).then_inc(
                    fin.h, 1)

            @block.scalar
            def _(s):
                for ti in range(NTT):
                    j = ti % 2
                    s.wait_ge(ldx[j].h, 16 * (ti // 2 + 1))
                    if ti >= 1:
                        s.wait_ge(smm.h, 4 * ti)
                    y23 = y2t[j][:, :].rearrange("p (c t) -> p c t", c=KC)
                    sq3 = sqt[:, :].rearrange("p (c t) -> p c t", c=KC)
                    for k in range(KC):
                        inst = s.activation(sq3[:, k, :], y23[:, k, :],
                                            AF.Square)
                    inst.then_inc(dvq.h, dvq.inc())
                s.wait_ge(fin.h, 1)
                s.activation(rstdv(), varv(), AF.Ln, bias=EPS, scale=1.0)
                s.activation(rstdv(), rstdv(), AF.Exp, bias=0.0, scale=-0.5
                             ).then_inc(fin.h, 1)

            @block.gpsimd
            def _(g):
                for r, row in enumerate([V_M3, V_RSTD3]):
                    g.wait_ge(fin.h, 5 + 2 * r)
                    g.dma_start(sc["vecs"][row:row + 1, :].rearrange(
                        "r (c t) -> (r c) t", c=32),
                        tpo[:, :]).then_inc(wbv.h, wbv.inc(16))
                g.wait_ge(wbv.h, wbv.n)


def emit_ffn_x2(b, io, sc, per):
    nc = b.nc
    ones_r, ones_c, gv, bv, bff1c, bff2c, ids = (
        per["ones_r"], per["ones_c"], per["gv"], per["bv"], per["bff1c"],
        per["bff2c"], per["ids"])
    y2T3, x1T3, x2T3 = fm(sc["y2T"]), fm(sc["x1T"]), fm(sc["x2T"])
    with ExitStack() as ph:
        P = ph.enter_context
        y2t = [P(nc.sbuf_tensor(f"f_y2{j}", [128, KC * TT], F32))
               for j in range(2)]
        x1t = P(nc.sbuf_tensor("f_x1", [128, KC * TT], F32))
        x2t = P(nc.sbuf_tensor("f_x2", [128, KC * TT], F32))
        sqt = P(nc.sbuf_tensor("f_sq", [128, KC * TT], F32))
        xnt = P(nc.sbuf_tensor("f_xn", [128, KC * TT], BF16))
        wf1 = [P(nc.sbuf_tensor(f"f_w1{j}", [128, 8 * D], BF16))
               for j in range(2)]
        wf2 = [P(nc.sbuf_tensor(f"f_w2{j}", [128, 8 * D], BF16))
               for j in range(2)]
        glb = [P(nc.sbuf_tensor(f"f_gl{j}", [128, 8 * TT], BF16))
               for j in range(2)]
        ffa = P(nc.sbuf_tensor("f_ffa", [128, KC * TT], F32))
        rowm = [P(nc.sbuf_tensor(f"f_rm{j}", [1, TT], F32)) for j in range(2)]
        rowr = [P(nc.sbuf_tensor(f"f_rr{j}", [1, TT], F32)) for j in range(2)]
        s1tok = P(nc.sbuf_tensor("f_s1", [128, NT128], F32))
        s2tok = P(nc.sbuf_tensor("f_s2", [128, NT128], F32))
        stt = P(nc.sbuf_tensor("f_stt", [128, 4 * NT128], F32))
        tpo = P(nc.sbuf_tensor("f_tpo", [32, 128], F32))
        jk = P(nc.sbuf_tensor("f_jk", [128, 2], F32))
        psB1 = P(nc.psum_tensor("f_pB1", [128, 512], F32))
        psB2 = P(nc.psum_tensor("f_pB2", [128, 512], F32))
        psF = [P(nc.psum_tensor(f"f_pF{j}", [128, 512], F32)) for j in range(2)]
        psC = [P(nc.psum_tensor(f"f_pC{j}", [128, 512], F32)) for j in range(2)]
        psSum = [P(nc.psum_tensor(f"f_pS{j}", [128, 512], F32))
                 for j in range(2)]


        ldy = [b.sem() for _ in range(2)]
        ldx1 = b.sem()
        ldw1 = [b.sem() for _ in range(2)]
        ldw2 = [b.sem() for _ in range(2)]
        ldr2 = [b.sem() for _ in range(2)]
        bcd = b.sem()
        dvn = b.sem()
        fmm = b.sem()
        gld = b.sem()
        cmm = b.sem()
        cev = b.sem()
        dvx = b.sem()
        smm = b.sem()
        sev = b.sem()
        wb1 = b.sem()
        fin = b.sem()
        wbv = b.sem()
        pw2 = b.sem()

        m1v = lambda: stt[:, 0:32]
        varv = lambda: stt[:, 32:64]
        rstdv = lambda: stt[:, 64:96]
        tmpv = lambda: stt[:, 96:128]

        w2r = io["w_ff2"].rearrange("(b c p) n -> b p c n", p=128, c=8)
        w1r = io["w_ff1"].rearrange("(c p) (b n) -> b p c n", p=128, b=4)

        with nc.Block("p8") as block:
            @block.sync
            def _(s):
                for ti in range(NTT):
                    j = ti % 2
                    ts_ = slice(ti * TT, (ti + 1) * TT)
                    if ti >= 2:
                        s.wait_ge(dvn.h, ti - 1)
                        s.wait_ge(dvx.h, 8 * (ti - 1))
                        s.wait_ge(bcd.h, ti - 1)
                    s.dma_start(
                        y2t[j][:, :].rearrange("p (c t) -> p c t", c=KC),
                        y2T3[:, :, ts_]).then_inc(ldy[j].h, ldy[j].inc(16))
                    if ti >= 1:
                        s.wait_ge(dvx.h, 8 * ti)
                    s.dma_start(
                        x1t[:, :].rearrange("p (c t) -> p c t", c=KC),
                        x1T3[:, :, ts_]).then_inc(ldx1.h, ldx1.inc(16))
                    s.dma_start(rowm[j][:, :], sc["vecs"][V_M3:V_M3 + 1, ts_]
                                ).then_inc(ldr2[j].h, ldr2[j].inc(16))
                    s.dma_start(rowr[j][:, :],
                                sc["vecs"][V_RSTD3:V_RSTD3 + 1, ts_]
                                ).then_inc(ldr2[j].h, ldr2[j].inc(16))

            @block.gpsimd
            def _(g):
                for ti in range(NTT):
                    for kb in range(4):
                        j = (4 * ti + kb) % 2
                        qb = 4 * ti + kb
                        if qb >= 2:
                            g.wait_ge(fmm.h, 8 * (qb - 1))
                        g.dma_start(wf1[j][:, :].rearrange(
                            "p (c n) -> p c n", c=KC), w1r[kb]
                        ).then_inc(ldw1[j].h, ldw1[j].inc(16))
                        if qb >= 2:
                            g.wait_ge(pw2.h, qb - 1)
                        g.dma_start(wf2[j][:, :].rearrange(
                            "p (c n) -> p c n", c=8), w2r[kb]
                        ).then_inc(ldw2[j].h, ldw2[j].inc(16))
                    g.wait_ge(dvx.h, 8 * (ti + 1))
                    if ti >= 1:
                        g.wait_ge(wb1.h, 16 * ti)
                    g.dma_start(x2T3[:, :, ti * TT:(ti + 1) * TT],
                                x2t[:, :].rearrange("p (c t) -> p c t", c=KC)
                                ).then_inc(wb1.h, wb1.inc(16))
                for r, row in enumerate([V_M4, V_RSTD4]):
                    g.wait_ge(fin.h, 5 + 2 * r)
                    g.dma_start(sc["vecs"][row:row + 1, :].rearrange(
                        "r (c t) -> (r c) t", c=32),
                        tpo[:, :]).then_inc(wbv.h, wbv.inc(16))
                g.wait_ge(wbv.h, wbv.n)

            @block.tensor
            def _(t):
                for ti in range(NTT):
                    j = ti % 2
                    ts_ = slice(ti * TT, (ti + 1) * TT)
                    t.wait_ge(ldr2[j].h, 32 * (ti // 2 + 1))
                    if ti >= 1:
                        t.wait_ge(dvn.h, ti)
                    t.matmul(psB1[:, :], ones_r[:, :], rowm[j][0:1, :],
                             start=True, stop=True)
                    t.matmul(psB2[:, :], ones_r[:, :], rowr[j][0:1, :],
                             start=True, stop=True).then_inc(bcd.h, bcd.inc())
                    t.wait_ge(dvn.h, ti + 1)
                    xn3 = xnt[:, :].rearrange("p (c t) -> p c t", c=KC)
                    for kb in range(4):
                        jw = (4 * ti + kb) % 2
                        qb = 4 * ti + kb
                        wf23 = wf2[jw][:, :].rearrange("p (c n) -> p c n", c=8)
                        wf13 = wf1[jw][:, :].rearrange("p (c n) -> p c n",
                                                       c=KC)
                        t.wait_ge(ldw1[jw].h, 16 * (qb // 2 + 1))
                        for nn in range(8):
                            n = 8 * kb + nn
                            gi = 32 * ti + n
                            if gi >= 2:
                                t.wait_ge(gld.h, gi - 1)
                            for k in range(KC):
                                inst = t.matmul(
                                    psF[gi % 2][:, :],
                                    wf13[:, k, nn * 128:(nn + 1) * 128],
                                    xn3[:, k, :], start=(k == 0),
                                    stop=(k == KC - 1))
                            inst.then_inc(fmm.h, fmm.inc())
                        t.wait_ge(ldw2[jw].h, 16 * (qb // 2 + 1))
                        t.wait_ge(gld.h, 32 * ti + 8 * kb + 8)
                        gl3 = glb[(4 * ti + kb) % 2][:, :].rearrange(
                            "p (c t) -> p c t", c=8)
                        for mo in range(KC):
                            ci = 32 * ti + 8 * kb + mo
                            if ci >= 2:
                                t.wait_ge(cev.h, ci - 1)
                            for k2 in range(8):
                                inst = t.matmul(
                                    psC[ci % 2][:, :],
                                    wf23[:, k2, mo * 128:(mo + 1) * 128],
                                    gl3[:, k2, :], start=(k2 == 0),
                                    stop=(k2 == 7))
                            inst.then_inc(cmm.h, cmm.inc())
                        t.nop().then_inc(pw2.h, pw2.inc())
                    t.wait_ge(dvx.h, 8 * (ti + 1))
                    x23 = x2t[:, :].rearrange("p (c t) -> p c t", c=KC)
                    sq3 = sqt[:, :].rearrange("p (c t) -> p c t", c=KC)
                    for g in range(4):
                        gi4 = 4 * ti + g
                        gs = slice(g * 128, (g + 1) * 128)
                        if gi4 >= 2:
                            t.wait_ge(sev.h, gi4 - 1)
                        for k in range(KC):
                            t.matmul(psSum[gi4 % 2][:, 0:1], x23[:, k, gs],
                                     ones_c[:, :], start=(k == 0),
                                     stop=(k == KC - 1))
                        for k in range(KC):
                            inst = t.matmul(psSum[gi4 % 2][:, 1:2],
                                            sq3[:, k, gs], ones_c[:, :],
                                            start=(k == 0), stop=(k == KC - 1))
                        inst.then_inc(smm.h, smm.inc())
                t.wait_ge(fin.h, 3)
                t.transpose(psF[0][0:32, 0:128], m1v(), ids[:, :]).then_inc(
                    fin.h, 1)
                t.wait_ge(fin.h, 5)
                t.transpose(psF[0][0:32, 0:128], rstdv(), ids[:, :]).then_inc(
                    fin.h, 1)

            @block.scalar
            def _(s):
                for ti in range(NTT):
                    for n in range(32):
                        gi = 32 * ti + n
                        s.wait_ge(fmm.h, gi + 1)
                        kb = n // 8
                        gl3 = glb[(4 * ti + kb) % 2][:, :].rearrange(
                            "p (c t) -> p c t", c=8)
                        if n % 8 == 0 and (4 * ti + kb) >= 2:
                            s.wait_ge(cmm.h, 8 * (4 * ti + kb - 1))
                        s.activation(gl3[:, n % 8, :], psF[gi % 2][:, :],
                                     AF.Gelu_apprx_tanh,
                                     bias=bff1c[:, n:n + 1], scale=1.0
                                     ).then_inc(gld.h, gld.inc())
                s.wait_ge(fin.h, 1)
                s.activation(rstdv(), varv(), AF.Ln, bias=EPS, scale=1.0)
                s.activation(rstdv(), rstdv(), AF.Exp, bias=0.0, scale=-0.5
                             ).then_inc(fin.h, 1)

            @block.vector
            def _(v):
                v.memset(jk[:, :], 0.0)
                for ti in range(NTT):
                    j = ti % 2
                    v.wait_ge(ldy[j].h, 16 * (ti // 2 + 1))
                    v.wait_ge(bcd.h, ti + 1)
                    if ti >= 1:
                        v.wait_ge(fmm.h, 32 * ti)
                    y23 = y2t[j][:, :].rearrange("p (c t) -> p c t", c=KC)
                    xn3 = xnt[:, :].rearrange("p (c t) -> p c t", c=KC)
                    for k in range(KC):
                        fa3 = ffa[:, :].rearrange("p (c t) -> p c t", c=KC)
                        v.tensor_sub(fa3[:, k, :], y23[:, k, :], psB1[:, :])
                        v.tensor_mul(fa3[:, k, :], fa3[:, k, :], psB2[:, :])
                        inst = v.tensor_scalar(
                            xn3[:, k, :], fa3[:, k, :],
                            gv[:, 2 * KC + k:2 * KC + k + 1],
                            bv[:, 2 * KC + k:2 * KC + k + 1],
                            ALU.mult, ALU.add)
                    inst.then_inc(dvn.h, dvn.inc())
                    fa3 = ffa[:, :].rearrange("p (c t) -> p c t", c=KC)
                    for kb in range(4):
                        for mo in range(KC):
                            ci = 32 * ti + 8 * kb + mo
                            v.wait_ge(cmm.h, ci + 1)
                            if kb == 0:
                                v.tensor_copy(fa3[:, mo, :], psC[ci % 2][:, :]
                                              ).then_inc(cev.h, cev.inc())
                            else:
                                v.tensor_add(fa3[:, mo, :], fa3[:, mo, :],
                                             psC[ci % 2][:, :]).then_inc(
                                    cev.h, cev.inc())
                    v.wait_ge(ldx1.h, 16 * (ti + 1))
                    if ti >= 1:
                        v.wait_ge(wb1.h, 16 * ti)
                        v.wait_ge(smm.h, 4 * ti)
                    x13 = x1t[:, :].rearrange("p (c t) -> p c t", c=KC)
                    x23 = x2t[:, :].rearrange("p (c t) -> p c t", c=KC)
                    sq3 = sqt[:, :].rearrange("p (c t) -> p c t", c=KC)
                    for k in range(KC):
                        v.scalar_tensor_tensor(
                            x23[:, k, :], fa3[:, k, :], bff2c[:, k:k + 1],
                            y23[:, k, :], ALU.add, ALU.add)
                        v.tensor_add(x23[:, k, :], x23[:, k, :], x13[:, k, :])
                        v.tensor_mul(sq3[:, k, :], x23[:, k, :], x23[:, k, :]
                                     ).then_inc(dvx.h, dvx.inc())
                    for g in range(4):
                        gi4 = 4 * ti + g
                        v.wait_ge(smm.h, gi4 + 1)
                        v.tensor_copy(s1tok[:, gi4:gi4 + 1],
                                      psSum[gi4 % 2][:, 0:1])
                        v.tensor_copy(s2tok[:, gi4:gi4 + 1],
                                      psSum[gi4 % 2][:, 1:2]).then_inc(
                            sev.h, sev.inc())
                v.tensor_scalar_mul(m1v(), s1tok[:, :], 1.0 / D)
                v.tensor_scalar_mul(tmpv(), s2tok[:, :], 1.0 / D)
                v.tensor_mul(varv(), m1v(), m1v())
                v.tensor_scalar_mul(jk[:, 0:1], jk[:, 1:2], 1.0)
                v.tensor_sub(varv(), tmpv(), varv()).then_inc(fin.h, 1)
                v.wait_ge(fin.h, 2)
                v.nop().then_inc(fin.h, 1)
                v.wait_ge(fin.h, 4)
                v.tensor_copy(tpo[:, :], psF[0][0:32, 0:128]).then_inc(
                    fin.h, 1)
                v.wait_ge(fin.h, 6)
                v.wait_ge(wbv.h, 16)
                v.tensor_copy(tpo[:, :], psF[0][0:32, 0:128]).then_inc(
                    fin.h, 1)


def emit_h3_gate(b, io, sc, per):
    nc = b.nc
    ones_r, gateT, ids = per["ones_r"], per["gateT"], per["ids"]
    x2T3, hT3 = fm(sc["x2T"]), fm(sc["hT"])
    with ExitStack() as ph:
        P = ph.enter_context
        x2t = [P(nc.sbuf_tensor(f"g_x2{j}", [128, KC * TT], F32))
               for j in range(2)]
        h3f = P(nc.sbuf_tensor("g_h3", [128, KC * TT], F32))
        h3b = P(nc.sbuf_tensor("g_hb", [128, KC * TT], BF16))
        rowm = P(nc.sbuf_tensor("g_rm", [1, T], F32))
        rowr = P(nc.sbuf_tensor("g_rr", [1, T], F32))
        gsb = P(nc.sbuf_tensor("g_gsb", [8, TT], F32))
        sctile = P(nc.sbuf_tensor("g_sc", [128, 10], F32))
        msc = P(nc.sbuf_tensor("g_msc", [128, 8], F32))
        jk = P(nc.sbuf_tensor("g_jk", [128, 2], F32))
        mst = [P(nc.sbuf_tensor(f"g_ms{j}", [8, 128], F32)) for j in range(2)]
        psB1 = P(nc.psum_tensor("g_pB1", [128, 512], F32))
        psB2 = P(nc.psum_tensor("g_pB2", [128, 512], F32))
        psG = P(nc.psum_tensor("g_pG", [128, 512], F32))
        psS = P(nc.psum_tensor("g_pS", [128, 512], F32))
        psM = P(nc.psum_tensor("g_pM", [128, 512], F32))

        ldx = [b.sem() for _ in range(2)]
        ldr = b.sem()
        bcd = b.sem()
        dvh = b.sem()
        gmm = b.sem()
        gev = b.sem()
        smm = b.sem()
        top = b.sem()
        mmk = b.sem()
        sev = b.sem()
        acr = b.sem()
        wbh = b.sem()
        wbm = b.sem()

        with nc.Block("p9") as block:
            @block.sync
            def _(s):
                s.dma_start(rowm[:, :], sc["vecs"][V_M4:V_M4 + 1, :]
                            ).then_inc(ldr.h, ldr.inc(16))
                s.dma_start(rowr[:, :], sc["vecs"][V_RSTD4:V_RSTD4 + 1, :]
                            ).then_inc(ldr.h, ldr.inc(16))
                for ti in range(NTT):
                    j = ti % 2
                    if ti >= 2:
                        s.wait_ge(dvh.h, ti - 1)
                    s.dma_start(
                        x2t[j][:, :].rearrange("p (c t) -> p c t", c=KC),
                        x2T3[:, :, ti * TT:(ti + 1) * TT]
                    ).then_inc(ldx[j].h, ldx[j].inc(16))

            @block.tensor
            def _(t):
                t.wait_ge(ldr.h, 32)
                for ti in range(NTT):
                    ts_ = slice(ti * TT, (ti + 1) * TT)
                    if ti >= 1:
                        t.wait_ge(dvh.h, ti)
                    t.matmul(psB1[:, :], ones_r[:, :], rowm[0:1, ts_],
                             start=True, stop=True)
                    t.matmul(psB2[:, :], ones_r[:, :], rowr[0:1, ts_],
                             start=True, stop=True).then_inc(bcd.h, bcd.inc())
                    t.wait_ge(dvh.h, ti + 1)
                    if ti >= 1:
                        t.wait_ge(gev.h, ti)
                    h33 = h3f[:, :].rearrange("p (c t) -> p c t", c=KC)
                    for k in range(KC):
                        inst = t.matmul(psG[0:8, :], gateT[:, k * 8:(k + 1) * 8],
                                        h33[:, k, :], start=(k == 0),
                                        stop=(k == KC - 1))
                    inst.then_inc(gmm.h, gmm.inc())
                    for g in range(4):
                        gi = 4 * ti + g
                        t.wait_ge(gev.h, ti + 1)
                        if gi >= 1:
                            t.wait_ge(top.h, 2 * gi - 1)
                        t.transpose(psS[0:128, 0:8], gsb[:, g * 128:(g + 1) * 128],
                                    ids[0:8, 0:8]).then_inc(smm.h, smm.inc())
                        t.wait_ge(top.h, 2 * gi + 2)
                        if gi >= 2:
                            t.wait_ge(sev.h, gi - 1)
                        t.transpose(psM[0:8, 0:128], msc[:, :], ids[:, :]
                                    ).then_inc(mmk.h, mmk.inc())

            @block.vector
            def _(v):
                v.memset(jk[:, :], 0.0)
                for ti in range(NTT):
                    j = ti % 2
                    v.wait_ge(ldx[j].h, 16 * (ti // 2 + 1))
                    v.wait_ge(bcd.h, ti + 1)
                    if ti >= 1:
                        v.wait_ge(gmm.h, ti)
                        v.wait_ge(wbh.h, 16 * ti)
                    x23 = x2t[j][:, :].rearrange("p (c t) -> p c t", c=KC)
                    h33 = h3f[:, :].rearrange("p (c t) -> p c t", c=KC)
                    hb3 = h3b[:, :].rearrange("p (c t) -> p c t", c=KC)
                    for k in range(KC):
                        v.tensor_sub(h33[:, k, :], x23[:, k, :], psB1[:, :])
                        v.tensor_mul(h33[:, k, :], h33[:, k, :], psB2[:, :])
                        inst = v.tensor_copy(hb3[:, k, :], h33[:, k, :])
                    inst.then_inc(dvh.h, dvh.inc())
                    v.wait_ge(gmm.h, ti + 1)
                    v.tensor_copy(gsb[:, :], psG[0:8, 0:TT]).then_inc(
                        gev.h, gev.inc())
                    sp = mk_sp(v, jk)
                    for g in range(4):
                        gi = 4 * ti + g
                        v.wait_ge(smm.h, gi + 1)
                        s8 = sctile[:, 0:8]
                        v.tensor_copy(s8, psS[0:128, 0:8])
                        sp()
                        v.reduce_max(sctile[:, 8:9], s8, AX.X)
                        sp()
                        v.tensor_scalar_mul(sctile[:, 9:10], sctile[:, 8:9],
                                            -1.0).then_inc(top.h, top.inc())
                        v.wait_ge(acr.h, gi + 1)
                        v.reduce_sum(sctile[:, 8:9], s8, AX.X)
                        sp()
                        v.reciprocal(sctile[:, 9:10], sctile[:, 8:9])
                        sp()
                        v.tensor_scalar_mul(s8, s8, sctile[:, 9:10])
                        sp()
                        v.reduce_max(sctile[:, 8:9], s8, AX.X)
                        sp()
                        v.tensor_single_scalar(msc[:, :], s8, sctile[:, 8:9],
                                               ALU.is_ge)
                        sp()
                        v.tensor_scalar_mul(msc[:, :], msc[:, :], 2.0)
                        sp()
                        v.tensor_sub(msc[:, :], s8, msc[:, :])
                        sp()
                        v.reduce_max(sctile[:, 8:9], msc[:, :], AX.X)
                        sp()
                        v.tensor_single_scalar(msc[:, :], s8, sctile[:, 8:9],
                                               ALU.is_ge)
                        sp()
                        v.tensor_mul(msc[:, :], s8, msc[:, :]).then_inc(
                            top.h, top.inc())
                        v.wait_ge(mmk.h, gi + 1)
                        if gi >= 2:
                            v.wait_ge(wbm.h, 16 * (gi - 1))
                        v.tensor_copy(mst[gi % 2][:, :], psM[0:8, 0:128]
                                      ).then_inc(sev.h, sev.inc())

            @block.scalar
            def _(s):
                for ti in range(NTT):
                    for g in range(4):
                        gi = 4 * ti + g
                        s.wait_ge(top.h, 2 * gi + 1)
                        s.activation(sctile[:, 0:8], sctile[:, 0:8], AF.Exp,
                                     bias=sctile[:, 9:10], scale=1.0
                                     ).then_inc(acr.h, acr.inc())

            @block.gpsimd
            def _(g):
                for ti in range(NTT):
                    g.wait_ge(dvh.h, ti + 1)
                    if ti >= 1:
                        g.wait_ge(wbh.h, 16 * ti)
                    g.dma_start(hT3[:, :, ti * TT:(ti + 1) * TT],
                                h3b[:, :].rearrange("p (c t) -> p c t", c=KC)
                                ).then_inc(wbh.h, wbh.inc(16))
                    for gg in range(4):
                        gi = 4 * ti + gg
                        g.wait_ge(sev.h, gi + 1)
                        if gi >= 1:
                            g.wait_ge(wbm.h, 16 * gi)
                        g.dma_start(
                            sc["masksd"][:, gi * 128:(gi + 1) * 128],
                            mst[gi % 2][:, :]).then_inc(wbm.h, wbm.inc(16))
                g.wait_ge(wbh.h, wbh.n)
                g.wait_ge(wbm.h, wbm.n)


def emit_final(b, io, sc, per):
    nc = b.nc
    ids = per["ids"]
    x2T3, facc3 = fm(sc["x2T"]), fm(sc["facc"])
    with ExitStack() as ph:
        P = ph.enter_context
        x2t = [P(nc.sbuf_tensor(f"z_x2{j}", [128, KC * TT], F32))
               for j in range(2)]
        fat = [P(nc.sbuf_tensor(f"z_fa{j}", [128, KC * TT], F32))
               for j in range(2)]
        ot = P(nc.sbuf_tensor("z_ot", [128, KC * TT], F32))
        ost = [P(nc.sbuf_tensor(f"z_os{j}", [128, D], F32)) for j in range(2)]
        psT = [P(nc.psum_tensor(f"z_pT{j}", [128, 512], F32)) for j in range(2)]

        ldx = [b.sem() for _ in range(2)]
        ldf = [b.sem() for _ in range(2)]
        dvo = b.sem()
        tpx = b.sem()
        evx = b.sem()
        wbo = b.sem()

        with nc.Block("p11") as block:
            @block.sync
            def _(s):
                for ti in range(NTT):
                    j = ti % 2
                    if ti >= 2:
                        s.wait_ge(dvo.h, ti - 1)
                    s.dma_start(
                        x2t[j][:, :].rearrange("p (c t) -> p c t", c=KC),
                        x2T3[:, :, ti * TT:(ti + 1) * TT]
                    ).then_inc(ldx[j].h, ldx[j].inc(16))
                    s.dma_start(
                        fat[j][:, :].rearrange("p (c t) -> p c t", c=KC),
                        facc3[:, :, ti * TT:(ti + 1) * TT]
                    ).then_inc(ldf[j].h, ldf[j].inc(16))

            @block.vector
            def _(v):
                for ti in range(NTT):
                    j = ti % 2
                    v.wait_ge(ldx[j].h, 16 * (ti // 2 + 1))
                    v.wait_ge(ldf[j].h, 16 * (ti // 2 + 1))
                    if ti >= 1:
                        v.wait_ge(tpx.h, 32 * ti)
                    x23 = x2t[j][:, :].rearrange("p (c t) -> p c t", c=KC)
                    fa3 = fat[j][:, :].rearrange("p (c t) -> p c t", c=KC)
                    ot3 = ot[:, :].rearrange("p (c t) -> p c t", c=KC)
                    for k in range(KC):
                        inst = v.scalar_tensor_tensor(
                            ot3[:, k, :], fa3[:, k, :], 0.5, x23[:, k, :],
                            ALU.mult, ALU.add)
                    inst.then_inc(dvo.h, dvo.inc())

            @block.tensor
            def _(t):
                for ti in range(NTT):
                    t.wait_ge(dvo.h, ti + 1)
                    ot3 = ot[:, :].rearrange("p (c t) -> p c t", c=KC)
                    for g in range(4):
                        for k in range(KC):
                            gidx = 32 * ti + 8 * g + k
                            if gidx >= 2:
                                t.wait_ge(evx.h, gidx - 1)
                            t.transpose(psT[gidx % 2][0:128, 0:128],
                                        ot3[:, k, g * 128:(g + 1) * 128],
                                        ids[:, :]).then_inc(tpx.h, tpx.inc())

            @block.scalar
            def _(s):
                for ti in range(NTT):
                    for g in range(4):
                        gi = 4 * ti + g
                        for k in range(KC):
                            gidx = 32 * ti + 8 * g + k
                            if k == 0 and gi >= 2:
                                s.wait_ge(wbo.h, 16 * (gi - 1))
                            s.wait_ge(tpx.h, gidx + 1)
                            s.copy(ost[gi % 2][:, k * 128:(k + 1) * 128],
                                   psT[gidx % 2][0:128, 0:128]).then_inc(
                                evx.h, evx.inc())

            @block.gpsimd
            def _(g):
                for ti in range(NTT):
                    for gg in range(4):
                        gi = 4 * ti + gg
                        g.wait_ge(evx.h, 8 * (gi + 1))
                        if gi >= 1:
                            g.wait_ge(wbo.h, 16 * gi)
                        g.dma_start(io["out"][gi * 128:(gi + 1) * 128, :],
                                    ost[gi % 2][:, :]).then_inc(
                            wbo.h, wbo.inc(16))
                g.wait_ge(wbo.h, wbo.n)


def _host_constants():
    cid = np.eye(128, dtype=np.float32)
    cidb = np.eye(128, dtype=ml_dtypes.bfloat16)
    cidb2 = np.vstack([np.eye(64), np.eye(64)]).astype(ml_dtypes.bfloat16)
    tril = np.triu(np.ones((128, 128), dtype=np.float32)) * ETA
    csign = np.tile(np.array([1.0, -1.0], dtype=np.float32), 256)[None, :]
    return dict(cid=cid, cidb=cidb, cidb2=cidb2, ctrileta=tril,
                csign=csign.astype(np.float32))


_NC_CACHE = {}


def kernel(**inputs):
    if "nc" not in _NC_CACHE:
        _NC_CACHE["nc"] = build_program()
    nc = _NC_CACHE["nc"]
    x = np.ascontiguousarray(np.asarray(inputs["L0_attn_x"], dtype=np.float32))
    consts = _host_constants()
    common = dict(
        ln1_g=inputs["L0_attn_ln1_g"], ln1_b=inputs["L0_attn_ln1_b"],
        ln2_g=inputs["L0_attn_ln2_g"], ln2_b=inputs["L0_attn_ln2_b"],
        ln3_g=inputs["L0_attn_ln3_g"], ln3_b=inputs["L0_attn_ln3_b"],
        w_pot=inputs["L0_attn_w_pot"], b_pot=inputs["L0_attn_b_pot"],
        w_bk=inputs["L0_attn_w_bk"], b_bk=inputs["L0_attn_b_bk"],
        w_qkv=inputs["L0_attn_w_qkv"], b_qkv=inputs["L0_attn_b_qkv"],
        w_out=inputs["L0_attn_w_out"], b_out=inputs["L0_attn_b_out"],
        w_ff1=inputs["L0_attn_w_ff1"], b_ff1=inputs["L0_attn_b_ff1"],
        w_ff2=inputs["L0_attn_w_ff2"], b_ff2=inputs["L0_attn_b_ff2"],
        gate_w=inputs["L0_ffn_gate_w"], w1=inputs["L0_ffn_w1"],
        w2=inputs["L0_ffn_w2"], w3=inputs["L0_ffn_w3"], **consts)
    feed = {}
    for k, v in common.items():
        a = np.asarray(v)
        if k in ("cidb", "cidb2"):
            feed[k] = np.ascontiguousarray(a)
        else:
            feed[k] = np.ascontiguousarray(a.astype(np.float32))
    common = feed
    in_maps = [dict(common, xin=np.ascontiguousarray(x[c]))
               for c in range(8)]
    res = run_bass_kernel_spmd(nc, in_maps, core_ids=list(range(8)))
    out = np.stack([res.results[c]["out"] for c in range(8)], axis=0)
    return out.astype(np.float32)


# version 07
# speedup vs baseline: 1.0512x; 1.0512x over previous
import numpy as np
import ml_dtypes
from contextlib import ExitStack

import concourse.bass as bass
import concourse.mybir as mybir
from concourse.bass_utils import run_bass_kernel_spmd

F32 = mybir.dt.float32
BF16 = mybir.dt.bfloat16
AF = mybir.ActivationFunctionType
ALU = mybir.AluOpType
AX = mybir.AxisListType

T = 4096
D = 1024
KC = D // 128
NT128 = T // 128
FF = 4096
NFC = FF // 128
H, HD = 8, 64
E = 8
HID = 2816
NHC = HID // 128
TT = 512
NTT = T // TT
ETA, DTC, GFLOOR, EPS = 0.1, 0.01, 0.1, 1e-5
CB = 32
HCH = 128
NHCH = T // HCH

V_M1, V_RSTD1, V_RSTD12, V_MR12 = 0, 1, 2, 3
V_POTV, V_POTG = 5, 6
V_GSC = 7
V_CUM = 8
V_BKRE, V_BKIM = 9, 10
V_MA, V_RSTDA, V_MRA = 11, 12, 13
V_M3, V_RSTD3, V_MR3 = 14, 15, 16
V_M4, V_RSTD4, V_MR4 = 17, 18, 19
V_SCEXP = 20


class Ctr:
    __slots__ = ("h", "n")

    def __init__(self, h):
        self.h = h
        self.n = 0

    def inc(self, k=1):
        self.n += k
        return k


class B:

    def __init__(self):
        self.stack = ExitStack()
        self.nc = bass.Bass("TRN2", target_bir_lowering=False,
                            detect_race_conditions=False)
        self.S = self.stack.enter_context
        self.pool = []
        self.keep = 8
        self._idx = 0

    def init_pool(self, n=48):
        self.pool = [self.S(self.nc.semaphore(f"sp{i}")) for i in range(n)]
        self._idx = 0

    def sem(self, name=None):
        assert self._idx < len(self.pool), "sem pool exhausted"
        h = self.pool[self._idx]
        self._idx += 1
        return Ctr(h)

    def new_phase(self):
        nc = self.nc
        self.keep += 1
        ids = sorted(h.num for h in self.pool)
        assert ids == list(range(ids[0], ids[-1] + 1)), "pool not contiguous"
        rng = range(ids[0], ids[-1] + 1)
        with nc.Block(f"semclr{self.keep}") as block:
            @block.gpsimd
            def _(g):
                g.dma_reset(rng)
                g.sem_clear(rng)
        self._idx = 0


def fm(dram_ap, c=KC):
    return dram_ap.rearrange("(c p) t -> p c t", p=128)


def mk_sp(v, jk):
    return lambda: v.tensor_scalar_mul(jk[:, 0:1], jk[:, 1:2], 1.0)


def build_program(nphases=None):
    b = B()
    nc = b.nc
    S = b.S
    DT = nc.dram_tensor

    b.init_pool(48)
    io = {}
    for name, shape in [
        ("xin", [T, D]), ("ln1_g", [D]), ("ln1_b", [D]), ("ln2_g", [D]),
        ("ln2_b", [D]), ("ln3_g", [D]), ("ln3_b", [D]), ("w_pot", [D, 2]),
        ("b_pot", [2]), ("w_bk", [2, D]), ("b_bk", [D]),
        ("w_qkv", [D, 1536]), ("b_qkv", [1536]), ("w_out", [512, D]),
        ("b_out", [D]), ("w_ff1", [D, FF]), ("b_ff1", [FF]),
        ("w_ff2", [FF, D]), ("b_ff2", [D]), ("gate_w", [E, D]),
        ("w1", [E, D, HID]), ("w2", [E, HID, D]), ("w3", [E, D, HID]),
        ("cid", [128, 128]), ("ctrileta", [128, 128]),
        ("csign", [1, 512]),
    ]:
        io[name] = DT(name, shape, F32, kind="ExternalInput").ap()
    io["cidb"] = DT("cidb", [128, 128], BF16, kind="ExternalInput").ap()
    io["cidb2"] = DT("cidb2", [128, 64], BF16, kind="ExternalInput").ap()
    io["out"] = DT("out", [T, D], F32, kind="ExternalOutput").ap()

    sc = {}
    for name, shape, dt in [
        ("x0T", [D, T], F32), ("x1T", [D, T], F32), ("y1T", [D, T], F32),
        ("y2T", [D, T], F32), ("x2T", [D, T], F32), ("hT", [D, T], BF16),
        ("facc", [D, T], F32), ("masksd", [E, T], F32),
        ("vecs", [32, T], F32),
    ]:
        sc[name] = DT(name, shape, dt, kind="Internal").ap()

    ids = S(nc.sbuf_tensor("ids", [128, 128], F32))
    idsb = S(nc.sbuf_tensor("idsb", [128, 128], BF16))
    idsb2 = S(nc.sbuf_tensor("idsb2", [128, 64], BF16))
    trileta = S(nc.sbuf_tensor("trileta", [128, 128], F32))
    ones_r = S(nc.sbuf_tensor("ones_r", [1, 128], F32))
    ones_c = S(nc.sbuf_tensor("ones_c", [128, 1], F32))
    gv = S(nc.sbuf_tensor("gv", [128, 3 * KC], F32))
    bv = S(nc.sbuf_tensor("bv", [128, 3 * KC], F32))
    bqkvc = S(nc.sbuf_tensor("bqkvc", [128, 12], F32))
    boutc = S(nc.sbuf_tensor("boutc", [128, KC], F32))
    bbkc = S(nc.sbuf_tensor("bbkc", [128, KC], F32))
    bff1c = S(nc.sbuf_tensor("bff1c", [128, NFC], F32))
    bff2c = S(nc.sbuf_tensor("bff2c", [128, KC], F32))
    gateT = S(nc.sbuf_tensor("gateT", [128, KC * E], F32))
    cbank = S(nc.sbuf_tensor("cbank", [128, 8], F32))
    _cvals = [0.0, 1.0, -0.5, -1.0, EPS, 0.5, 2.0, -2.0]
    for _i, _vv in enumerate(_cvals):
        nc.const_aps.register(F32, _vv, cbank[:, _i:_i + 1])
    per = dict(ids=ids, idsb=idsb, idsb2=idsb2, trileta=trileta,
               ones_r=ones_r, ones_c=ones_c, gv=gv, bv=bv, bqkvc=bqkvc,
               boutc=boutc, bbkc=bbkc, bff1c=bff1c, bff2c=bff2c, gateT=gateT)

    ld0 = b.sem("ld0")
    with nc.Block("consts") as block:
        @block.sync
        def _(s):
            cp = lambda dst, src: s.dma_start(
                dst, src, allow_slow_non_contiguous=True
            ).then_inc(ld0.h, ld0.inc(16))
            cp(ids[:, :], io["cid"])
            cp(idsb[:, :], io["cidb"])
            cp(idsb2[:, :], io["cidb2"])
            cp(trileta[:, :], io["ctrileta"])
            r = lambda a: a.rearrange("(c p) -> p c", p=128)
            cp(gv[:, 0:KC], r(io["ln1_g"]))
            cp(gv[:, KC:2 * KC], r(io["ln2_g"]))
            cp(gv[:, 2 * KC:3 * KC], r(io["ln3_g"]))
            cp(bv[:, 0:KC], r(io["ln1_b"]))
            cp(bv[:, KC:2 * KC], r(io["ln2_b"]))
            cp(bv[:, 2 * KC:3 * KC], r(io["ln3_b"]))
            cp(bqkvc[:, :], r(io["b_qkv"]))
            cp(boutc[:, :], r(io["b_out"]))
            cp(bbkc[:, :], r(io["b_bk"]))
            cp(bff1c[:, :], r(io["b_ff1"]))
            cp(bff2c[:, :], r(io["b_ff2"]))
            s.wait_ge(ld0.h, ld0.n)

        @block.vector
        def _(v):
            v.memset(ones_r[:, :], 1.0)
            v.memset(ones_c[:, :], 1.0)
            for _i, _vv in enumerate(_cvals):
                v.memset(cbank[:, _i:_i + 1], _vv)

    with ExitStack() as ph:
        gsb = ph.enter_context(nc.sbuf_tensor("gp_gsb", [8, D], F32))
        psg = [ph.enter_context(nc.psum_tensor(f"gp_ps{j}", [128, 8], F32))
               for j in range(2)]
        gl = b.sem()
        tp = b.sem()
        ev = b.sem()
        with nc.Block("gateprep") as block:
            @block.sync
            def _(s):
                s.dma_start(gsb[:, :], io["gate_w"]).then_inc(gl.h, gl.inc(16))

            @block.tensor
            def _(t):
                t.wait_ge(gl.h, 16)
                t.wait_ge(ld0.h, ld0.n)
                for k in range(KC):
                    if k >= 2:
                        t.wait_ge(ev.h, k - 1)
                    t.transpose(psg[k % 2][:, :], gsb[:, k * 128:(k + 1) * 128],
                                ids[0:8, 0:8]).then_inc(tp.h, tp.inc())

            @block.vector
            def _(v):
                for k in range(KC):
                    v.wait_ge(tp.h, k + 1)
                    v.tensor_copy(gateT[:, k * 8:(k + 1) * 8],
                                  psg[k % 2][:, :]).then_inc(ev.h, ev.inc())

    plist = [
        (emit_p0, ()),
        (emit_moe, (0,)),
        (emit_combine_pot, ()),
        (emit_bk_scan, ()),
        (emit_attn_core, ()),
        (emit_y2stats, ()),
        (emit_ffn_x2, ()),
        (emit_h3_gate, ()),
        (emit_moe, (1,)),
        (emit_final, ()),
    ]
    if nphases is not None:
        plist = plist[:nphases]
    for fn, args in plist:
        b.new_phase()
        fn(b, io, sc, per, *args)

    b.stack.close()
    return nc


def _ln_rstd(eng, rstd, var_plus, tmp):
    eng.activation(tmp, var_plus, AF.Ln, bias=EPS, scale=1.0)
    eng.activation(rstd, tmp, AF.Exp, bias=0.0, scale=-0.5)


def emit_p0(b, io, sc, per):
    nc = b.nc
    ids, gateT = per["ids"], per["gateT"]
    with ExitStack() as ph:
        P = ph.enter_context
        xts = [P(nc.sbuf_tensor(f"p0_x{j}", [128, D], F32)) for j in range(2)]
        sqt = P(nc.sbuf_tensor("p0_sq", [128, D], F32))
        stat = P(nc.sbuf_tensor("p0_st", [128, 8], F32))
        h1t = P(nc.sbuf_tensor("p0_h1", [128, D], F32))
        xst = P(nc.sbuf_tensor("p0_xs", [128, D], F32))
        hst = P(nc.sbuf_tensor("p0_hs", [128, D], F32))
        sctile = P(nc.sbuf_tensor("p0_sc", [128, 10], F32))
        msc = P(nc.sbuf_tensor("p0_msc", [128, 8], F32))
        jk = P(nc.sbuf_tensor("p0_jk", [128, 2], F32))
        gevr = P(nc.sbuf_tensor("p0_gev", [8, 128], F32))
        mst = [P(nc.sbuf_tensor(f"p0_ms{j}", [8, 128], F32)) for j in range(2)]
        psT = [P(nc.psum_tensor(f"p0_psT{j}", [128, 128], F32)) for j in range(2)]
        psG = [P(nc.psum_tensor(f"p0_psG{j}", [8, 128], F32)) for j in range(2)]
        psS = P(nc.psum_tensor("p0_psS", [128, 8], F32))

        ldx = [b.sem() for _ in range(2)]
        dst = b.sem()
        acr = b.sem()
        dap = b.sem()
        tpx = b.sem()
        evx = b.sem()
        gmm = b.sem()
        gev = b.sem()
        top = b.sem()
        smm = b.sem()
        sev = b.sem()
        wbx = b.sem()
        wbh = b.sem()
        wbm = b.sem()

        x0T3, hT3 = fm(sc["x0T"]), fm(sc["hT"])

        with nc.Block("p0") as block:
            @block.sync
            def _(s):
                for i in range(NT128):
                    j = i % 2
                    if i >= 2:
                        s.wait_ge(dap.h, i - 1)
                        s.wait_ge(tpx.h, 16 * (i - 2) + 8)
                    s.dma_start(xts[j][:, :],
                                io["xin"][i * 128:(i + 1) * 128, :]
                                ).then_inc(ldx[j].h, ldx[j].inc(16))

            @block.vector
            def _(v):
                sp = mk_sp(v, jk)
                v.memset(jk[:, :], 0.0)
                for i in range(NT128):
                    j = i % 2
                    v.wait_ge(ldx[j].h, 16 * (i // 2 + 1))
                    xt = xts[j]
                    v.reduce_sum(stat[:, 0:1], xt[:, :], AX.X)
                    v.tensor_mul(sqt[:, :], xt[:, :], xt[:, :])
                    v.reduce_sum(stat[:, 1:2], sqt[:, :], AX.X)
                    v.tensor_scalar_mul(stat[:, 2:3], stat[:, 0:1], 1.0 / D)
                    v.tensor_scalar_mul(stat[:, 3:4], stat[:, 1:2], 1.0 / D)
                    v.tensor_mul(stat[:, 4:5], stat[:, 2:3], stat[:, 2:3])
                    sp()
                    v.tensor_sub(stat[:, 5:6], stat[:, 3:4], stat[:, 4:5]
                                 ).then_inc(dst.h, dst.inc())
                    v.wait_ge(acr.h, 2 * i + 1)
                    if i >= 1:
                        v.wait_ge(tpx.h, 16 * i)
                    v.tensor_scalar(h1t[:, :], xt[:, :], stat[:, 2:3],
                                    stat[:, 6:7], ALU.subtract, ALU.mult
                                    ).then_inc(dap.h, dap.inc())
                    v.wait_ge(gmm.h, i + 1)
                    v.tensor_copy(gevr[:, :], psG[i % 2][:, :]).then_inc(
                        gev.h, gev.inc())
                    v.wait_ge(smm.h, 2 * i + 1)
                    s8 = sctile[:, 0:8]
                    v.tensor_copy(s8, psS[:, :])
                    sp()
                    v.reduce_max(sctile[:, 8:9], s8, AX.X)
                    sp()
                    v.tensor_scalar_mul(sctile[:, 9:10], sctile[:, 8:9], -1.0
                                        ).then_inc(top.h, top.inc())
                    v.wait_ge(acr.h, 2 * i + 2)
                    v.reduce_sum(sctile[:, 8:9], s8, AX.X)
                    sp()
                    v.reciprocal(sctile[:, 9:10], sctile[:, 8:9])
                    sp()
                    v.tensor_scalar_mul(s8, s8, sctile[:, 9:10])
                    sp()
                    v.reduce_max(sctile[:, 8:9], s8, AX.X)
                    sp()
                    v.tensor_single_scalar(msc[:, :], s8, sctile[:, 8:9],
                                           ALU.is_ge)
                    sp()
                    v.tensor_scalar_mul(msc[:, :], msc[:, :], 2.0)
                    sp()
                    v.tensor_sub(msc[:, :], s8, msc[:, :])
                    sp()
                    v.reduce_max(sctile[:, 8:9], msc[:, :], AX.X)
                    sp()
                    v.tensor_single_scalar(msc[:, :], s8, sctile[:, 8:9],
                                           ALU.is_ge)
                    sp()
                    v.tensor_mul(msc[:, :], s8, msc[:, :]).then_inc(
                        top.h, top.inc())
                    v.wait_ge(smm.h, 2 * i + 2)
                    if i >= 2:
                        v.wait_ge(wbm.h, 16 * (i - 1))
                    v.tensor_copy(mst[i % 2][:, :], psG[i % 2][:, :]).then_inc(
                        sev.h, sev.inc())

            @block.scalar
            def _(s):
                for i in range(NT128):
                    s.wait_ge(dst.h, i + 1)
                    s.activation(stat[:, 7:8], stat[:, 5:6], AF.Ln,
                                 bias=EPS, scale=1.0)
                    s.activation(stat[:, 6:7], stat[:, 7:8], AF.Exp,
                                 bias=0.0, scale=-0.5).then_inc(acr.h, acr.inc())
                    for k in range(16):
                        gidx = 16 * i + k
                        if k == 0 and i >= 1:
                            s.wait_ge(wbx.h, 16 * i)
                        if k == 8:
                            if i >= 1:
                                s.wait_ge(wbh.h, 16 * i)
                            s.wait_ge(gmm.h, i)
                        s.wait_ge(tpx.h, gidx + 1)
                        dstt = xst if k < 8 else hst
                        kk = k % 8
                        s.copy(dstt[:, kk * 128:(kk + 1) * 128],
                               psT[gidx % 2][:, :]).then_inc(evx.h, evx.inc())
                    s.wait_ge(top.h, 2 * i + 1)
                    s.activation(sctile[:, 0:8], sctile[:, 0:8], AF.Exp,
                                 bias=sctile[:, 9:10], scale=1.0).then_inc(
                        acr.h, acr.inc())

            @block.tensor
            def _(t):
                for i in range(NT128):
                    j = i % 2
                    t.wait_ge(ldx[j].h, 16 * (i // 2 + 1))
                    for k in range(8):
                        gidx = 16 * i + k
                        if gidx >= 2:
                            t.wait_ge(evx.h, gidx - 1)
                        t.transpose(psT[gidx % 2][:, :],
                                    xts[j][:, k * 128:(k + 1) * 128],
                                    ids[:, :]).then_inc(tpx.h, tpx.inc())
                    t.wait_ge(dap.h, i + 1)
                    for k in range(8):
                        gidx = 16 * i + 8 + k
                        t.wait_ge(evx.h, gidx - 1)
                        t.transpose(psT[gidx % 2][:, :],
                                    h1t[:, k * 128:(k + 1) * 128],
                                    ids[:, :]).then_inc(tpx.h, tpx.inc())
                    if i >= 2:
                        t.wait_ge(sev.h, i - 1)
                    for k in range(KC):
                        t.wait_ge(evx.h, 16 * i + 8 + k + 1)
                        inst = t.matmul(psG[i % 2][:, :],
                                        gateT[:, k * 8:(k + 1) * 8],
                                        hst[:, k * 128:(k + 1) * 128],
                                        start=(k == 0), stop=(k == KC - 1))
                    inst.then_inc(gmm.h, gmm.inc())
                    t.wait_ge(gev.h, i + 1)
                    if i >= 1:
                        t.wait_ge(top.h, 2 * (i - 1) + 1)
                    t.transpose(psS[:, :], gevr[:, :],
                                ids[0:8, 0:8]).then_inc(smm.h, smm.inc())
                    t.wait_ge(top.h, 2 * i + 2)
                    t.transpose(psG[i % 2][:, :], msc[:, :],
                                ids[:, :]).then_inc(smm.h, smm.inc())

            @block.gpsimd
            def _(g):
                for i in range(NT128):
                    g.wait_ge(evx.h, 16 * i + 8)
                    if i >= 1:
                        g.wait_ge(wbx.h, 16 * i)
                    g.dma_start(x0T3[:, :, i * 128:(i + 1) * 128],
                                xst[:, :].rearrange("p (c t) -> p c t", c=KC)
                                ).then_inc(wbx.h, wbx.inc(16))
                    g.wait_ge(evx.h, 16 * i + 16)
                    if i >= 1:
                        g.wait_ge(wbh.h, 16 * i)
                    g.dma_start(hT3[:, :, i * 128:(i + 1) * 128],
                                hst[:, :].rearrange("p (c t) -> p c t", c=KC)
                                ).then_inc(wbh.h, wbh.inc(16))
                    g.wait_ge(sev.h, i + 1)
                    if i >= 1:
                        g.wait_ge(wbm.h, 16 * i)
                    g.dma_start(sc["masksd"][:, i * 128:(i + 1) * 128],
                                mst[i % 2][:, :]).then_inc(wbm.h, wbm.inc(16))
                g.wait_ge(wbx.h, wbx.n)
                g.wait_ge(wbh.h, wbh.n)
                g.wait_ge(wbm.h, wbm.n)


def emit_moe(b, io, sc, per, inv):
    nc = b.nc
    ones_r = per["ones_r"]
    hT3 = fm(sc["hT"])
    facc3 = fm(sc["facc"])
    with ExitStack() as ph:
        P = ph.enter_context
        A = P(nc.sbuf_tensor(f"m{inv}_A", [128, KC * HID], BF16))
        Bm = P(nc.sbuf_tensor(f"m{inv}_B", [128, KC * HID], BF16))
        C = P(nc.sbuf_tensor(f"m{inv}_C", [128, NHC * D], BF16))
        ht = P(nc.sbuf_tensor(f"m{inv}_h", [128, KC * TT], BF16))
        xe = P(nc.sbuf_tensor(f"m{inv}_xe", [128, KC * TT], BF16))
        ag = P(nc.sbuf_tensor(f"m{inv}_ag", [128, NHC * TT], BF16))
        outT = P(nc.sbuf_tensor(f"m{inv}_o", [128, KC * TT], F32))
        mrow = [P(nc.sbuf_tensor(f"m{inv}_mr{j}", [1, TT], F32)) for j in range(2)]
        jk = P(nc.sbuf_tensor(f"m{inv}_jk", [128, 2], F32))
        psA = [P(nc.psum_tensor(f"m{inv}_psA{j}", [128, TT], F32)) for j in range(3)]
        psC = [P(nc.psum_tensor(f"m{inv}_psC{j}", [128, TT], F32)) for j in range(2)]
        psM = P(nc.psum_tensor(f"m{inv}_psM", [128, TT], F32))

        A3 = A[:, :].rearrange("p (c n) -> p c n", c=KC)
        B3 = Bm[:, :].rearrange("p (c n) -> p c n", c=KC)
        C3 = C[:, :].rearrange("p (c n) -> p c n", c=NHC)
        ht3 = ht[:, :].rearrange("p (c t) -> p c t", c=KC)
        xe3 = xe[:, :].rearrange("p (c t) -> p c t", c=KC)
        ag3 = ag[:, :].rearrange("p (c t) -> p c t", c=NHC)
        oT3 = outT[:, :].rearrange("p (c t) -> p c t", c=KC)

        wldA, wldB, wldC = b.sem(), b.sem(), b.sem()
        wfA, wfB, wfC = b.sem(), b.sem(), b.sem()
        hld = b.sem()
        mld = [b.sem() for _ in range(2)]
        bcm = b.sem()
        xeb = b.sem()
        mmA = b.sem()
        evA = b.sem()
        evB = b.sem()
        mmC = b.sem()
        evC = b.sem()
        ord_ = b.sem()
        oacc = b.sem()
        phw3 = b.sem()

        w1r = io["w1"].rearrange("e (c p) n -> e p c n", p=128)
        w3r = io["w3"].rearrange("e (c p) n -> e p c n", p=128)
        w2r = io["w2"].rearrange("e (c p) n -> e p c n", p=128)

        with nc.Block(f"moe{inv}") as block:
            @block.gpsimd
            def _(g):
                g.dma_start(A3, w1r[0]).then_inc(wldA.h, wldA.inc(16))
                g.dma_start(B3, w3r[0]).then_inc(wldB.h, wldB.inc(16))
                g.dma_start(C3, w2r[0]).then_inc(wldC.h, wldC.inc(16))
                for e in range(E):
                    for ti in range(NTT):
                        q = e * NTT + ti
                        if e + 1 < E and ti == NTT - 2:
                            g.wait_ge(wfA.h, e + 1)
                            g.dma_start(A3, w1r[e + 1]).then_inc(
                                wldA.h, wldA.inc(16))
                        if e + 1 < E and ti == NTT - 1:
                            g.wait_ge(wfB.h, e + 1)
                            g.dma_start(B3, w3r[e + 1]).then_inc(
                                wldB.h, wldB.inc(16))
                        g.wait_ge(ord_.h, q + 1)
                        if q >= 1:
                            g.wait_ge(oacc.h, 16 * q)
                        g.dma_start(
                            facc3[:, :, ti * TT:(ti + 1) * TT], oT3,
                            accum_op=(ALU.bypass if e == 0 else ALU.add)
                        ).then_inc(oacc.h, oacc.inc(16))
                        if e + 1 < E and ti == NTT - 1:
                            g.wait_ge(wfC.h, e + 1)
                            g.dma_start(C3, w2r[e + 1]).then_inc(
                                wldC.h, wldC.inc(16))
                g.wait_ge(oacc.h, oacc.n)

            @block.sync
            def _(s):
                for e in range(E):
                    for ti in range(NTT):
                        q = e * NTT + ti
                        if q >= 1:
                            s.wait_ge(xeb.h, q)
                        s.dma_start(ht3, hT3[:, :, ti * TT:(ti + 1) * TT]
                                    ).then_inc(hld.h, hld.inc(16))
                        j = q % 2
                        if q >= 2:
                            s.wait_ge(xeb.h, q - 1)
                        s.dma_start(mrow[j][:, :],
                                    sc["masksd"][e:e + 1, ti * TT:(ti + 1) * TT]
                                    ).then_inc(mld[j].h, mld[j].inc(16))

            @block.tensor
            def _(t):
                for e in range(E):
                    for ti in range(NTT):
                        q = e * NTT + ti
                        j = q % 2
                        t.wait_ge(mld[j].h, 16 * (q // 2 + 1))
                        if q >= 1:
                            t.wait_ge(xeb.h, q)
                        t.matmul(psM[:, :], ones_r[:, :], mrow[j][:, :],
                                 start=True, stop=True).then_inc(bcm.h, bcm.inc())
                        if ti == 0:
                            t.wait_ge(wldA.h, 16 * (e + 1))
                        t.wait_ge(xeb.h, q + 1)
                        for m in range(NHC):
                            if m >= 3:
                                t.wait_ge(evA.h, 22 * q + m - 2)
                            elif q >= 1:
                                t.wait_ge(evB.h, 22 * (q - 1) + 20 + m)
                            for k in range(KC):
                                inst = t.matmul(
                                    psA[(44 * q + m) % 3][:, :],
                                    A3[:, k, m * 128:(m + 1) * 128],
                                    xe3[:, k, :],
                                    start=(k == 0), stop=(k == KC - 1))
                            inst.then_inc(mmA.h, mmA.inc())
                        if ti == NTT - 1:
                            t.nop().then_inc(wfA.h, wfA.inc())
                        if ti == 0:
                            t.wait_ge(wldB.h, 16 * (e + 1))
                        for m in range(NHC):
                            if m >= 3:
                                t.wait_ge(evB.h, 22 * q + m - 2)
                            else:
                                t.wait_ge(evA.h, 22 * q + 20 + m)
                            for k in range(KC):
                                inst = t.matmul(
                                    psA[(44 * q + NHC + m) % 3][:, :],
                                    B3[:, k, m * 128:(m + 1) * 128],
                                    xe3[:, k, :],
                                    start=(k == 0), stop=(k == KC - 1))
                            inst.then_inc(mmA.h, mmA.inc())
                        if ti == NTT - 1:
                            t.nop().then_inc(wfB.h, wfB.inc())
                        t.nop().then_inc(phw3.h, phw3.inc())
                        if ti == 0:
                            t.wait_ge(wldC.h, 16 * (e + 1))
                        t.wait_ge(evB.h, 22 * (q + 1))
                        for mo in range(KC):
                            ci = 8 * q + mo
                            if ci >= 2:
                                t.wait_ge(evC.h, ci - 1)
                            for k2 in range(NHC):
                                inst = t.matmul(
                                    psC[ci % 2][:, :],
                                    C3[:, k2, mo * 128:(mo + 1) * 128],
                                    ag3[:, k2, :],
                                    start=(k2 == 0), stop=(k2 == NHC - 1))
                            inst.then_inc(mmC.h, mmC.inc())
                        if ti == NTT - 1:
                            t.nop().then_inc(wfC.h, wfC.inc())

            @block.vector
            def _(v):
                v.memset(jk[:, :], 0.0)
                for e in range(E):
                    for ti in range(NTT):
                        q = e * NTT + ti
                        v.wait_ge(hld.h, 16 * (q + 1))
                        v.wait_ge(bcm.h, q + 1)
                        if q >= 1:
                            v.wait_ge(phw3.h, q)
                        for k in range(KC):
                            inst = v.tensor_mul(xe3[:, k, :], ht3[:, k, :],
                                                psM[:, :])
                        inst.then_inc(xeb.h, xeb.inc())
                        for m in range(NHC):
                            gi = 44 * q + NHC + m
                            v.wait_ge(mmA.h, gi + 1)
                            v.wait_ge(evA.h, 22 * q + m + 1)
                            v.tensor_mul(ag3[:, m, :], ag3[:, m, :],
                                         psA[gi % 3][:, :]).then_inc(
                                evB.h, evB.inc())
                        for mo in range(KC):
                            ci = 8 * q + mo
                            v.wait_ge(mmC.h, ci + 1)
                            if q >= 1 and mo == 0:
                                v.wait_ge(oacc.h, 16 * q)
                            inst = v.tensor_copy(oT3[:, mo, :],
                                                 psC[ci % 2][:, :])
                            inst.then_inc(evC.h, evC.inc())
                        v.tensor_scalar_mul(jk[:, 0:1], jk[:, 1:2], 1.0
                                            ).then_inc(ord_.h, ord_.inc())

            @block.scalar
            def _(s):
                for e in range(E):
                    for ti in range(NTT):
                        q = e * NTT + ti
                        for m in range(NHC):
                            gi = 44 * q + m
                            s.wait_ge(mmA.h, gi + 1)
                            s.activation(ag3[:, m, :], psA[gi % 3][:, :],
                                         AF.Silu).then_inc(evA.h, evA.inc())


def emit_combine_pot(b, io, sc, per):
    nc = b.nc
    ones_c, ones_r, gv, bv = per["ones_c"], per["ones_r"], per["gv"], per["bv"]
    ids = per["ids"]
    x0T3, facc3, x1T3 = fm(sc["x0T"]), fm(sc["facc"]), fm(sc["x1T"])
    with ExitStack() as ph:
        P = ph.enter_context
        x0t = [P(nc.sbuf_tensor(f"c_x0{j}", [128, KC * TT], F32)) for j in range(2)]
        fat = [P(nc.sbuf_tensor(f"c_fa{j}", [128, KC * TT], F32)) for j in range(2)]
        x1t = P(nc.sbuf_tensor("c_x1", [128, KC * TT], F32))
        sqt = P(nc.sbuf_tensor("c_sqt", [128, KC * TT], F32))
        wp = P(nc.sbuf_tensor("c_wp", [128, KC * 2], F32))
        wt = P(nc.sbuf_tensor("c_wt", [128, KC * 2], F32))
        mv3 = P(nc.sbuf_tensor("c_mv", [128, KC * 3], F32))
        bp = P(nc.sbuf_tensor("c_bp", [1, 2], F32))
        crow = P(nc.sbuf_tensor("c_cr", [1, 8], F32))
        s1tok = P(nc.sbuf_tensor("c_s1", [128, NT128], F32))
        s2tok = P(nc.sbuf_tensor("c_s2", [128, NT128], F32))
        praw = P(nc.sbuf_tensor("c_pr", [128, 2 * NT128], F32))
        stt = P(nc.sbuf_tensor("c_stt", [128, 8 * NT128], F32))
        gam = P(nc.sbuf_tensor("c_gam", [128, NT128], F32))
        colb = P(nc.sbuf_tensor("c_col", [128, 8], F32))
        tpo = P(nc.sbuf_tensor("c_tpo", [32, 128], F32))
        jk = P(nc.sbuf_tensor("c_jk", [128, 2], F32))
        ell32 = P(nc.sbuf_tensor("c_ell", [32, 128], F32))
        cum32 = P(nc.sbuf_tensor("c_cum", [32, 128], F32))
        one32 = P(nc.sbuf_tensor("c_one", [32, 128], F32))
        psSq = [P(nc.psum_tensor(f"c_pQ{j}", [128, 512], F32)) for j in range(2)]
        psP = [P(nc.psum_tensor(f"c_pP{j}", [128, 512], F32)) for j in range(2)]
        psT2 = P(nc.psum_tensor("c_pT", [128, 512], F32))
        psX = P(nc.psum_tensor("c_pX", [128, 512], F32))

        ldx = [b.sem() for _ in range(2)]
        ldf = [b.sem() for _ in range(2)]
        ldw = b.sem()
        dvx = b.sem()
        smm = b.sem()
        sev = b.sem()
        wb1 = b.sem()
        fin = b.sem()
        wbv = b.sem()

        m1v = lambda: stt[:, 0:32]
        varv = lambda: stt[:, 32:64]
        rstd1v = lambda: stt[:, 64:96]
        rstd12v = lambda: stt[:, 96:128]
        tmpv = lambda: stt[:, 128:160]
        potVv = lambda: stt[:, 160:192]
        abv = lambda: stt[:, 192:224]

        with nc.Block("combpot") as block:
            @block.sync
            def _(s):
                s.dma_start(wp[:, :].rearrange("p (c j) -> p c j", c=KC),
                            io["w_pot"].rearrange("(c p) j -> p c j", p=128),
                            allow_slow_non_contiguous=True
                            ).then_inc(ldw.h, ldw.inc(16))
                s.dma_start(bp[:, :], io["b_pot"].rearrange("(r j) -> r j", r=1)
                            ).then_inc(ldw.h, ldw.inc(16))
                for ti in range(NTT):
                    j = ti % 2
                    if ti >= 2:
                        s.wait_ge(dvx.h, ti - 1)
                    s.dma_start(x0t[j][:, :].rearrange("p (c t) -> p c t", c=KC),
                                x0T3[:, :, ti * TT:(ti + 1) * TT]
                                ).then_inc(ldx[j].h, ldx[j].inc(16))
                    s.dma_start(fat[j][:, :].rearrange("p (c t) -> p c t", c=KC),
                                facc3[:, :, ti * TT:(ti + 1) * TT]
                                ).then_inc(ldf[j].h, ldf[j].inc(16))

            @block.gpsimd
            def _(g):
                for ti in range(NTT):
                    g.wait_ge(dvx.h, ti + 1)
                    if ti >= 1:
                        g.wait_ge(wb1.h, 16 * ti)
                    g.dma_start(x1T3[:, :, ti * TT:(ti + 1) * TT],
                                x1t[:, :].rearrange("p (c t) -> p c t", c=KC)
                                ).then_inc(wb1.h, wb1.inc(16))
                for r, row in enumerate([V_M1, V_RSTD1, V_POTV, V_POTG]):
                    g.wait_ge(fin.h, 15 + 2 * r)
                    g.dma_start(
                        sc["vecs"][row:row + 1, :].rearrange(
                            "r (c t) -> (r c) t", c=32),
                        tpo[:, :]).then_inc(wbv.h, wbv.inc(16))
                g.wait_ge(fin.h, 22)
                g.dma_start(
                    sc["vecs"][V_CUM:V_CUM + 1, :].rearrange(
                        "r (c t) -> (r c) t", c=32),
                    cum32[:, :]).then_inc(wbv.h, wbv.inc(16))
                g.wait_ge(wbv.h, wbv.n)

            @block.vector
            def _(v):
                sp = mk_sp(v, jk)
                v.memset(jk[:, :], 0.0)
                v.wait_ge(ldw.h, 32)
                wp3 = wp[:, :].rearrange("p (c j) -> p c j", c=KC)
                wt3 = wt[:, :].rearrange("p (c j) -> p c j", c=KC)
                v.memset(one32[:, :], 1.0)
                v.tensor_mul(wt3[:, :, 0], wp3[:, :, 0], gv[:, 0:KC])
                v.tensor_mul(wt3[:, :, 1], wp3[:, :, 1], gv[:, 0:KC])
                v.memset(mv3[:, 0:3 * KC:3], 1.0)
                v.tensor_copy(mv3[:, 1:3 * KC:3], wt[:, 0:2 * KC:2])
                v.tensor_copy(mv3[:, 2:3 * KC:3], wt[:, 1:2 * KC:2]
                              ).then_inc(fin.h, 1)
                for ti in range(NTT):
                    j = ti % 2
                    v.wait_ge(ldx[j].h, 16 * (ti // 2 + 1))
                    v.wait_ge(ldf[j].h, 16 * (ti // 2 + 1))
                    if ti >= 1:
                        v.wait_ge(wb1.h, 16 * ti)
                        v.wait_ge(smm.h, 4 * ti)
                    x03 = x0t[j][:, :].rearrange("p (c t) -> p c t", c=KC)
                    fa3 = fat[j][:, :].rearrange("p (c t) -> p c t", c=KC)
                    x13 = x1t[:, :].rearrange("p (c t) -> p c t", c=KC)
                    sq3 = sqt[:, :].rearrange("p (c t) -> p c t", c=KC)
                    for k in range(KC):
                        v.scalar_tensor_tensor(
                            x13[:, k, :], fa3[:, k, :], 0.5, x03[:, k, :],
                            ALU.mult, ALU.add)
                    for k in range(KC):
                        inst = v.tensor_mul(sq3[:, k, :], x13[:, k, :],
                                            x13[:, k, :])
                    inst.then_inc(dvx.h, dvx.inc())
                    for g in range(4):
                        gi = 4 * ti + g
                        v.wait_ge(smm.h, gi + 1)
                        v.tensor_copy(s1tok[:, gi:gi + 1], psP[gi % 2][:, 0:1])
                        v.tensor_copy(s2tok[:, gi:gi + 1], psSq[gi % 2][:, 0:1])
                        v.tensor_copy(praw[:, 2 * gi:2 * gi + 2],
                                      psP[gi % 2][:, 1:3]).then_inc(
                            sev.h, sev.inc())
                v.tensor_scalar_mul(m1v(), s1tok[:, :], 1.0 / D)
                v.tensor_scalar_mul(tmpv(), s2tok[:, :], 1.0 / D)
                v.tensor_mul(varv(), m1v(), m1v())
                sp()
                v.tensor_sub(varv(), tmpv(), varv()).then_inc(fin.h, 1)
                v.wait_ge(fin.h, 3)
                v.tensor_scalar(tmpv(), varv(), EPS, None, ALU.add)
                sp()
                v.reciprocal(tmpv(), tmpv())
                sp()
                v.tensor_mul(tmpv(), varv(), tmpv()).then_inc(fin.h, 1)
                v.wait_ge(fin.h, 5)
                v.tensor_mul(rstd12v(), rstd1v(), tmpv())
                v.wait_ge(fin.h, 7)
                v.tensor_copy(crow[0:1, 0:2], psX[0:1, 0:2])
                v.tensor_copy(crow[0:1, 2:4], psX[0:1, 2:4])
                sp()
                v.tensor_add(crow[0:1, 2:4], crow[0:1, 2:4], bp[0:1, 0:2]
                             ).then_inc(fin.h, 1)
                v.wait_ge(fin.h, 9)
                v.tensor_copy(colb[:, 0:4], psX[:, 4:8])
                sp()
                sp()
                prV = praw[:, 0:2 * NT128:2]
                prG = praw[:, 1:2 * NT128:2]
                v.tensor_scalar_mul(potVv(), m1v(), colb[:, 0:1])
                v.tensor_scalar_mul(tmpv(), m1v(), colb[:, 1:2])
                v.tensor_sub(potVv(), prV, potVv())
                v.tensor_sub(tmpv(), prG, tmpv())
                v.tensor_mul(potVv(), potVv(), rstd12v())
                v.tensor_mul(tmpv(), tmpv(), rstd12v())
                v.tensor_scalar_add(potVv(), potVv(), colb[:, 2:3])
                v.tensor_scalar_add(tmpv(), tmpv(), colb[:, 3:4]
                                    ).then_inc(fin.h, 1)
                v.wait_ge(fin.h, 12)
                v.tensor_scalar_max(gam[:, :], tmpv(), 0.0)
                sp()
                v.tensor_add(gam[:, :], gam[:, :], abv())
                sp()
                v.tensor_scalar_add(gam[:, :], gam[:, :], GFLOOR
                                    ).then_inc(fin.h, 1)
                for r in range(4):
                    v.wait_ge(fin.h, 14 + 2 * r)
                    v.wait_ge(wbv.h, 16 * r)
                    v.tensor_copy(tpo[:, :], psT2[0:32, 0:128]).then_inc(
                        fin.h, 1)
                sp()
                v.tensor_scalar_mul(ell32[:, :], tpo[:, :], -DTC)
                v.tensor_tensor_scan(cum32[:, :], one32[:, :], ell32[:, :],
                                     0.0, ALU.mult, ALU.add).then_inc(
                    fin.h, 1)

            @block.scalar
            def _(s):
                s.wait_ge(fin.h, 2)
                s.activation(rstd1v(), varv(), AF.Ln, bias=EPS, scale=1.0)
                s.activation(rstd1v(), rstd1v(), AF.Exp, bias=0.0, scale=-0.5
                             ).then_inc(fin.h, 1)
                s.wait_ge(fin.h, 4)
                s.activation(tmpv(), tmpv(), AF.Ln, bias=EPS, scale=1.0)
                s.activation(tmpv(), tmpv(), AF.Exp, bias=0.0, scale=-0.5
                             ).then_inc(fin.h, 1)
                s.wait_ge(fin.h, 10)
                s.activation(abv(), tmpv(), AF.Abs).then_inc(fin.h, 1)
                s.activation(abv(), abv(), AF.Exp, bias=0.0, scale=-1.0)
                s.activation(abv(), abv(), AF.Ln, bias=1.0, scale=1.0
                             ).then_inc(fin.h, 1)

            @block.tensor
            def _(t):
                t.wait_ge(fin.h, 1)
                for ti in range(NTT):
                    t.wait_ge(dvx.h, ti + 1)
                    x13 = x1t[:, :].rearrange("p (c t) -> p c t", c=KC)
                    sq3 = sqt[:, :].rearrange("p (c t) -> p c t", c=KC)
                    wt3 = wt[:, :].rearrange("p (c j) -> p c j", c=KC)
                    for g in range(4):
                        gi = 4 * ti + g
                        gs = slice(g * 128, (g + 1) * 128)
                        if gi >= 2:
                            t.wait_ge(sev.h, gi - 1)
                        for k in range(KC):
                            t.matmul(psP[gi % 2][:, 0:3], x13[:, k, gs],
                                     mv3[:, 3 * k:3 * k + 3], start=(k == 0),
                                     stop=(k == KC - 1))
                        for k in range(KC):
                            inst = t.matmul(psSq[gi % 2][:, 0:1], sq3[:, k, gs],
                                            ones_c[:, :], start=(k == 0),
                                            stop=(k == KC - 1))
                        inst.then_inc(smm.h, smm.inc())
                t.wait_ge(fin.h, 5)
                wt3 = wt[:, :].rearrange("p (c j) -> p c j", c=KC)
                wp3 = wp[:, :].rearrange("p (c j) -> p c j", c=KC)
                for k in range(KC):
                    t.matmul(psX[0:1, 0:2], ones_c[:, :], wt3[:, k, :],
                             start=(k == 0), stop=(k == KC - 1))
                for k in range(KC):
                    inst = t.matmul(psX[0:1, 2:4], bv[:, k:k + 1], wp3[:, k, :],
                                    start=(k == 0), stop=(k == KC - 1))
                inst.then_inc(fin.h, 2)
                t.wait_ge(fin.h, 8)
                for jj in range(4):
                    inst = t.matmul(psX[:, 4 + jj:5 + jj], ones_r[:, :],
                                    crow[0:1, jj:jj + 1], start=True, stop=True)
                inst.then_inc(fin.h, 1)
                t.wait_ge(fin.h, 13)
                srcs = [m1v(), rstd1v(), potVv(), gam[:, :]]
                for r, sr in enumerate(srcs):
                    if r >= 1:
                        t.wait_ge(fin.h, 13 + 2 * r)
                    t.transpose(psT2[0:32, 0:128], sr, ids[:, :]).then_inc(
                        fin.h, 1)


def emit_bk_scan(b, io, sc, per):
    nc = b.nc
    with ExitStack() as ph:
        P = ph.enter_context
        dre = P(nc.sbuf_tensor("k_dre", [32, 128], F32))
        dim = P(nc.sbuf_tensor("k_dim", [32, 128], F32))
        sfac = P(nc.sbuf_tensor("k_sf", [32, 128], F32))
        drs = P(nc.sbuf_tensor("k_drs", [32, 128], F32))
        Rf = P(nc.sbuf_tensor("k_Rf", [32, 128 * 8], F32))
        Rb = P(nc.sbuf_tensor("k_Rb", [32, 128 * 8], F32))
        sc4 = P(nc.sbuf_tensor("k_sc4", [32, 8], F32))
        sc4b = P(nc.sbuf_tensor("k_sc4b", [32, 8], F32))
        lrow = P(nc.sbuf_tensor("k_lrow", [1, 512], F32))
        srow = P(nc.sbuf_tensor("k_srow", [1, 512], F32))
        csg = P(nc.sbuf_tensor("k_csg", [1, 512], F32))
        v8 = [P(nc.sbuf_tensor(f"k_v8{j}", [1, 8], F32)) for j in range(4)]
        pscr = [P(nc.sbuf_tensor(f"k_pscr{j}", [1, 8], F32)) for j in range(2)]
        vbrow = P(nc.sbuf_tensor("k_vb", [1, 256], F32))
        vb32 = P(nc.sbuf_tensor("k_vb32", [32, 8], F32))
        ar = P(nc.sbuf_tensor("k_ar", [32, 128], F32))
        ai = P(nc.sbuf_tensor("k_ai", [32, 128], F32))
        br = P(nc.sbuf_tensor("k_br", [32, 128], F32))
        bi = P(nc.sbuf_tensor("k_bi", [32, 128], F32))
        t1 = P(nc.sbuf_tensor("k_t1", [32, 128], F32))
        t2 = P(nc.sbuf_tensor("k_t2", [32, 128], F32))
        t3 = P(nc.sbuf_tensor("k_t3", [32, 128], F32))
        gout = P(nc.sbuf_tensor("k_go", [32, 256], F32))

        ld = b.sem()
        fin = b.sem()
        wbv = b.sem()

        vrow = lambda r: sc["vecs"][r:r + 1, :].rearrange(
            "r (c t) -> (r c) t", c=32)

        def lvl1_gen(v, R, rev, s4):
            R8 = R[:, :].rearrange("p (t x) -> p t x", t=128)
            slot = (lambda ts: 127 - ts) if rev else (lambda ts: ts)
            p0 = 127 if rev else 0
            s0 = slot(0)
            yield v.tensor_mul(R8[:, s0, 0:1], dre[:, p0:p0 + 1],
                               sfac[:, p0:p0 + 1])
            yield v.tensor_mul(R8[:, s0, 1:2], dim[:, p0:p0 + 1],
                               sfac[:, p0:p0 + 1])
            yield v.tensor_scalar_mul(R8[:, s0, 2:3], sfac[:, p0:p0 + 1], -1.0)
            yield v.memset(R8[:, s0, 3:4], 0.0)
            yield v.tensor_copy(R8[:, s0, 4:5], sfac[:, p0:p0 + 1])
            yield v.memset(R8[:, s0, 5:8], 0.0)
            for ts in range(1, 128):
                pos = 127 - ts if rev else ts
                prev = R8[:, slot(ts - 1), :]
                cur = R8[:, slot(ts), :]
                dcol = dre[:, pos:pos + 1]
                icol = dim[:, pos:pos + 1]
                scol = sfac[:, pos:pos + 1]
                yield v.tensor_scalar_mul(s4[:, 0:4], prev[:, 0:4], dcol)
                yield v.tensor_scalar_mul(s4[:, 4:8], prev[:, 0:4], icol)
                yield v.tensor_sub(cur[:, 0:4:2], s4[:, 0:4:2], s4[:, 5:8:2])
                yield v.tensor_sub(cur[:, 0:4:2], cur[:, 0:4:2], prev[:, 4:8:2])
                yield v.tensor_scalar_mul(cur[:, 0:4:2], cur[:, 0:4:2], scol)
                yield v.tensor_add(cur[:, 1:4:2], s4[:, 1:4:2], s4[:, 4:8:2])
                yield v.tensor_sub(cur[:, 1:4:2], cur[:, 1:4:2], prev[:, 5:8:2])
                yield v.tensor_scalar_mul(cur[:, 1:4:2], cur[:, 1:4:2], scol)
                yield v.tensor_scalar_mul(cur[:, 4:8], prev[:, 0:4], scol)

        def lvl2_gen(v, half, va, vb_, ps):
            rev = (half == 1)
            base = 256 * half
            yield v.memset(va[:, :], 0.0)
            yield v.memset(va[:, 0:1], 1.0)
            yield v.memset(va[:, 5:6], 1.0)
            first = 31 if rev else 0
            yield v.memset(
                vbrow[:, 8 * first + 4 * half:8 * first + 4 * half + 4], 0.0)
            yield v.memset(
                vbrow[:, 8 * first + 4 * half:8 * first + 4 * half + 1], 1.0)
            order = list(range(31, 0, -1)) if rev else list(range(31))
            cur, nxt = va, vb_
            for c in order:
                nxtc = c - 1 if rev else c + 1
                g0 = base + 8 * c
                yield v.tensor_mul(ps[:, 0:4], srow[0:1, g0:g0 + 4],
                                   cur[:, 0:4])
                yield v.reduce_sum(nxt[:, 0:1], ps[:, 0:4], AX.X)
                yield v.tensor_mul(ps[:, 0:4], lrow[0:1, g0:g0 + 4],
                                   cur[:, 4:8])
                yield v.reduce_sum(nxt[:, 1:2], ps[:, 0:4], AX.X)
                yield v.tensor_mul(ps[:, 0:4], srow[0:1, g0 + 4:g0 + 8],
                                   cur[:, 0:4])
                yield v.reduce_sum(nxt[:, 2:3], ps[:, 0:4], AX.X)
                yield v.tensor_mul(ps[:, 0:4], lrow[0:1, g0 + 4:g0 + 8],
                                   cur[:, 4:8])
                yield v.reduce_sum(nxt[:, 3:4], ps[:, 0:4], AX.X)
                yield v.tensor_scalar_mul(ps[:, 4:8], nxt[:, 0:4], -1.0)
                yield v.tensor_max(ps[:, 0:4], nxt[:, 0:4], ps[:, 4:8])
                yield v.reduce_sum(ps[:, 4:5], ps[:, 0:4], AX.X)
                yield v.reciprocal(ps[:, 5:6], ps[:, 4:5])
                yield v.tensor_scalar_mul(nxt[:, 0:4], nxt[:, 0:4],
                                          ps[0:1, 5:6])
                yield v.tensor_copy(nxt[:, 4:5], nxt[:, 1:2])
                yield v.tensor_copy(nxt[:, 5:6], nxt[:, 0:1])
                yield v.tensor_copy(nxt[:, 6:7], nxt[:, 3:4])
                yield v.tensor_copy(nxt[:, 7:8], nxt[:, 2:3])
                yield v.tensor_copy(
                    vbrow[:, 8 * nxtc + 4 * half:8 * nxtc + 4 * half + 4],
                    nxt[:, 0:4])
                cur, nxt = nxt, cur

        def interleave(*gens):
            gens = list(gens)
            last = None
            while gens:
                nxt_gens = []
                for g in gens:
                    try:
                        r = next(g)
                        if r is not None:
                            last = r
                        nxt_gens.append(g)
                    except StopIteration:
                        pass
                gens = nxt_gens
            return last

        def apply_dir(v, R, half):
            R8 = R[:, :].rearrange("p (t x) -> p t x", t=128)
            o = 4 * half
            outr = ar if half == 0 else br
            outi = ai if half == 0 else bi
            vc = lambda k: vb32[:, o + k:o + k + 1]
            v.tensor_scalar_mul(t1[:, :], R8[:, :, 0], vc(0))
            v.tensor_scalar_mul(t2[:, :], R8[:, :, 1], vc(1))
            v.tensor_sub(t1[:, :], t1[:, :], t2[:, :])
            v.tensor_scalar_mul(t2[:, :], R8[:, :, 2], vc(2))
            v.tensor_add(t1[:, :], t1[:, :], t2[:, :])
            v.tensor_scalar_mul(t2[:, :], R8[:, :, 3], vc(3))
            v.tensor_sub(t1[:, :], t1[:, :], t2[:, :])
            v.tensor_scalar_mul(t2[:, :], R8[:, :, 0], vc(1))
            v.tensor_scalar_mul(t3[:, :], R8[:, :, 1], vc(0))
            v.tensor_add(t2[:, :], t2[:, :], t3[:, :])
            v.tensor_scalar_mul(t3[:, :], R8[:, :, 2], vc(3))
            v.tensor_add(t2[:, :], t2[:, :], t3[:, :])
            v.tensor_scalar_mul(t3[:, :], R8[:, :, 3], vc(2))
            v.tensor_add(t2[:, :], t2[:, :], t3[:, :])
            v.tensor_scalar_mul(outr[:, :], R8[:, :, 4], vc(0))
            v.tensor_scalar_mul(t3[:, :], R8[:, :, 5], vc(1))
            v.tensor_sub(outr[:, :], outr[:, :], t3[:, :])
            v.tensor_scalar_mul(t3[:, :], R8[:, :, 6], vc(2))
            v.tensor_add(outr[:, :], outr[:, :], t3[:, :])
            v.tensor_scalar_mul(t3[:, :], R8[:, :, 7], vc(3))
            v.tensor_sub(outr[:, :], outr[:, :], t3[:, :])
            v.tensor_scalar_mul(outi[:, :], R8[:, :, 4], vc(1))
            v.tensor_scalar_mul(t3[:, :], R8[:, :, 5], vc(0))
            v.tensor_add(outi[:, :], outi[:, :], t3[:, :])
            v.tensor_scalar_mul(t3[:, :], R8[:, :, 6], vc(3))
            v.tensor_add(outi[:, :], outi[:, :], t3[:, :])
            v.tensor_scalar_mul(t3[:, :], R8[:, :, 7], vc(2))
            v.tensor_add(outi[:, :], outi[:, :], t3[:, :])
            v.tensor_mul(t3[:, :], outr[:, :], outr[:, :])
            v.tensor_mul(drs[:, :], outi[:, :], outi[:, :])
            v.tensor_add(t3[:, :], t3[:, :], drs[:, :])
            v.reciprocal(t3[:, :], t3[:, :])
            v.tensor_mul(drs[:, :], t1[:, :], outr[:, :])
            v.tensor_mul(sfac[:, :], t2[:, :], outi[:, :])
            v.tensor_add(drs[:, :], drs[:, :], sfac[:, :])
            v.tensor_mul(sfac[:, :], t2[:, :], outr[:, :])
            v.tensor_mul(t1[:, :], t1[:, :], outi[:, :])
            v.tensor_sub(sfac[:, :], sfac[:, :], t1[:, :])
            v.tensor_mul(outr[:, :], drs[:, :], t3[:, :])
            v.tensor_mul(outi[:, :], sfac[:, :], t3[:, :])

        with nc.Block("bkscan") as block:
            @block.sync
            def _(s):
                s.dma_start(dre[:, :], vrow(V_POTV)).then_inc(ld.h, ld.inc(16))
                s.dma_start(dim[:, :], vrow(V_POTG)).then_inc(ld.h, ld.inc(16))
                s.dma_start(csg[:, :], io["csign"]).then_inc(ld.h, ld.inc(16))

            @block.gpsimd
            def _(g):
                g.wait_ge(fin.h, 3)
                g.dma_start(sc["vecs"][24:25, 0:256].rearrange(
                    "r (p x) -> (r p) x", p=32),
                    Rf[:, :].rearrange("p (t x) -> p t x", t=128)[:, 127, :]
                ).then_inc(wbv.h, wbv.inc(16))
                g.wait_ge(wbv.h, 16)
                g.dma_start(sc["vecs"][25:26, 0:256].rearrange(
                    "r (p x) -> (r p) x", p=32),
                    Rb[:, :].rearrange("p (t x) -> p t x", t=128)[:, 0, :]
                ).then_inc(wbv.h, wbv.inc(16))
                g.wait_ge(wbv.h, 32)
                g.dma_start(lrow[0:1, 0:256], sc["vecs"][24:25, 0:256]
                            ).then_inc(wbv.h, wbv.inc(16))
                g.wait_ge(wbv.h, 48)
                g.dma_start(lrow[0:1, 256:512], sc["vecs"][25:26, 0:256]
                            ).then_inc(wbv.h, wbv.inc(16))
                g.wait_ge(wbv.h, 64)
                g.nop().then_inc(fin.h, 1)
                g.wait_ge(fin.h, 5)
                g.dma_start(sc["vecs"][24:25, 0:256], vbrow[0:1, 0:256]
                            ).then_inc(wbv.h, wbv.inc(16))
                g.wait_ge(wbv.h, 80)
                g.dma_start(vb32[:, :], sc["vecs"][24:25, 0:256].rearrange(
                    "r (c x) -> (r c) x", c=32)).then_inc(wbv.h, wbv.inc(16))
                g.wait_ge(wbv.h, 96)
                g.nop().then_inc(fin.h, 1)
                g.wait_ge(fin.h, 7)
                g.dma_start(vrow(V_BKRE), gout[:, 0:128]).then_inc(
                    wbv.h, wbv.inc(16))
                g.wait_ge(wbv.h, 112)
                g.dma_start(vrow(V_BKIM), gout[:, 128:256]).then_inc(
                    wbv.h, wbv.inc(16))
                g.wait_ge(wbv.h, wbv.n)

            @block.vector
            def _(v):
                v.wait_ge(ld.h, 48)
                v.tensor_scalar_add(dre[:, :], dre[:, :], 2.0)
                v.tensor_scalar_mul(dim[:, :], dim[:, :], -1.0)
                v.tensor_mul(t1[:, :], dre[:, :], dre[:, :])
                v.tensor_mul(t2[:, :], dim[:, :], dim[:, :])
                v.tensor_add(t1[:, :], t1[:, :], t2[:, :])
                v.tensor_scalar_add(t1[:, :], t1[:, :], 1.0
                                    ).then_inc(fin.h, 1)
                v.wait_ge(fin.h, 2)
                interleave(lvl1_gen(v, Rf, False, sc4),
                           lvl1_gen(v, Rb, True, sc4b)
                           ).then_inc(fin.h, 1)
                v.wait_ge(fin.h, 4)
                v.tensor_mul(srow[:, :], lrow[:, :], csg[:, :])
                interleave(lvl2_gen(v, 0, v8[0], v8[1], pscr[0]),
                           lvl2_gen(v, 1, v8[2], v8[3], pscr[1])
                           ).then_inc(fin.h, 1)
                v.wait_ge(fin.h, 6)
                apply_dir(v, Rf, 0)
                apply_dir(v, Rb, 1)
                v.tensor_add(t1[:, :], ar[:, :], br[:, :])
                v.tensor_sub(t1[:, :], t1[:, :], dre[:, :])
                v.tensor_add(t2[:, :], ai[:, :], bi[:, :])
                v.tensor_sub(t2[:, :], t2[:, :], dim[:, :])
                v.tensor_mul(t3[:, :], t1[:, :], t1[:, :])
                v.tensor_mul(drs[:, :], t2[:, :], t2[:, :])
                v.tensor_add(t3[:, :], t3[:, :], drs[:, :])
                v.reciprocal(t3[:, :], t3[:, :])
                v.tensor_mul(gout[:, 0:128], t1[:, :], t3[:, :])
                v.tensor_mul(gout[:, 128:256], t2[:, :], t3[:, :])
                v.tensor_scalar_mul(gout[:, 128:256], gout[:, 128:256], -1.0
                                    ).then_inc(fin.h, 1)

            @block.scalar
            def _(s):
                s.wait_ge(fin.h, 1)
                s.activation(t1[:, :], t1[:, :], AF.Ln)
                s.activation(sfac[:, :], t1[:, :], AF.Exp, bias=0.0,
                             scale=-0.5).then_inc(fin.h, 1)


def emit_attn_core(b, io, sc, per):
    nc = b.nc
    x1T3, y1T3, y2T3 = fm(sc["x1T"]), fm(sc["y1T"]), fm(sc["y2T"])
    with ExitStack() as outer:
        O_ = outer.enter_context
        qkvT = O_(nc.sbuf_tensor("a_qkv", [128, 12 * T], BF16))
        qkvT3 = qkvT[:, :].rearrange("p (c t) -> p c t", c=12)

        _emit_p5a(b, io, sc, per, x1T3, y1T3)
        b.new_phase()
        _emit_p5b(b, io, sc, per, y1T3, qkvT3)
        b.new_phase()
        _emit_p6(b, io, sc, per, qkvT3, y1T3, y2T3)


def _emit_p5a(b, io, sc, per, x1T3, y1T3):
    nc = b.nc
    ones_r, ones_c, bbkc, ids = (per["ones_r"], per["ones_c"], per["bbkc"],
                                 per["ids"])
    with ExitStack() as ph:
        P = ph.enter_context
        x1t = [P(nc.sbuf_tensor(f"a_x1{j}", [128, KC * TT], F32))
               for j in range(2)]
        y1t = P(nc.sbuf_tensor("a_y1", [128, KC * TT], F32))
        sqt = P(nc.sbuf_tensor("a_sq", [128, KC * TT], F32))
        rowm = [P(nc.sbuf_tensor(f"a_rm{j}", [1, TT], F32)) for j in range(2)]
        rowr = [P(nc.sbuf_tensor(f"a_rr{j}", [1, TT], F32)) for j in range(2)]
        bkT = [P(nc.sbuf_tensor(f"a_bk{j}", [2, TT], F32)) for j in range(2)]
        wbk = P(nc.sbuf_tensor("a_wbk", [2, D], F32))
        s1tok = P(nc.sbuf_tensor("a_s1", [128, NT128], F32))
        s2tok = P(nc.sbuf_tensor("a_s2", [128, NT128], F32))
        stt = P(nc.sbuf_tensor("a_stt", [128, 4 * NT128], F32))
        tpo = P(nc.sbuf_tensor("a_tpo", [32, 128], F32))
        jk = P(nc.sbuf_tensor("a_jk", [128, 2], F32))
        psB1 = P(nc.psum_tensor("a_pB1", [128, 512], F32))
        psB2 = P(nc.psum_tensor("a_pB2", [128, 512], F32))
        psY = [P(nc.psum_tensor(f"a_pY{j}", [128, 512], F32)) for j in range(2)]
        psSum = [P(nc.psum_tensor(f"a_pS{j}", [128, 512], F32))
                 for j in range(2)]
        psT2 = P(nc.psum_tensor("a_pT", [128, 512], F32))

        ldx = [b.sem() for _ in range(2)]
        ldr = b.sem()
        ldr2 = [b.sem() for _ in range(2)]
        bcd = b.sem()
        ykd = b.sem()
        dvy = b.sem()
        sqd = b.sem()
        smm = b.sem()
        sev = b.sem()
        wb1 = b.sem()
        fin = b.sem()
        wbv = b.sem()

        m1v = lambda: stt[:, 0:32]
        varv = lambda: stt[:, 32:64]
        rstdv = lambda: stt[:, 64:96]
        tmpv = lambda: stt[:, 96:128]

        with nc.Block("p5a") as block:
            @block.sync
            def _(s):
                s.dma_start(wbk[:, :], io["w_bk"]).then_inc(ldr.h, ldr.inc(16))
                for ti in range(NTT):
                    j = ti % 2
                    ts_ = slice(ti * TT, (ti + 1) * TT)
                    if ti >= 2:
                        s.wait_ge(dvy.h, 8 * (ti - 1))
                        s.wait_ge(bcd.h, ti - 1)
                    s.dma_start(
                        x1t[j][:, :].rearrange("p (c t) -> p c t", c=KC),
                        x1T3[:, :, ts_]).then_inc(ldx[j].h, ldx[j].inc(16))
                    s.dma_start(rowm[j][:, :], sc["vecs"][V_M1:V_M1 + 1, ts_]
                                ).then_inc(ldr2[j].h, ldr2[j].inc(16))
                    s.dma_start(rowr[j][:, :],
                                sc["vecs"][V_RSTD1:V_RSTD1 + 1, ts_]
                                ).then_inc(ldr2[j].h, ldr2[j].inc(16))
                    s.dma_start(bkT[j][0:1, :],
                                sc["vecs"][V_BKRE:V_BKRE + 1, ts_]
                                ).then_inc(ldr2[j].h, ldr2[j].inc(16))
                    s.dma_start(bkT[j][1:2, :],
                                sc["vecs"][V_BKIM:V_BKIM + 1, ts_]
                                ).then_inc(ldr2[j].h, ldr2[j].inc(16))

            @block.tensor
            def _(t):
                t.wait_ge(ldr.h, 16)
                for ti in range(NTT):
                    j = ti % 2
                    t.wait_ge(ldr2[j].h, 64 * (ti // 2 + 1))
                    if ti >= 1:
                        t.wait_ge(dvy.h, 8 * ti)
                    t.matmul(psB1[:, :], ones_r[:, :], rowm[j][0:1, :],
                             start=True, stop=True)
                    t.matmul(psB2[:, :], ones_r[:, :], rowr[j][0:1, :],
                             start=True, stop=True).then_inc(bcd.h, bcd.inc())
                    for k in range(KC):
                        gi = 8 * ti + k
                        if gi >= 2:
                            t.wait_ge(dvy.h, gi - 1)
                        t.matmul(psY[gi % 2][:, :],
                                 wbk[:, k * 128:(k + 1) * 128],
                                 bkT[j][:, :], start=True, stop=True
                                 ).then_inc(ykd.h, ykd.inc())
                    t.wait_ge(dvy.h, 8 * (ti + 1))
                    t.wait_ge(sqd.h, ti + 1)
                    y13 = y1t[:, :].rearrange("p (c t) -> p c t", c=KC)
                    sq3 = sqt[:, :].rearrange("p (c t) -> p c t", c=KC)
                    for g in range(4):
                        gi4 = 4 * ti + g
                        gs = slice(g * 128, (g + 1) * 128)
                        if gi4 >= 2:
                            t.wait_ge(sev.h, gi4 - 1)
                        for k in range(KC):
                            t.matmul(psSum[gi4 % 2][:, 0:1], y13[:, k, gs],
                                     ones_c[:, :], start=(k == 0),
                                     stop=(k == KC - 1))
                        for k in range(KC):
                            inst = t.matmul(psSum[gi4 % 2][:, 1:2],
                                            sq3[:, k, gs], ones_c[:, :],
                                            start=(k == 0), stop=(k == KC - 1))
                        inst.then_inc(smm.h, smm.inc())
                t.wait_ge(fin.h, 3)
                t.transpose(psT2[0:32, 0:128], m1v(), ids[:, :]).then_inc(
                    fin.h, 1)
                t.wait_ge(fin.h, 5)
                t.transpose(psT2[0:32, 0:128], rstdv(), ids[:, :]).then_inc(
                    fin.h, 1)

            @block.vector
            def _(v):
                v.memset(jk[:, :], 0.0)
                for ti in range(NTT):
                    j = ti % 2
                    v.wait_ge(ldx[j].h, 16 * (ti // 2 + 1))
                    v.wait_ge(bcd.h, ti + 1)
                    if ti >= 1:
                        v.wait_ge(wb1.h, 16 * ti)
                        v.wait_ge(smm.h, 4 * ti)
                        v.wait_ge(sqd.h, ti)
                    x13 = x1t[j][:, :].rearrange("p (c t) -> p c t", c=KC)
                    y13 = y1t[:, :].rearrange("p (c t) -> p c t", c=KC)
                    sq3 = sqt[:, :].rearrange("p (c t) -> p c t", c=KC)
                    for k in range(KC):
                        gi = 8 * ti + k
                        v.wait_ge(ykd.h, gi + 1)
                        v.tensor_sub(y13[:, k, :], x13[:, k, :], psB1[:, :])
                        v.tensor_mul(y13[:, k, :], y13[:, k, :], psB2[:, :])
                        v.scalar_tensor_tensor(
                            y13[:, k, :], psY[gi % 2][:, :], bbkc[:, k:k + 1],
                            y13[:, k, :], ALU.add, ALU.add
                        ).then_inc(dvy.h, dvy.inc())
                    for g in range(4):
                        gi4 = 4 * ti + g
                        v.wait_ge(smm.h, gi4 + 1)
                        v.tensor_copy(s1tok[:, gi4:gi4 + 1],
                                      psSum[gi4 % 2][:, 0:1])
                        v.tensor_copy(s2tok[:, gi4:gi4 + 1],
                                      psSum[gi4 % 2][:, 1:2]).then_inc(
                            sev.h, sev.inc())
                v.tensor_scalar_mul(m1v(), s1tok[:, :], 1.0 / D)
                v.tensor_scalar_mul(tmpv(), s2tok[:, :], 1.0 / D)
                v.tensor_mul(varv(), m1v(), m1v())
                v.tensor_scalar_mul(jk[:, 0:1], jk[:, 1:2], 1.0)
                v.tensor_sub(varv(), tmpv(), varv()).then_inc(fin.h, 1)
                v.wait_ge(fin.h, 2)
                v.nop().then_inc(fin.h, 1)
                v.wait_ge(fin.h, 4)
                v.tensor_copy(tpo[:, :], psT2[0:32, 0:128]).then_inc(
                    fin.h, 1)
                v.wait_ge(fin.h, 6)
                v.wait_ge(wbv.h, 16)
                v.tensor_copy(tpo[:, :], psT2[0:32, 0:128]).then_inc(
                    fin.h, 1)

            @block.scalar
            def _(s):
                y13 = y1t[:, :].rearrange("p (c t) -> p c t", c=KC)
                sq3 = sqt[:, :].rearrange("p (c t) -> p c t", c=KC)
                for ti in range(NTT):
                    s.wait_ge(dvy.h, 8 * (ti + 1))
                    if ti >= 1:
                        s.wait_ge(smm.h, 4 * ti)
                    for k in range(KC):
                        inst = s.activation(sq3[:, k, :], y13[:, k, :],
                                            AF.Square)
                    inst.then_inc(sqd.h, sqd.inc())
                s.wait_ge(fin.h, 1)
                s.activation(rstdv(), varv(), AF.Ln, bias=EPS, scale=1.0)
                s.activation(rstdv(), rstdv(), AF.Exp, bias=0.0, scale=-0.5
                             ).then_inc(fin.h, 1)

            @block.gpsimd
            def _(g):
                for ti in range(NTT):
                    g.wait_ge(dvy.h, 8 * (ti + 1))
                    if ti >= 1:
                        g.wait_ge(wb1.h, 16 * ti)
                    g.dma_start(y1T3[:, :, ti * TT:(ti + 1) * TT],
                                y1t[:, :].rearrange("p (c t) -> p c t", c=KC)
                                ).then_inc(wb1.h, wb1.inc(16))
                for r, row in enumerate([V_MA, V_RSTDA]):
                    g.wait_ge(fin.h, 5 + 2 * r)
                    g.dma_start(sc["vecs"][row:row + 1, :].rearrange(
                        "r (c t) -> (r c) t", c=32),
                        tpo[:, :]).then_inc(wbv.h, wbv.inc(16))
                g.wait_ge(wbv.h, wbv.n)


def _emit_p5b(b, io, sc, per, y1T3, qkvT3):
    nc = b.nc
    ones_r, gv, bv, bqkvc = (per["ones_r"], per["gv"], per["bv"],
                             per["bqkvc"])
    with ExitStack() as ph:
        P = ph.enter_context
        y1b = [P(nc.sbuf_tensor(f"b_y1{j}", [128, KC * TT], F32))
               for j in range(2)]
        xnt = [P(nc.sbuf_tensor(f"b_xn{j}", [128, KC * TT], BF16))
               for j in range(2)]
        wq = P(nc.sbuf_tensor("b_wq", [128, KC * 1536], BF16))
        rowm = [P(nc.sbuf_tensor(f"b_rm{j}", [1, TT], F32)) for j in range(2)]
        rowr = [P(nc.sbuf_tensor(f"b_rr{j}", [1, TT], F32)) for j in range(2)]
        psB1 = P(nc.psum_tensor("b_pB1", [128, 512], F32))
        psB2 = P(nc.psum_tensor("b_pB2", [128, 512], F32))
        psQ = [P(nc.psum_tensor(f"b_pQ{j}", [128, 512], F32)) for j in range(2)]
        wq3 = wq[:, :].rearrange("p (c n) -> p c n", c=KC)

        ldx = [b.sem() for _ in range(2)]
        ldr2 = [b.sem() for _ in range(2)]
        ldw = b.sem()
        bcd = b.sem()
        dvx = b.sem()
        qmm = b.sem()
        qev = b.sem()

        with nc.Block("p5b") as block:
            @block.sync
            def _(s):
                for ti in range(NTT):
                    j = ti % 2
                    ts_ = slice(ti * TT, (ti + 1) * TT)
                    if ti >= 2:
                        s.wait_ge(dvx.h, ti - 1)
                        s.wait_ge(bcd.h, ti - 1)
                    s.dma_start(
                        y1b[j][:, :].rearrange("p (c t) -> p c t", c=KC),
                        y1T3[:, :, ts_]).then_inc(ldx[j].h, ldx[j].inc(16))
                    s.dma_start(rowm[j][:, :], sc["vecs"][V_MA:V_MA + 1, ts_]
                                ).then_inc(ldr2[j].h, ldr2[j].inc(16))
                    s.dma_start(rowr[j][:, :],
                                sc["vecs"][V_RSTDA:V_RSTDA + 1, ts_]
                                ).then_inc(ldr2[j].h, ldr2[j].inc(16))

            @block.gpsimd
            def _(g):
                g.dma_start(wq3, io["w_qkv"].rearrange(
                    "(c p) n -> p c n", p=128)).then_inc(ldw.h, ldw.inc(16))

            @block.tensor
            def _(t):
                t.wait_ge(ldw.h, 16)
                for ti in range(NTT):
                    j = ti % 2
                    ts_ = slice(ti * TT, (ti + 1) * TT)
                    t.wait_ge(ldr2[j].h, 32 * (ti // 2 + 1))
                    if ti >= 1:
                        t.wait_ge(dvx.h, ti)
                    t.matmul(psB1[:, :], ones_r[:, :], rowm[j][0:1, :],
                             start=True, stop=True)
                    t.matmul(psB2[:, :], ones_r[:, :], rowr[j][0:1, :],
                             start=True, stop=True).then_inc(bcd.h, bcd.inc())
                    t.wait_ge(dvx.h, ti + 1)
                    xn3 = xnt[ti % 2][:, :].rearrange("p (c t) -> p c t", c=KC)
                    for n in range(12):
                        gi = 12 * ti + n
                        if gi >= 2:
                            t.wait_ge(qev.h, gi - 1)
                        for k in range(KC):
                            inst = t.matmul(
                                psQ[gi % 2][:, :],
                                wq3[:, k, n * 128:(n + 1) * 128],
                                xn3[:, k, :], start=(k == 0),
                                stop=(k == KC - 1))
                        inst.then_inc(qmm.h, qmm.inc())

            @block.vector
            def _(v):
                for ti in range(NTT):
                    j = ti % 2
                    v.wait_ge(ldx[j].h, 16 * (ti // 2 + 1))
                    v.wait_ge(bcd.h, ti + 1)
                    if ti >= 2:
                        v.wait_ge(qmm.h, 12 * (ti - 1))
                    y13 = y1b[j][:, :].rearrange("p (c t) -> p c t", c=KC)
                    xn3 = xnt[ti % 2][:, :].rearrange("p (c t) -> p c t", c=KC)
                    for k in range(KC):
                        v.tensor_sub(y13[:, k, :], y13[:, k, :], psB1[:, :])
                        v.tensor_mul(y13[:, k, :], y13[:, k, :], psB2[:, :])
                        inst = v.tensor_scalar(
                            xn3[:, k, :], y13[:, k, :],
                            gv[:, KC + k:KC + k + 1], bv[:, KC + k:KC + k + 1],
                            ALU.mult, ALU.add)
                    inst.then_inc(dvx.h, dvx.inc())

            @block.scalar
            def _(s):
                for ti in range(NTT):
                    ts_ = slice(ti * TT, (ti + 1) * TT)
                    for n in range(12):
                        gi = 12 * ti + n
                        s.wait_ge(qmm.h, gi + 1)
                        s.activation(
                            qkvT3[:, n, ts_], psQ[gi % 2][:, :], AF.Identity,
                            bias=bqkvc[:, n:n + 1], scale=1.0
                        ).then_inc(qev.h, qev.inc())


def _emit_p6(b, io, sc, per, qkvT3, y1T3, y2T3):
    nc = b.nc
    ones_r, ids, idsb, trileta, boutc = (per["ones_r"], per["ids"],
                                         per["idsb"], per["trileta"],
                                         per["boutc"])
    with ExitStack() as ph:
        P = ph.enter_context
        cumr = P(nc.sbuf_tensor("h_cum", [1, T], F32))
        epr = P(nc.sbuf_tensor("h_ep", [1, T], F32))
        scr = P(nc.sbuf_tensor("h_sc", [1, 32], F32))
        one1 = P(nc.sbuf_tensor("h_one1", [1, 1], F32))
        Dm = P(nc.sbuf_tensor("h_Dm", [128, 128], F32))
        kcol = P(nc.sbuf_tensor("h_kc", [128, 1], F32))
        epcol = P(nc.sbuf_tensor("h_epc", [128, 1], F32))
        sccol = P(nc.sbuf_tensor("h_scc", [128, 1], F32))
        ktm = P(nc.sbuf_tensor("h_ktm", [128, 128], BF16))
        vtm = P(nc.sbuf_tensor("h_vtm", [128, 128], BF16))
        qtm = P(nc.sbuf_tensor("h_qtm", [128, 128], BF16))
        qfm = P(nc.sbuf_tensor("h_qfm", [128, 128], BF16))
        Pt = [P(nc.sbuf_tensor(f"h_Pt{j}", [128, 128], BF16)) for j in range(2)]
        Wst = P(nc.sbuf_tensor("h_W", [128, 4 * 64], F32))
        Wb = P(nc.sbuf_tensor("h_Wb", [128, 64], BF16))
        jk = P(nc.sbuf_tensor("h_jk", [128, 2], F32))
        oc = [P(nc.sbuf_tensor(f"h_oc{j}", [64, 8 * 128], BF16))
              for j in range(2)]
        wo = P(nc.sbuf_tensor("h_wo", [64, 8 * D], BF16))
        y1s = [P(nc.sbuf_tensor(f"h_y1{j}", [128, KC * 128], F32))
               for j in range(2)]
        y2s = [P(nc.sbuf_tensor(f"h_y2{j}", [128, KC * 128], F32))
               for j in range(2)]
        pB1 = P(nc.psum_tensor("h_B1", [128, 1024], BF16))
        pB2 = P(nc.psum_tensor("h_B2", [128, 1024], BF16))
        pB3 = P(nc.psum_tensor("h_B3", [128, 1024], BF16))
        pF1 = P(nc.psum_tensor("h_F1", [128, 512], F32))
        pF2 = P(nc.psum_tensor("h_F2", [128, 512], F32))
        pDX = P(nc.psum_tensor("h_DX", [128, 512], F32))
        pW = [P(nc.psum_tensor(f"h_pW{j}", [128, 512], F32)) for j in range(2)]

        ldr = b.sem()
        ldw = b.sem()
        ldy = [b.sem() for _ in range(2)]
        acr = b.sem()
        pe6 = b.sem()
        dv6 = b.sem()
        wb2 = b.sem()
        Wst3 = Wst[:, :].rearrange("p (c x) -> p c x", c=4)
        wo3 = wo[:, :].rearrange("p (c n) -> p c n", c=8)

        PEC = 45

        def pe_cnt(j, c2, e):
            return PEC * j + (1 if e == 0 else 1 + 9 * c2 + e)

        def pe_wout(j, k):
            return PEC * j + 37 + k + 1

        def dv_cnt(j, c2, e):
            return 1 + PEC * j + (1 if e == 0 else 1 + 9 * c2 + e)

        def dv_wout(j, k):
            return 1 + PEC * j + 37 + k + 1

        def prev_pair(j, c2):
            return (j, c2 - 1) if c2 > 0 else (j - 1, 3)

        with nc.Block("p6") as block:
            @block.sync
            def _(s):
                s.dma_start(cumr[:, :], sc["vecs"][V_CUM:V_CUM + 1, :]
                            ).then_inc(ldr.h, ldr.inc(16))
                s.dma_start(scr[:, :], sc["vecs"][V_CUM:V_CUM + 1, 127:T:128],
                            allow_slow_non_contiguous=True
                            ).then_inc(ldr.h, ldr.inc(16))
                for j2 in range(NHCH):
                    sl = j2 % 2
                    if j2 >= 2:
                        s.wait_ge(dv6.h, dv_wout(j2 - 2, 7))
                    s.dma_start(
                        y1s[sl][:, :].rearrange("p (c t) -> p c t", c=KC),
                        y1T3[:, :, j2 * 128:(j2 + 1) * 128]
                    ).then_inc(ldy[sl].h, ldy[sl].inc(16))

            @block.gpsimd
            def _(g):
                g.dma_start(wo3, io["w_out"].rearrange("(c p) n -> p c n",
                                                       p=64)
                            ).then_inc(ldw.h, ldw.inc(16))
                for j2 in range(NHCH):
                    sl = j2 % 2
                    g.wait_ge(dv6.h, dv_wout(j2, 7))
                    if j2 >= 2:
                        g.wait_ge(wb2.h, 16 * (j2 - 1))
                    g.dma_start(y2T3[:, :, j2 * 128:(j2 + 1) * 128],
                                y2s[sl][:, :].rearrange("p (c t) -> p c t",
                                                        c=KC)
                                ).then_inc(wb2.h, wb2.inc(16))
                g.wait_ge(wb2.h, wb2.n)

            @block.scalar
            def _(s):
                s.wait_ge(ldr.h, 32)
                s.activation(epr[:, :], cumr[:, :], AF.Exp)
                s.activation(cumr[:, :], cumr[:, :], AF.Exp, bias=0.0,
                             scale=-1.0)
                s.activation(scr[:, :], scr[:, :], AF.Exp).then_inc(
                    acr.h, acr.inc())

            enr = cumr

            @block.tensor
            def _(t):
                t.wait_ge(acr.h, 1)
                t.wait_ge(dv6.h, 1)
                t.wait_ge(ldw.h, 16)
                for j in range(NHCH):
                    js = slice(j * HCH, (j + 1) * HCH)
                    if j >= 1:
                        t.wait_ge(dv6.h, dv_cnt(j - 1, 0, 0))
                    t.matmul(pDX[:, 0:128], enr[0:1, js], epr[0:1, js],
                             start=True, stop=True)
                    t.matmul(pDX[:, 128:129], enr[0:1, js], scr[0:1, j:j + 1],
                             start=True, stop=True)
                    t.matmul(pDX[:, 129:130], ones_r[:, :], scr[0:1, j:j + 1],
                             start=True, stop=True)
                    t.matmul(pDX[:, 130:131], epr[0:1, js], one1[0:1, 0:1],
                             start=True, stop=True).then_inc(pe6.h, 1)
                    for c2 in range(4):
                        pj, pc = prev_pair(j, c2)
                        kview = qkvT3[:, 4 + c2, js]
                        vview = qkvT3[:, 8 + c2, js]
                        qview = qkvT3[:, c2, js]
                        if pj >= 0:
                            t.wait_ge(dv6.h, dv_cnt(pj, pc, 4))
                        t.transpose(pB1[:, 0:128], kview, idsb[:, :]
                                    ).then_inc(pe6.h, 1)
                        if pj >= 0:
                            t.wait_ge(dv6.h, dv_cnt(pj, pc, 2))
                        t.transpose(pB2[:, 0:128], vview, idsb[:, :]
                                    ).then_inc(pe6.h, 1)
                        if pj >= 0:
                            t.wait_ge(dv6.h, dv_cnt(pj, pc, 3))
                        t.transpose(pB3[:, 0:128], qview, idsb[:, :]
                                    ).then_inc(pe6.h, 1)
                        t.wait_ge(dv6.h, dv_cnt(j, c2, 3))
                        t.transpose(pB1[:, 0:128], qtm[:, :], idsb[:, :]
                                    ).then_inc(pe6.h, 1)
                        if pj >= 0:
                            t.wait_ge(dv6.h, dv_cnt(pj, pc, 9))
                        t.matmul(pF1[:, 0:128], kview[0:64, :], qview[0:64, :],
                                 start=True, stop=True).then_inc(pe6.h, 1)
                        if pj >= 0:
                            t.wait_ge(dv6.h, dv_cnt(pj, pc, 8))
                        t.matmul(pF2[:, 0:128], kview[64:128, :],
                                 qview[64:128, :], start=True, stop=True
                                 ).then_inc(pe6.h, 1)
                        t.wait_ge(dv6.h, dv_cnt(j, c2, 5))
                        t.matmul(pF1[:, 0:128], ktm[:, :], vtm[:, :],
                                 start=True, stop=True).then_inc(pe6.h, 1)
                        t.wait_ge(dv6.h, dv_cnt(j, c2, 7))
                        t.matmul(pF2[0:64, 0:128], Wb[0:64, :], qfm[0:64, :],
                                 start=True, stop=False)
                        t.matmul(pF2[0:64, 0:128], vtm[:, 0:64], Pt[0][:, :],
                                 start=False, stop=True).then_inc(pe6.h, 1)
                        t.matmul(pF1[0:64, 0:128], Wb[64:128, :],
                                 qfm[64:128, :], start=True, stop=False)
                        t.matmul(pF1[0:64, 0:128], vtm[:, 64:128], Pt[1][:, :],
                                 start=False, stop=True).then_inc(pe6.h, 1)
                    t.wait_ge(dv6.h, dv_cnt(j, 3, 9))
                    oc3 = oc[j % 2][:, :].rearrange("p (c t) -> p c t", c=8)
                    for k in range(KC):
                        if pe_wout(j, k) > 2:
                            pass
                        if k >= 2 or j >= 1:
                            kk_prev = (j, k - 2) if k >= 2 else (j - 1, k + 6)
                            t.wait_ge(dv6.h, dv_wout(*kk_prev))
                        for kk in range(8):
                            inst = t.matmul(
                                pW[k % 2][:, 0:128],
                                wo3[:, kk, k * 128:(k + 1) * 128],
                                oc3[:, kk, :], start=(kk == 0), stop=(kk == 7))
                        inst.then_inc(pe6.h, 1)

            @block.vector
            def _(v):
                v.memset(Wst[:, :], 0.0)
                v.memset(one1[:, :], 1.0)
                v.memset(jk[:, :], 0.0)
                v.tensor_scalar_mul(jk[:, 0:1], jk[:, 1:2], 1.0)
                v.tensor_scalar_mul(jk[:, 0:1], jk[:, 1:2], 1.0
                                    ).then_inc(dv6.h, 1)
                for j in range(NHCH):
                    js = slice(j * HCH, (j + 1) * HCH)
                    sl = j % 2
                    v.wait_ge(pe6.h, pe_cnt(j, 0, 0))
                    v.tensor_mul(Dm[:, :], pDX[:, 0:128], trileta[:, :])
                    v.tensor_copy(kcol[:, :], pDX[:, 128:129])
                    v.tensor_copy(sccol[:, :], pDX[:, 129:130])
                    v.tensor_copy(epcol[:, :], pDX[:, 130:131]).then_inc(
                        dv6.h, 1)
                    for c2 in range(4):
                        v.wait_ge(pe6.h, pe_cnt(j, c2, 1))
                        v.tensor_scalar_mul(ktm[:, :], pB1[:, 0:128],
                                            kcol[:, 0:1]).then_inc(dv6.h, 1)
                        v.wait_ge(pe6.h, pe_cnt(j, c2, 2))
                        v.tensor_copy(vtm[:, :], pB2[:, 0:128]).then_inc(
                            dv6.h, 1)
                        v.wait_ge(pe6.h, pe_cnt(j, c2, 3))
                        v.tensor_scalar_mul(qtm[:, :], pB3[:, 0:128],
                                            epcol[:, 0:1]).then_inc(dv6.h, 1)
                        v.wait_ge(pe6.h, pe_cnt(j, c2, 4))
                        v.tensor_copy(qfm[:, :], pB1[:, 0:128]).then_inc(
                            dv6.h, 1)
                        v.wait_ge(pe6.h, pe_cnt(j, c2, 5))
                        v.tensor_mul(Pt[0][:, :], pF1[:, 0:128], Dm[:, :]
                                     ).then_inc(dv6.h, 1)
                        v.wait_ge(pe6.h, pe_cnt(j, c2, 6))
                        v.tensor_mul(Pt[1][:, :], pF2[:, 0:128], Dm[:, :]
                                     ).then_inc(dv6.h, 1)
                        v.wait_ge(pe6.h, pe_cnt(j, c2, 7))
                        v.tensor_copy(Wb[:, :], Wst3[:, c2, :])
                        for hh in range(2):
                            bs = slice(hh * 64, hh * 64 + 64)
                            v.tensor_scalar_mul(Wst3[:, c2, :][bs, :],
                                                Wst3[:, c2, :][bs, :],
                                                sccol[bs, 0:1])
                            v.scalar_tensor_tensor(
                                Wst3[:, c2, :][bs, :],
                                pF1[bs, 0:128][:, slice(hh * 64, hh * 64 + 64)],
                                ETA, Wst3[:, c2, :][bs, :],
                                ALU.mult, ALU.add)
                        v.tensor_scalar_mul(jk[:, 0:1], jk[:, 1:2], 1.0
                                            ).then_inc(dv6.h, 1)
                        oc3 = oc[sl][:, :].rearrange("p (c t) -> p c t", c=8)
                        v.wait_ge(pe6.h, pe_cnt(j, c2, 8))
                        if j >= 2 and c2 == 0:
                            v.wait_ge(pe6.h, pe_wout(j - 2, 7))
                        v.tensor_copy(oc3[:, 2 * c2, :], pF2[0:64, 0:128]
                                      ).then_inc(dv6.h, 1)
                        v.wait_ge(pe6.h, pe_cnt(j, c2, 9))
                        v.tensor_copy(oc3[:, 2 * c2 + 1, :], pF1[0:64, 0:128]
                                      ).then_inc(dv6.h, 1)
                    v.wait_ge(ldy[sl].h, 16 * (j // 2 + 1))
                    if j >= 2:
                        v.wait_ge(wb2.h, 16 * (j - 1))
                    y13 = y1s[sl][:, :].rearrange("p (c t) -> p c t", c=KC)
                    y23 = y2s[sl][:, :].rearrange("p (c t) -> p c t", c=KC)
                    for k in range(KC):
                        v.wait_ge(pe6.h, pe_wout(j, k))
                        v.scalar_tensor_tensor(
                            y23[:, k, :], pW[k % 2][:, 0:128],
                            boutc[:, k:k + 1], y13[:, k, :],
                            ALU.add, ALU.add).then_inc(dv6.h, 1)


def emit_y2stats(b, io, sc, per):
    nc = b.nc
    ones_c, ids = per["ones_c"], per["ids"]
    y2T3 = fm(sc["y2T"])
    with ExitStack() as ph:
        P = ph.enter_context
        y2t = [P(nc.sbuf_tensor(f"w_y2{j}", [128, KC * TT], F32))
               for j in range(2)]
        sqt = P(nc.sbuf_tensor("w_sq", [128, KC * TT], F32))
        s1tok = P(nc.sbuf_tensor("w_s1", [128, NT128], F32))
        s2tok = P(nc.sbuf_tensor("w_s2", [128, NT128], F32))
        stt = P(nc.sbuf_tensor("w_stt", [128, 4 * NT128], F32))
        tpo = P(nc.sbuf_tensor("w_tpo", [32, 128], F32))
        jk = P(nc.sbuf_tensor("w_jk", [128, 2], F32))
        psSum = [P(nc.psum_tensor(f"w_pS{j}", [128, 512], F32))
                 for j in range(2)]
        psT2 = P(nc.psum_tensor("w_pT", [128, 512], F32))

        ldx = [b.sem() for _ in range(2)]
        dvq = b.sem()
        smm = b.sem()
        sev = b.sem()
        fin = b.sem()
        wbv = b.sem()

        m1v = lambda: stt[:, 0:32]
        varv = lambda: stt[:, 32:64]
        rstdv = lambda: stt[:, 64:96]
        tmpv = lambda: stt[:, 96:128]

        with nc.Block("p7b") as block:
            @block.sync
            def _(s):
                for ti in range(NTT):
                    j = ti % 2
                    if ti >= 2:
                        s.wait_ge(smm.h, 4 * (ti - 1))
                    s.dma_start(
                        y2t[j][:, :].rearrange("p (c t) -> p c t", c=KC),
                        y2T3[:, :, ti * TT:(ti + 1) * TT]
                    ).then_inc(ldx[j].h, ldx[j].inc(16))

            @block.vector
            def _(v):
                v.memset(jk[:, :], 0.0)
                for ti in range(NTT):
                    for g in range(4):
                        gi4 = 4 * ti + g
                        v.wait_ge(smm.h, gi4 + 1)
                        v.tensor_copy(s1tok[:, gi4:gi4 + 1],
                                      psSum[gi4 % 2][:, 0:1])
                        v.tensor_copy(s2tok[:, gi4:gi4 + 1],
                                      psSum[gi4 % 2][:, 1:2]).then_inc(
                            sev.h, sev.inc())
                v.tensor_scalar_mul(m1v(), s1tok[:, :], 1.0 / D)
                v.tensor_scalar_mul(tmpv(), s2tok[:, :], 1.0 / D)
                v.tensor_mul(varv(), m1v(), m1v())
                v.tensor_scalar_mul(jk[:, 0:1], jk[:, 1:2], 1.0)
                v.tensor_sub(varv(), tmpv(), varv()).then_inc(fin.h, 1)
                v.wait_ge(fin.h, 2)
                v.nop().then_inc(fin.h, 1)
                v.wait_ge(fin.h, 4)
                v.tensor_copy(tpo[:, :], psT2[0:32, 0:128]).then_inc(
                    fin.h, 1)
                v.wait_ge(fin.h, 6)
                v.wait_ge(wbv.h, 16)
                v.tensor_copy(tpo[:, :], psT2[0:32, 0:128]).then_inc(
                    fin.h, 1)

            @block.tensor
            def _(t):
                for ti in range(NTT):
                    j = ti % 2
                    t.wait_ge(dvq.h, ti + 1)
                    y23 = y2t[j][:, :].rearrange("p (c t) -> p c t", c=KC)
                    sq3 = sqt[:, :].rearrange("p (c t) -> p c t", c=KC)
                    for g in range(4):
                        gi4 = 4 * ti + g
                        gs = slice(g * 128, (g + 1) * 128)
                        if gi4 >= 2:
                            t.wait_ge(sev.h, gi4 - 1)
                        for k in range(KC):
                            t.matmul(psSum[gi4 % 2][:, 0:1], y23[:, k, gs],
                                     ones_c[:, :], start=(k == 0),
                                     stop=(k == KC - 1))
                        for k in range(KC):
                            inst = t.matmul(psSum[gi4 % 2][:, 1:2],
                                            sq3[:, k, gs], ones_c[:, :],
                                            start=(k == 0), stop=(k == KC - 1))
                        inst.then_inc(smm.h, smm.inc())
                t.wait_ge(fin.h, 3)
                t.transpose(psT2[0:32, 0:128], m1v(), ids[:, :]).then_inc(
                    fin.h, 1)
                t.wait_ge(fin.h, 5)
                t.transpose(psT2[0:32, 0:128], rstdv(), ids[:, :]).then_inc(
                    fin.h, 1)

            @block.scalar
            def _(s):
                for ti in range(NTT):
                    j = ti % 2
                    s.wait_ge(ldx[j].h, 16 * (ti // 2 + 1))
                    if ti >= 1:
                        s.wait_ge(smm.h, 4 * ti)
                    y23 = y2t[j][:, :].rearrange("p (c t) -> p c t", c=KC)
                    sq3 = sqt[:, :].rearrange("p (c t) -> p c t", c=KC)
                    for k in range(KC):
                        inst = s.activation(sq3[:, k, :], y23[:, k, :],
                                            AF.Square)
                    inst.then_inc(dvq.h, dvq.inc())
                s.wait_ge(fin.h, 1)
                s.activation(rstdv(), varv(), AF.Ln, bias=EPS, scale=1.0)
                s.activation(rstdv(), rstdv(), AF.Exp, bias=0.0, scale=-0.5
                             ).then_inc(fin.h, 1)

            @block.gpsimd
            def _(g):
                for r, row in enumerate([V_M3, V_RSTD3]):
                    g.wait_ge(fin.h, 5 + 2 * r)
                    g.dma_start(sc["vecs"][row:row + 1, :].rearrange(
                        "r (c t) -> (r c) t", c=32),
                        tpo[:, :]).then_inc(wbv.h, wbv.inc(16))
                g.wait_ge(wbv.h, wbv.n)


def emit_ffn_x2(b, io, sc, per):
    nc = b.nc
    ones_r, ones_c, gv, bv, bff1c, bff2c, ids = (
        per["ones_r"], per["ones_c"], per["gv"], per["bv"], per["bff1c"],
        per["bff2c"], per["ids"])
    y2T3, x1T3, x2T3 = fm(sc["y2T"]), fm(sc["x1T"]), fm(sc["x2T"])
    with ExitStack() as ph:
        P = ph.enter_context
        y2t = [P(nc.sbuf_tensor(f"f_y2{j}", [128, KC * TT], F32))
               for j in range(2)]
        x1t = P(nc.sbuf_tensor("f_x1", [128, KC * TT], F32))
        x2t = P(nc.sbuf_tensor("f_x2", [128, KC * TT], F32))
        sqt = P(nc.sbuf_tensor("f_sq", [128, KC * TT], F32))
        xnt = [P(nc.sbuf_tensor(f"f_xn{j}", [128, KC * TT], BF16))
               for j in range(2)]
        wf1 = [P(nc.sbuf_tensor(f"f_w1{j}", [128, 8 * D], BF16))
               for j in range(2)]
        wf2 = [P(nc.sbuf_tensor(f"f_w2{j}", [128, 8 * D], BF16))
               for j in range(2)]
        glb = [P(nc.sbuf_tensor(f"f_gl{j}", [128, 8 * TT], BF16))
               for j in range(2)]
        ffa = P(nc.sbuf_tensor("f_ffa", [128, KC * TT], F32))
        rowm = [P(nc.sbuf_tensor(f"f_rm{j}", [1, TT], F32)) for j in range(2)]
        rowr = [P(nc.sbuf_tensor(f"f_rr{j}", [1, TT], F32)) for j in range(2)]
        s1tok = P(nc.sbuf_tensor("f_s1", [128, NT128], F32))
        s2tok = P(nc.sbuf_tensor("f_s2", [128, NT128], F32))
        stt = P(nc.sbuf_tensor("f_stt", [128, 4 * NT128], F32))
        tpo = P(nc.sbuf_tensor("f_tpo", [32, 128], F32))
        jk = P(nc.sbuf_tensor("f_jk", [128, 2], F32))
        psB1 = P(nc.psum_tensor("f_pB1", [128, 512], F32))
        psB2 = P(nc.psum_tensor("f_pB2", [128, 512], F32))
        psF = [P(nc.psum_tensor(f"f_pF{j}", [128, 512], F32)) for j in range(2)]
        psC = [P(nc.psum_tensor(f"f_pC{j}", [128, 512], F32)) for j in range(2)]
        psSum = [P(nc.psum_tensor(f"f_pS{j}", [128, 512], F32))
                 for j in range(2)]


        ldy = [b.sem() for _ in range(2)]
        ldx1 = b.sem()
        ldw1 = [b.sem() for _ in range(2)]
        ldw2 = [b.sem() for _ in range(2)]
        ldr2 = [b.sem() for _ in range(2)]
        bcd = b.sem()
        dvn = b.sem()
        sqd = b.sem()
        fmm = b.sem()
        gld = b.sem()
        cmm = b.sem()
        cev = b.sem()
        dvx = b.sem()
        smm = b.sem()
        sev = b.sem()
        wb1 = b.sem()
        fin = b.sem()
        wbv = b.sem()
        pw2 = b.sem()

        m1v = lambda: stt[:, 0:32]
        varv = lambda: stt[:, 32:64]
        rstdv = lambda: stt[:, 64:96]
        tmpv = lambda: stt[:, 96:128]

        w2r = io["w_ff2"].rearrange("(b c p) n -> b p c n", p=128, c=8)
        w1r = io["w_ff1"].rearrange("(c p) (b n) -> b p c n", p=128, b=4)

        with nc.Block("p8") as block:
            @block.sync
            def _(s):
                for ti in range(NTT):
                    j = ti % 2
                    ts_ = slice(ti * TT, (ti + 1) * TT)
                    if ti >= 2:
                        s.wait_ge(dvn.h, ti - 1)
                        s.wait_ge(dvx.h, 8 * (ti - 1))
                        s.wait_ge(bcd.h, ti - 1)
                    s.dma_start(
                        y2t[j][:, :].rearrange("p (c t) -> p c t", c=KC),
                        y2T3[:, :, ts_]).then_inc(ldy[j].h, ldy[j].inc(16))
                    if ti >= 1:
                        s.wait_ge(dvx.h, 8 * ti)
                    s.dma_start(
                        x1t[:, :].rearrange("p (c t) -> p c t", c=KC),
                        x1T3[:, :, ts_]).then_inc(ldx1.h, ldx1.inc(16))
                    s.dma_start(rowm[j][:, :], sc["vecs"][V_M3:V_M3 + 1, ts_]
                                ).then_inc(ldr2[j].h, ldr2[j].inc(16))
                    s.dma_start(rowr[j][:, :],
                                sc["vecs"][V_RSTD3:V_RSTD3 + 1, ts_]
                                ).then_inc(ldr2[j].h, ldr2[j].inc(16))

            @block.gpsimd
            def _(g):
                for ti in range(NTT):
                    for kb in range(4):
                        j = (4 * ti + kb) % 2
                        qb = 4 * ti + kb
                        if qb >= 2:
                            g.wait_ge(fmm.h, 8 * (qb - 1))
                        g.dma_start(wf1[j][:, :].rearrange(
                            "p (c n) -> p c n", c=KC), w1r[kb]
                        ).then_inc(ldw1[j].h, ldw1[j].inc(16))
                        if qb >= 2:
                            g.wait_ge(pw2.h, qb - 1)
                        g.dma_start(wf2[j][:, :].rearrange(
                            "p (c n) -> p c n", c=8), w2r[kb]
                        ).then_inc(ldw2[j].h, ldw2[j].inc(16))
                    g.wait_ge(dvx.h, 8 * (ti + 1))
                    if ti >= 1:
                        g.wait_ge(wb1.h, 16 * ti)
                    g.dma_start(x2T3[:, :, ti * TT:(ti + 1) * TT],
                                x2t[:, :].rearrange("p (c t) -> p c t", c=KC)
                                ).then_inc(wb1.h, wb1.inc(16))
                for r, row in enumerate([V_M4, V_RSTD4]):
                    g.wait_ge(fin.h, 5 + 2 * r)
                    g.dma_start(sc["vecs"][row:row + 1, :].rearrange(
                        "r (c t) -> (r c) t", c=32),
                        tpo[:, :]).then_inc(wbv.h, wbv.inc(16))
                g.wait_ge(wbv.h, wbv.n)

            @block.tensor
            def _(t):
                for ti in range(NTT):
                    j = ti % 2
                    ts_ = slice(ti * TT, (ti + 1) * TT)
                    t.wait_ge(ldr2[j].h, 32 * (ti // 2 + 1))
                    if ti >= 1:
                        t.wait_ge(dvn.h, ti)
                    t.matmul(psB1[:, :], ones_r[:, :], rowm[j][0:1, :],
                             start=True, stop=True)
                    t.matmul(psB2[:, :], ones_r[:, :], rowr[j][0:1, :],
                             start=True, stop=True).then_inc(bcd.h, bcd.inc())
                    t.wait_ge(dvn.h, ti + 1)
                    xn3 = xnt[ti % 2][:, :].rearrange("p (c t) -> p c t", c=KC)
                    for kb in range(4):
                        jw = (4 * ti + kb) % 2
                        qb = 4 * ti + kb
                        wf23 = wf2[jw][:, :].rearrange("p (c n) -> p c n", c=8)
                        wf13 = wf1[jw][:, :].rearrange("p (c n) -> p c n",
                                                       c=KC)
                        t.wait_ge(ldw1[jw].h, 16 * (qb // 2 + 1))
                        for nn in range(8):
                            n = 8 * kb + nn
                            gi = 32 * ti + n
                            if gi >= 2:
                                t.wait_ge(gld.h, gi - 1)
                            for k in range(KC):
                                inst = t.matmul(
                                    psF[gi % 2][:, :],
                                    wf13[:, k, nn * 128:(nn + 1) * 128],
                                    xn3[:, k, :], start=(k == 0),
                                    stop=(k == KC - 1))
                            inst.then_inc(fmm.h, fmm.inc())
                        t.wait_ge(ldw2[jw].h, 16 * (qb // 2 + 1))
                        t.wait_ge(gld.h, 32 * ti + 8 * kb + 8)
                        gl3 = glb[(4 * ti + kb) % 2][:, :].rearrange(
                            "p (c t) -> p c t", c=8)
                        for mo in range(KC):
                            ci = 32 * ti + 8 * kb + mo
                            if ci >= 2:
                                t.wait_ge(cev.h, ci - 1)
                            for k2 in range(8):
                                inst = t.matmul(
                                    psC[ci % 2][:, :],
                                    wf23[:, k2, mo * 128:(mo + 1) * 128],
                                    gl3[:, k2, :], start=(k2 == 0),
                                    stop=(k2 == 7))
                            inst.then_inc(cmm.h, cmm.inc())
                        t.nop().then_inc(pw2.h, pw2.inc())
                    t.wait_ge(dvx.h, 8 * (ti + 1))
                    t.wait_ge(sqd.h, ti + 1)
                    x23 = x2t[:, :].rearrange("p (c t) -> p c t", c=KC)
                    sq3 = sqt[:, :].rearrange("p (c t) -> p c t", c=KC)
                    for g in range(4):
                        gi4 = 4 * ti + g
                        gs = slice(g * 128, (g + 1) * 128)
                        if gi4 >= 2:
                            t.wait_ge(sev.h, gi4 - 1)
                        for k in range(KC):
                            t.matmul(psSum[gi4 % 2][:, 0:1], x23[:, k, gs],
                                     ones_c[:, :], start=(k == 0),
                                     stop=(k == KC - 1))
                        for k in range(KC):
                            inst = t.matmul(psSum[gi4 % 2][:, 1:2],
                                            sq3[:, k, gs], ones_c[:, :],
                                            start=(k == 0), stop=(k == KC - 1))
                        inst.then_inc(smm.h, smm.inc())
                t.wait_ge(fin.h, 3)
                t.transpose(psF[0][0:32, 0:128], m1v(), ids[:, :]).then_inc(
                    fin.h, 1)
                t.wait_ge(fin.h, 5)
                t.transpose(psF[0][0:32, 0:128], rstdv(), ids[:, :]).then_inc(
                    fin.h, 1)

            @block.scalar
            def _(s):
                for ti in range(NTT):
                    for n in range(32):
                        gi = 32 * ti + n
                        s.wait_ge(fmm.h, gi + 1)
                        kb = n // 8
                        gl3 = glb[(4 * ti + kb) % 2][:, :].rearrange(
                            "p (c t) -> p c t", c=8)
                        if n % 8 == 0 and (4 * ti + kb) >= 2:
                            s.wait_ge(cmm.h, 8 * (4 * ti + kb - 1))
                        s.activation(gl3[:, n % 8, :], psF[gi % 2][:, :],
                                     AF.Gelu_apprx_tanh,
                                     bias=bff1c[:, n:n + 1], scale=1.0
                                     ).then_inc(gld.h, gld.inc())
                    s.wait_ge(dvx.h, 8 * (ti + 1))
                    if ti >= 1:
                        s.wait_ge(smm.h, 4 * ti)
                    x23s = x2t[:, :].rearrange("p (c t) -> p c t", c=KC)
                    sq3s = sqt[:, :].rearrange("p (c t) -> p c t", c=KC)
                    for k in range(KC):
                        inst = s.activation(sq3s[:, k, :], x23s[:, k, :],
                                            AF.Square)
                    inst.then_inc(sqd.h, sqd.inc())
                s.wait_ge(fin.h, 1)
                s.activation(rstdv(), varv(), AF.Ln, bias=EPS, scale=1.0)
                s.activation(rstdv(), rstdv(), AF.Exp, bias=0.0, scale=-0.5
                             ).then_inc(fin.h, 1)

            @block.vector
            def _(v):
                v.memset(jk[:, :], 0.0)
                for ti in range(NTT):
                    j = ti % 2
                    v.wait_ge(ldy[j].h, 16 * (ti // 2 + 1))
                    v.wait_ge(bcd.h, ti + 1)
                    if ti >= 2:
                        v.wait_ge(fmm.h, 32 * (ti - 1))
                    y23 = y2t[j][:, :].rearrange("p (c t) -> p c t", c=KC)
                    xn3 = xnt[ti % 2][:, :].rearrange("p (c t) -> p c t", c=KC)
                    for k in range(KC):
                        fa3 = ffa[:, :].rearrange("p (c t) -> p c t", c=KC)
                        v.tensor_sub(fa3[:, k, :], y23[:, k, :], psB1[:, :])
                        v.tensor_mul(fa3[:, k, :], fa3[:, k, :], psB2[:, :])
                        inst = v.tensor_scalar(
                            xn3[:, k, :], fa3[:, k, :],
                            gv[:, 2 * KC + k:2 * KC + k + 1],
                            bv[:, 2 * KC + k:2 * KC + k + 1],
                            ALU.mult, ALU.add)
                    inst.then_inc(dvn.h, dvn.inc())
                    fa3 = ffa[:, :].rearrange("p (c t) -> p c t", c=KC)
                    for kb in range(4):
                        for mo in range(KC):
                            ci = 32 * ti + 8 * kb + mo
                            v.wait_ge(cmm.h, ci + 1)
                            if kb == 0:
                                v.tensor_copy(fa3[:, mo, :], psC[ci % 2][:, :]
                                              ).then_inc(cev.h, cev.inc())
                            else:
                                v.tensor_add(fa3[:, mo, :], fa3[:, mo, :],
                                             psC[ci % 2][:, :]).then_inc(
                                    cev.h, cev.inc())
                    v.wait_ge(ldx1.h, 16 * (ti + 1))
                    if ti >= 1:
                        v.wait_ge(wb1.h, 16 * ti)
                        v.wait_ge(smm.h, 4 * ti)
                        v.wait_ge(sqd.h, ti)
                    x13 = x1t[:, :].rearrange("p (c t) -> p c t", c=KC)
                    x23 = x2t[:, :].rearrange("p (c t) -> p c t", c=KC)
                    for k in range(KC):
                        v.scalar_tensor_tensor(
                            x23[:, k, :], fa3[:, k, :], bff2c[:, k:k + 1],
                            y23[:, k, :], ALU.add, ALU.add)
                        v.tensor_add(x23[:, k, :], x23[:, k, :], x13[:, k, :]
                                     ).then_inc(dvx.h, dvx.inc())
                    for g in range(4):
                        gi4 = 4 * ti + g
                        v.wait_ge(smm.h, gi4 + 1)
                        v.tensor_copy(s1tok[:, gi4:gi4 + 1],
                                      psSum[gi4 % 2][:, 0:1])
                        v.tensor_copy(s2tok[:, gi4:gi4 + 1],
                                      psSum[gi4 % 2][:, 1:2]).then_inc(
                            sev.h, sev.inc())
                v.tensor_scalar_mul(m1v(), s1tok[:, :], 1.0 / D)
                v.tensor_scalar_mul(tmpv(), s2tok[:, :], 1.0 / D)
                v.tensor_mul(varv(), m1v(), m1v())
                v.tensor_scalar_mul(jk[:, 0:1], jk[:, 1:2], 1.0)
                v.tensor_sub(varv(), tmpv(), varv()).then_inc(fin.h, 1)
                v.wait_ge(fin.h, 2)
                v.nop().then_inc(fin.h, 1)
                v.wait_ge(fin.h, 4)
                v.tensor_copy(tpo[:, :], psF[0][0:32, 0:128]).then_inc(
                    fin.h, 1)
                v.wait_ge(fin.h, 6)
                v.wait_ge(wbv.h, 16)
                v.tensor_copy(tpo[:, :], psF[0][0:32, 0:128]).then_inc(
                    fin.h, 1)


def emit_h3_gate(b, io, sc, per):
    nc = b.nc
    ones_r, gateT, ids = per["ones_r"], per["gateT"], per["ids"]
    x2T3, hT3 = fm(sc["x2T"]), fm(sc["hT"])
    with ExitStack() as ph:
        P = ph.enter_context
        x2t = [P(nc.sbuf_tensor(f"g_x2{j}", [128, KC * TT], F32))
               for j in range(2)]
        h3f = [P(nc.sbuf_tensor(f"g_h3{j}", [128, KC * TT], F32))
               for j in range(2)]
        h3b = P(nc.sbuf_tensor("g_hb", [128, KC * TT], BF16))
        rowm = P(nc.sbuf_tensor("g_rm", [1, T], F32))
        rowr = P(nc.sbuf_tensor("g_rr", [1, T], F32))
        gsb = P(nc.sbuf_tensor("g_gsb", [8, TT], F32))
        sctile = P(nc.sbuf_tensor("g_sc", [128, 10], F32))
        msc = P(nc.sbuf_tensor("g_msc", [128, 8], F32))
        jk = P(nc.sbuf_tensor("g_jk", [128, 2], F32))
        mst = [P(nc.sbuf_tensor(f"g_ms{j}", [8, 128], F32)) for j in range(2)]
        psB1 = P(nc.psum_tensor("g_pB1", [128, 512], F32))
        psB2 = P(nc.psum_tensor("g_pB2", [128, 512], F32))
        psG = P(nc.psum_tensor("g_pG", [128, 512], F32))
        psS = P(nc.psum_tensor("g_pS", [128, 512], F32))
        psM = P(nc.psum_tensor("g_pM", [128, 512], F32))

        ldx = [b.sem() for _ in range(2)]
        ldr = b.sem()
        bcd = b.sem()
        dvh = b.sem()
        gmm = b.sem()
        gev = b.sem()
        smm = b.sem()
        top = b.sem()
        mmk = b.sem()
        sev = b.sem()
        acr = b.sem()
        wbh = b.sem()
        wbm = b.sem()

        with nc.Block("p9") as block:
            @block.sync
            def _(s):
                s.dma_start(rowm[:, :], sc["vecs"][V_M4:V_M4 + 1, :]
                            ).then_inc(ldr.h, ldr.inc(16))
                s.dma_start(rowr[:, :], sc["vecs"][V_RSTD4:V_RSTD4 + 1, :]
                            ).then_inc(ldr.h, ldr.inc(16))
                for ti in range(NTT):
                    j = ti % 2
                    if ti >= 2:
                        s.wait_ge(dvh.h, ti - 1)
                    s.dma_start(
                        x2t[j][:, :].rearrange("p (c t) -> p c t", c=KC),
                        x2T3[:, :, ti * TT:(ti + 1) * TT]
                    ).then_inc(ldx[j].h, ldx[j].inc(16))

            @block.tensor
            def _(t):
                t.wait_ge(ldr.h, 32)
                for ti in range(NTT):
                    ts_ = slice(ti * TT, (ti + 1) * TT)
                    if ti >= 1:
                        t.wait_ge(dvh.h, ti)
                    t.matmul(psB1[:, :], ones_r[:, :], rowm[0:1, ts_],
                             start=True, stop=True)
                    t.matmul(psB2[:, :], ones_r[:, :], rowr[0:1, ts_],
                             start=True, stop=True).then_inc(bcd.h, bcd.inc())
                    t.wait_ge(dvh.h, ti + 1)
                    if ti >= 1:
                        t.wait_ge(gev.h, ti)
                    h33 = h3f[ti % 2][:, :].rearrange("p (c t) -> p c t", c=KC)
                    for k in range(KC):
                        inst = t.matmul(psG[0:8, :], gateT[:, k * 8:(k + 1) * 8],
                                        h33[:, k, :], start=(k == 0),
                                        stop=(k == KC - 1))
                    inst.then_inc(gmm.h, gmm.inc())
                    for g in range(4):
                        gi = 4 * ti + g
                        t.wait_ge(gev.h, ti + 1)
                        if gi >= 1:
                            t.wait_ge(top.h, 2 * gi - 1)
                        t.transpose(psS[0:128, 0:8], gsb[:, g * 128:(g + 1) * 128],
                                    ids[0:8, 0:8]).then_inc(smm.h, smm.inc())
                        t.wait_ge(top.h, 2 * gi + 2)
                        if gi >= 2:
                            t.wait_ge(sev.h, gi - 1)
                        t.transpose(psM[0:8, 0:128], msc[:, :], ids[:, :]
                                    ).then_inc(mmk.h, mmk.inc())

            @block.vector
            def _(v):
                v.memset(jk[:, :], 0.0)
                for ti in range(NTT):
                    j = ti % 2
                    v.wait_ge(ldx[j].h, 16 * (ti // 2 + 1))
                    v.wait_ge(bcd.h, ti + 1)
                    if ti >= 2:
                        v.wait_ge(gmm.h, ti - 1)
                    if ti >= 1:
                        v.wait_ge(wbh.h, 16 * ti)
                    x23 = x2t[j][:, :].rearrange("p (c t) -> p c t", c=KC)
                    h33 = h3f[ti % 2][:, :].rearrange("p (c t) -> p c t", c=KC)
                    hb3 = h3b[:, :].rearrange("p (c t) -> p c t", c=KC)
                    for k in range(KC):
                        v.tensor_sub(h33[:, k, :], x23[:, k, :], psB1[:, :])
                        v.tensor_mul(h33[:, k, :], h33[:, k, :], psB2[:, :])
                        inst = v.tensor_copy(hb3[:, k, :], h33[:, k, :])
                    inst.then_inc(dvh.h, dvh.inc())
                    v.wait_ge(gmm.h, ti + 1)
                    v.tensor_copy(gsb[:, :], psG[0:8, 0:TT]).then_inc(
                        gev.h, gev.inc())
                    sp = mk_sp(v, jk)
                    for g in range(4):
                        gi = 4 * ti + g
                        v.wait_ge(smm.h, gi + 1)
                        s8 = sctile[:, 0:8]
                        v.tensor_copy(s8, psS[0:128, 0:8])
                        sp()
                        v.reduce_max(sctile[:, 8:9], s8, AX.X)
                        sp()
                        v.tensor_scalar_mul(sctile[:, 9:10], sctile[:, 8:9],
                                            -1.0).then_inc(top.h, top.inc())
                        v.wait_ge(acr.h, gi + 1)
                        v.reduce_sum(sctile[:, 8:9], s8, AX.X)
                        sp()
                        v.reciprocal(sctile[:, 9:10], sctile[:, 8:9])
                        sp()
                        v.tensor_scalar_mul(s8, s8, sctile[:, 9:10])
                        sp()
                        v.reduce_max(sctile[:, 8:9], s8, AX.X)
                        sp()
                        v.tensor_single_scalar(msc[:, :], s8, sctile[:, 8:9],
                                               ALU.is_ge)
                        sp()
                        v.tensor_scalar_mul(msc[:, :], msc[:, :], 2.0)
                        sp()
                        v.tensor_sub(msc[:, :], s8, msc[:, :])
                        sp()
                        v.reduce_max(sctile[:, 8:9], msc[:, :], AX.X)
                        sp()
                        v.tensor_single_scalar(msc[:, :], s8, sctile[:, 8:9],
                                               ALU.is_ge)
                        sp()
                        v.tensor_mul(msc[:, :], s8, msc[:, :]).then_inc(
                            top.h, top.inc())
                        v.wait_ge(mmk.h, gi + 1)
                        if gi >= 2:
                            v.wait_ge(wbm.h, 16 * (gi - 1))
                        v.tensor_copy(mst[gi % 2][:, :], psM[0:8, 0:128]
                                      ).then_inc(sev.h, sev.inc())

            @block.scalar
            def _(s):
                for ti in range(NTT):
                    for g in range(4):
                        gi = 4 * ti + g
                        s.wait_ge(top.h, 2 * gi + 1)
                        s.activation(sctile[:, 0:8], sctile[:, 0:8], AF.Exp,
                                     bias=sctile[:, 9:10], scale=1.0
                                     ).then_inc(acr.h, acr.inc())

            @block.gpsimd
            def _(g):
                for ti in range(NTT):
                    g.wait_ge(dvh.h, ti + 1)
                    if ti >= 1:
                        g.wait_ge(wbh.h, 16 * ti)
                    g.dma_start(hT3[:, :, ti * TT:(ti + 1) * TT],
                                h3b[:, :].rearrange("p (c t) -> p c t", c=KC)
                                ).then_inc(wbh.h, wbh.inc(16))
                    for gg in range(4):
                        gi = 4 * ti + gg
                        g.wait_ge(sev.h, gi + 1)
                        if gi >= 1:
                            g.wait_ge(wbm.h, 16 * gi)
                        g.dma_start(
                            sc["masksd"][:, gi * 128:(gi + 1) * 128],
                            mst[gi % 2][:, :]).then_inc(wbm.h, wbm.inc(16))
                g.wait_ge(wbh.h, wbh.n)
                g.wait_ge(wbm.h, wbm.n)


def emit_final(b, io, sc, per):
    nc = b.nc
    ids = per["ids"]
    x2T3, facc3 = fm(sc["x2T"]), fm(sc["facc"])
    with ExitStack() as ph:
        P = ph.enter_context
        x2t = [P(nc.sbuf_tensor(f"z_x2{j}", [128, KC * TT], F32))
               for j in range(2)]
        fat = [P(nc.sbuf_tensor(f"z_fa{j}", [128, KC * TT], F32))
               for j in range(2)]
        ot = P(nc.sbuf_tensor("z_ot", [128, KC * TT], F32))
        ost = [P(nc.sbuf_tensor(f"z_os{j}", [128, D], F32)) for j in range(2)]
        psT = [P(nc.psum_tensor(f"z_pT{j}", [128, 512], F32)) for j in range(2)]

        ldx = [b.sem() for _ in range(2)]
        ldf = [b.sem() for _ in range(2)]
        dvo = b.sem()
        tpx = b.sem()
        evx = b.sem()
        wbo = b.sem()

        with nc.Block("p11") as block:
            @block.sync
            def _(s):
                for ti in range(NTT):
                    j = ti % 2
                    if ti >= 2:
                        s.wait_ge(dvo.h, ti - 1)
                    s.dma_start(
                        x2t[j][:, :].rearrange("p (c t) -> p c t", c=KC),
                        x2T3[:, :, ti * TT:(ti + 1) * TT]
                    ).then_inc(ldx[j].h, ldx[j].inc(16))
                    s.dma_start(
                        fat[j][:, :].rearrange("p (c t) -> p c t", c=KC),
                        facc3[:, :, ti * TT:(ti + 1) * TT]
                    ).then_inc(ldf[j].h, ldf[j].inc(16))

            @block.vector
            def _(v):
                for ti in range(NTT):
                    j = ti % 2
                    v.wait_ge(ldx[j].h, 16 * (ti // 2 + 1))
                    v.wait_ge(ldf[j].h, 16 * (ti // 2 + 1))
                    if ti >= 1:
                        v.wait_ge(tpx.h, 32 * ti)
                    x23 = x2t[j][:, :].rearrange("p (c t) -> p c t", c=KC)
                    fa3 = fat[j][:, :].rearrange("p (c t) -> p c t", c=KC)
                    ot3 = ot[:, :].rearrange("p (c t) -> p c t", c=KC)
                    for k in range(KC):
                        inst = v.scalar_tensor_tensor(
                            ot3[:, k, :], fa3[:, k, :], 0.5, x23[:, k, :],
                            ALU.mult, ALU.add)
                    inst.then_inc(dvo.h, dvo.inc())

            @block.tensor
            def _(t):
                for ti in range(NTT):
                    t.wait_ge(dvo.h, ti + 1)
                    ot3 = ot[:, :].rearrange("p (c t) -> p c t", c=KC)
                    for g in range(4):
                        for k in range(KC):
                            gidx = 32 * ti + 8 * g + k
                            if gidx >= 2:
                                t.wait_ge(evx.h, gidx - 1)
                            t.transpose(psT[gidx % 2][0:128, 0:128],
                                        ot3[:, k, g * 128:(g + 1) * 128],
                                        ids[:, :]).then_inc(tpx.h, tpx.inc())

            @block.scalar
            def _(s):
                for ti in range(NTT):
                    for g in range(4):
                        gi = 4 * ti + g
                        for k in range(KC):
                            gidx = 32 * ti + 8 * g + k
                            if k == 0 and gi >= 2:
                                s.wait_ge(wbo.h, 16 * (gi - 1))
                            s.wait_ge(tpx.h, gidx + 1)
                            s.copy(ost[gi % 2][:, k * 128:(k + 1) * 128],
                                   psT[gidx % 2][0:128, 0:128]).then_inc(
                                evx.h, evx.inc())

            @block.gpsimd
            def _(g):
                for ti in range(NTT):
                    for gg in range(4):
                        gi = 4 * ti + gg
                        g.wait_ge(evx.h, 8 * (gi + 1))
                        if gi >= 1:
                            g.wait_ge(wbo.h, 16 * gi)
                        g.dma_start(io["out"][gi * 128:(gi + 1) * 128, :],
                                    ost[gi % 2][:, :]).then_inc(
                            wbo.h, wbo.inc(16))
                g.wait_ge(wbo.h, wbo.n)


def _host_constants():
    cid = np.eye(128, dtype=np.float32)
    cidb = np.eye(128, dtype=ml_dtypes.bfloat16)
    cidb2 = np.vstack([np.eye(64), np.eye(64)]).astype(ml_dtypes.bfloat16)
    tril = np.triu(np.ones((128, 128), dtype=np.float32)) * ETA
    csign = np.tile(np.array([1.0, -1.0], dtype=np.float32), 256)[None, :]
    return dict(cid=cid, cidb=cidb, cidb2=cidb2, ctrileta=tril,
                csign=csign.astype(np.float32))


_NC_CACHE = {}


def kernel(**inputs):
    if "nc" not in _NC_CACHE:
        _NC_CACHE["nc"] = build_program()
    nc = _NC_CACHE["nc"]
    x = np.ascontiguousarray(np.asarray(inputs["L0_attn_x"], dtype=np.float32))
    consts = _host_constants()
    common = dict(
        ln1_g=inputs["L0_attn_ln1_g"], ln1_b=inputs["L0_attn_ln1_b"],
        ln2_g=inputs["L0_attn_ln2_g"], ln2_b=inputs["L0_attn_ln2_b"],
        ln3_g=inputs["L0_attn_ln3_g"], ln3_b=inputs["L0_attn_ln3_b"],
        w_pot=inputs["L0_attn_w_pot"], b_pot=inputs["L0_attn_b_pot"],
        w_bk=inputs["L0_attn_w_bk"], b_bk=inputs["L0_attn_b_bk"],
        w_qkv=inputs["L0_attn_w_qkv"], b_qkv=inputs["L0_attn_b_qkv"],
        w_out=inputs["L0_attn_w_out"], b_out=inputs["L0_attn_b_out"],
        w_ff1=inputs["L0_attn_w_ff1"], b_ff1=inputs["L0_attn_b_ff1"],
        w_ff2=inputs["L0_attn_w_ff2"], b_ff2=inputs["L0_attn_b_ff2"],
        gate_w=inputs["L0_ffn_gate_w"], w1=inputs["L0_ffn_w1"],
        w2=inputs["L0_ffn_w2"], w3=inputs["L0_ffn_w3"], **consts)
    feed = {}
    for k, v in common.items():
        a = np.asarray(v)
        if k in ("cidb", "cidb2"):
            feed[k] = np.ascontiguousarray(a)
        else:
            feed[k] = np.ascontiguousarray(a.astype(np.float32))
    common = feed
    in_maps = [dict(common, xin=np.ascontiguousarray(x[c]))
               for c in range(8)]
    res = run_bass_kernel_spmd(nc, in_maps, core_ids=list(range(8)))
    out = np.stack([res.results[c]["out"] for c in range(8)], axis=0)
    return out.astype(np.float32)


# version 08 (final)
# speedup vs baseline: 1.0682x; 1.0163x over previous
import numpy as np
import ml_dtypes
from contextlib import ExitStack

import concourse.bass as bass
import concourse.mybir as mybir
from concourse.bass_utils import run_bass_kernel_spmd

F32 = mybir.dt.float32
BF16 = mybir.dt.bfloat16
AF = mybir.ActivationFunctionType
ALU = mybir.AluOpType
AX = mybir.AxisListType

T = 4096
D = 1024
KC = D // 128
NT128 = T // 128
FF = 4096
NFC = FF // 128
H, HD = 8, 64
E = 8
HID = 2816
NHC = HID // 128
TT = 512
NTT = T // TT
ETA, DTC, GFLOOR, EPS = 0.1, 0.01, 0.1, 1e-5
CB = 32
HCH = 128
NHCH = T // HCH

V_M1, V_RSTD1, V_RSTD12, V_MR12 = 0, 1, 2, 3
V_POTV, V_POTG = 5, 6
V_GSC = 7
V_CUM = 8
V_BKRE, V_BKIM = 9, 10
V_MA, V_RSTDA, V_MRA = 11, 12, 13
V_M3, V_RSTD3, V_MR3 = 14, 15, 16
V_M4, V_RSTD4, V_MR4 = 17, 18, 19
V_SCEXP = 20


class Ctr:
    __slots__ = ("h", "n")

    def __init__(self, h):
        self.h = h
        self.n = 0

    def inc(self, k=1):
        self.n += k
        return k


class B:

    def __init__(self):
        self.stack = ExitStack()
        self.nc = bass.Bass("TRN2", target_bir_lowering=False,
                            detect_race_conditions=False)
        self.S = self.stack.enter_context
        self.pool = []
        self.keep = 8
        self._idx = 0

    def init_pool(self, n=48):
        self.pool = [self.S(self.nc.semaphore(f"sp{i}")) for i in range(n)]
        self._idx = 0

    def sem(self, name=None):
        assert self._idx < len(self.pool), "sem pool exhausted"
        h = self.pool[self._idx]
        self._idx += 1
        return Ctr(h)

    def new_phase(self):
        nc = self.nc
        self.keep += 1
        ids = sorted(h.num for h in self.pool)
        assert ids == list(range(ids[0], ids[-1] + 1)), "pool not contiguous"
        rng = range(ids[0], ids[-1] + 1)
        with nc.Block(f"semclr{self.keep}") as block:
            @block.gpsimd
            def _(g):
                g.dma_reset(rng)
                g.sem_clear(rng)
        self._idx = 0


def fm(dram_ap, c=KC):
    return dram_ap.rearrange("(c p) t -> p c t", p=128)


def mk_sp(v, jk):
    return lambda: v.tensor_scalar_mul(jk[:, 0:1], jk[:, 1:2], 1.0)


def build_program(nphases=None):
    b = B()
    nc = b.nc
    S = b.S
    DT = nc.dram_tensor

    b.init_pool(48)
    io = {}
    for name, shape in [
        ("xin", [T, D]), ("ln1_g", [D]), ("ln1_b", [D]), ("ln2_g", [D]),
        ("ln2_b", [D]), ("ln3_g", [D]), ("ln3_b", [D]), ("w_pot", [D, 2]),
        ("b_pot", [2]), ("w_bk", [2, D]), ("b_bk", [D]),
        ("w_qkv", [D, 1536]), ("b_qkv", [1536]), ("w_out", [512, D]),
        ("b_out", [D]), ("w_ff1", [D, FF]), ("b_ff1", [FF]),
        ("w_ff2", [FF, D]), ("b_ff2", [D]), ("gate_w", [E, D]),
        ("w1", [E, D, HID]), ("w2", [E, HID, D]), ("w3", [E, D, HID]),
        ("cid", [128, 128]), ("ctrileta", [128, 128]),
        ("csign", [1, 512]),
    ]:
        io[name] = DT(name, shape, F32, kind="ExternalInput").ap()
    io["cidb"] = DT("cidb", [128, 128], BF16, kind="ExternalInput").ap()
    io["cidb2"] = DT("cidb2", [128, 64], BF16, kind="ExternalInput").ap()
    io["out"] = DT("out", [T, D], F32, kind="ExternalOutput").ap()

    sc = {}
    for name, shape, dt in [
        ("x0T", [D, T], F32), ("x1T", [D, T], F32), ("y1T", [D, T], F32),
        ("y2T", [D, T], F32), ("x2T", [D, T], F32), ("hT", [D, T], BF16),
        ("facc", [D, T], F32), ("masksd", [E, T], F32),
        ("vecs", [32, T], F32),
    ]:
        sc[name] = DT(name, shape, dt, kind="Internal").ap()

    ids = S(nc.sbuf_tensor("ids", [128, 128], F32))
    idsb = S(nc.sbuf_tensor("idsb", [128, 128], BF16))
    idsb2 = S(nc.sbuf_tensor("idsb2", [128, 64], BF16))
    trileta = S(nc.sbuf_tensor("trileta", [128, 128], F32))
    ones_r = S(nc.sbuf_tensor("ones_r", [1, 128], F32))
    ones_c = S(nc.sbuf_tensor("ones_c", [128, 1], F32))
    gv = S(nc.sbuf_tensor("gv", [128, 3 * KC], F32))
    bv = S(nc.sbuf_tensor("bv", [128, 3 * KC], F32))
    bqkvc = S(nc.sbuf_tensor("bqkvc", [128, 12], F32))
    boutc = S(nc.sbuf_tensor("boutc", [128, KC], F32))
    bbkc = S(nc.sbuf_tensor("bbkc", [128, KC], F32))
    bff1c = S(nc.sbuf_tensor("bff1c", [128, NFC], F32))
    bff2c = S(nc.sbuf_tensor("bff2c", [128, KC], F32))
    gateT = S(nc.sbuf_tensor("gateT", [128, KC * E], F32))
    cbank = S(nc.sbuf_tensor("cbank", [128, 8], F32))
    _cvals = [0.0, 1.0, -0.5, -1.0, EPS, 0.5, 2.0, -2.0]
    for _i, _vv in enumerate(_cvals):
        nc.const_aps.register(F32, _vv, cbank[:, _i:_i + 1])
    per = dict(ids=ids, idsb=idsb, idsb2=idsb2, trileta=trileta,
               ones_r=ones_r, ones_c=ones_c, gv=gv, bv=bv, bqkvc=bqkvc,
               boutc=boutc, bbkc=bbkc, bff1c=bff1c, bff2c=bff2c, gateT=gateT)

    ld0 = b.sem("ld0")
    with nc.Block("consts") as block:
        @block.sync
        def _(s):
            cp = lambda dst, src: s.dma_start(
                dst, src, allow_slow_non_contiguous=True
            ).then_inc(ld0.h, ld0.inc(16))
            cp(ids[:, :], io["cid"])
            cp(idsb[:, :], io["cidb"])
            cp(idsb2[:, :], io["cidb2"])
            cp(trileta[:, :], io["ctrileta"])
            r = lambda a: a.rearrange("(c p) -> p c", p=128)
            cp(gv[:, 0:KC], r(io["ln1_g"]))
            cp(gv[:, KC:2 * KC], r(io["ln2_g"]))
            cp(gv[:, 2 * KC:3 * KC], r(io["ln3_g"]))
            cp(bv[:, 0:KC], r(io["ln1_b"]))
            cp(bv[:, KC:2 * KC], r(io["ln2_b"]))
            cp(bv[:, 2 * KC:3 * KC], r(io["ln3_b"]))
            cp(bqkvc[:, :], r(io["b_qkv"]))
            cp(boutc[:, :], r(io["b_out"]))
            cp(bbkc[:, :], r(io["b_bk"]))
            cp(bff1c[:, :], r(io["b_ff1"]))
            cp(bff2c[:, :], r(io["b_ff2"]))
            s.wait_ge(ld0.h, ld0.n)

        @block.vector
        def _(v):
            v.memset(ones_r[:, :], 1.0)
            v.memset(ones_c[:, :], 1.0)
            for _i, _vv in enumerate(_cvals):
                v.memset(cbank[:, _i:_i + 1], _vv)

    with ExitStack() as ph:
        gsb = ph.enter_context(nc.sbuf_tensor("gp_gsb", [8, D], F32))
        psg = [ph.enter_context(nc.psum_tensor(f"gp_ps{j}", [128, 8], F32))
               for j in range(2)]
        gl = b.sem()
        tp = b.sem()
        ev = b.sem()
        with nc.Block("gateprep") as block:
            @block.sync
            def _(s):
                s.dma_start(gsb[:, :], io["gate_w"]).then_inc(gl.h, gl.inc(16))

            @block.tensor
            def _(t):
                t.wait_ge(gl.h, 16)
                t.wait_ge(ld0.h, ld0.n)
                for k in range(KC):
                    if k >= 2:
                        t.wait_ge(ev.h, k - 1)
                    t.transpose(psg[k % 2][:, :], gsb[:, k * 128:(k + 1) * 128],
                                ids[0:8, 0:8]).then_inc(tp.h, tp.inc())

            @block.vector
            def _(v):
                for k in range(KC):
                    v.wait_ge(tp.h, k + 1)
                    v.tensor_copy(gateT[:, k * 8:(k + 1) * 8],
                                  psg[k % 2][:, :]).then_inc(ev.h, ev.inc())

    plist = [
        (emit_p0, ()),
        (emit_moe, (0,)),
        (emit_combine_pot, ()),
        (emit_bk_scan, ()),
        (emit_attn_core, ()),
        (emit_y2stats, ()),
        (emit_ffn_x2, ()),
        (emit_h3_gate, ()),
        (emit_moe, (1,)),
        (emit_final, ()),
    ]
    if nphases is not None:
        plist = plist[:nphases]
    for fn, args in plist:
        b.new_phase()
        fn(b, io, sc, per, *args)

    b.stack.close()
    return nc


def _ln_rstd(eng, rstd, var_plus, tmp):
    eng.activation(tmp, var_plus, AF.Ln, bias=EPS, scale=1.0)
    eng.activation(rstd, tmp, AF.Exp, bias=0.0, scale=-0.5)


def emit_p0(b, io, sc, per):
    nc = b.nc
    ids, gateT = per["ids"], per["gateT"]
    with ExitStack() as ph:
        P = ph.enter_context
        xts = [P(nc.sbuf_tensor(f"p0_x{j}", [128, D], F32)) for j in range(2)]
        sqt = P(nc.sbuf_tensor("p0_sq", [128, D], F32))
        stat = P(nc.sbuf_tensor("p0_st", [128, 8], F32))
        h1t = P(nc.sbuf_tensor("p0_h1", [128, D], F32))
        xst = P(nc.sbuf_tensor("p0_xs", [128, D], F32))
        hst = P(nc.sbuf_tensor("p0_hs", [128, D], F32))
        sctile = P(nc.sbuf_tensor("p0_sc", [128, 10], F32))
        msc = P(nc.sbuf_tensor("p0_msc", [128, 8], F32))
        jk = P(nc.sbuf_tensor("p0_jk", [128, 2], F32))
        gevr = P(nc.sbuf_tensor("p0_gev", [8, 128], F32))
        mst = [P(nc.sbuf_tensor(f"p0_ms{j}", [8, 128], F32)) for j in range(2)]
        psT = [P(nc.psum_tensor(f"p0_psT{j}", [128, 128], F32)) for j in range(2)]
        psG = [P(nc.psum_tensor(f"p0_psG{j}", [8, 128], F32)) for j in range(2)]
        psS = P(nc.psum_tensor("p0_psS", [128, 8], F32))

        ldx = [b.sem() for _ in range(2)]
        dst = b.sem()
        acr = b.sem()
        dap = b.sem()
        tpx = b.sem()
        evx = b.sem()
        gmm = b.sem()
        gev = b.sem()
        top = b.sem()
        smm = b.sem()
        sev = b.sem()
        wbx = b.sem()
        wbh = b.sem()
        wbm = b.sem()

        x0T3, hT3 = fm(sc["x0T"]), fm(sc["hT"])

        with nc.Block("p0") as block:
            @block.sync
            def _(s):
                for i in range(NT128):
                    j = i % 2
                    if i >= 2:
                        s.wait_ge(dap.h, i - 1)
                        s.wait_ge(tpx.h, 16 * (i - 2) + 8)
                    s.dma_start(xts[j][:, :],
                                io["xin"][i * 128:(i + 1) * 128, :]
                                ).then_inc(ldx[j].h, ldx[j].inc(16))

            @block.vector
            def _(v):
                sp = mk_sp(v, jk)
                v.memset(jk[:, :], 0.0)
                for i in range(NT128):
                    j = i % 2
                    v.wait_ge(ldx[j].h, 16 * (i // 2 + 1))
                    xt = xts[j]
                    v.reduce_sum(stat[:, 0:1], xt[:, :], AX.X)
                    v.tensor_mul(sqt[:, :], xt[:, :], xt[:, :])
                    v.reduce_sum(stat[:, 1:2], sqt[:, :], AX.X)
                    v.tensor_scalar_mul(stat[:, 2:3], stat[:, 0:1], 1.0 / D)
                    v.tensor_scalar_mul(stat[:, 3:4], stat[:, 1:2], 1.0 / D)
                    v.tensor_mul(stat[:, 4:5], stat[:, 2:3], stat[:, 2:3])
                    sp()
                    v.tensor_sub(stat[:, 5:6], stat[:, 3:4], stat[:, 4:5]
                                 ).then_inc(dst.h, dst.inc())
                    v.wait_ge(acr.h, 2 * i + 1)
                    if i >= 1:
                        v.wait_ge(tpx.h, 16 * i)
                    v.tensor_scalar(h1t[:, :], xt[:, :], stat[:, 2:3],
                                    stat[:, 6:7], ALU.subtract, ALU.mult
                                    ).then_inc(dap.h, dap.inc())
                    v.wait_ge(gmm.h, i + 1)
                    v.tensor_copy(gevr[:, :], psG[i % 2][:, :]).then_inc(
                        gev.h, gev.inc())
                    v.wait_ge(smm.h, 2 * i + 1)
                    s8 = sctile[:, 0:8]
                    v.tensor_copy(s8, psS[:, :])
                    sp()
                    v.reduce_max(sctile[:, 8:9], s8, AX.X)
                    sp()
                    v.tensor_scalar_mul(sctile[:, 9:10], sctile[:, 8:9], -1.0
                                        ).then_inc(top.h, top.inc())
                    v.wait_ge(acr.h, 2 * i + 2)
                    v.reduce_sum(sctile[:, 8:9], s8, AX.X)
                    sp()
                    v.reciprocal(sctile[:, 9:10], sctile[:, 8:9])
                    sp()
                    v.tensor_scalar_mul(s8, s8, sctile[:, 9:10])
                    sp()
                    v.reduce_max(sctile[:, 8:9], s8, AX.X)
                    sp()
                    v.tensor_single_scalar(msc[:, :], s8, sctile[:, 8:9],
                                           ALU.is_ge)
                    sp()
                    v.tensor_scalar_mul(msc[:, :], msc[:, :], 2.0)
                    sp()
                    v.tensor_sub(msc[:, :], s8, msc[:, :])
                    sp()
                    v.reduce_max(sctile[:, 8:9], msc[:, :], AX.X)
                    sp()
                    v.tensor_single_scalar(msc[:, :], s8, sctile[:, 8:9],
                                           ALU.is_ge)
                    sp()
                    v.tensor_mul(msc[:, :], s8, msc[:, :]).then_inc(
                        top.h, top.inc())
                    v.wait_ge(smm.h, 2 * i + 2)
                    if i >= 2:
                        v.wait_ge(wbm.h, 16 * (i - 1))
                    v.tensor_copy(mst[i % 2][:, :], psG[i % 2][:, :]).then_inc(
                        sev.h, sev.inc())

            @block.scalar
            def _(s):
                for i in range(NT128):
                    s.wait_ge(dst.h, i + 1)
                    s.activation(stat[:, 7:8], stat[:, 5:6], AF.Ln,
                                 bias=EPS, scale=1.0)
                    s.activation(stat[:, 6:7], stat[:, 7:8], AF.Exp,
                                 bias=0.0, scale=-0.5).then_inc(acr.h, acr.inc())
                    for k in range(16):
                        gidx = 16 * i + k
                        if k == 0 and i >= 1:
                            s.wait_ge(wbx.h, 16 * i)
                        if k == 8:
                            if i >= 1:
                                s.wait_ge(wbh.h, 16 * i)
                            s.wait_ge(gmm.h, i)
                        s.wait_ge(tpx.h, gidx + 1)
                        dstt = xst if k < 8 else hst
                        kk = k % 8
                        s.copy(dstt[:, kk * 128:(kk + 1) * 128],
                               psT[gidx % 2][:, :]).then_inc(evx.h, evx.inc())
                    s.wait_ge(top.h, 2 * i + 1)
                    s.activation(sctile[:, 0:8], sctile[:, 0:8], AF.Exp,
                                 bias=sctile[:, 9:10], scale=1.0).then_inc(
                        acr.h, acr.inc())

            @block.tensor
            def _(t):
                for i in range(NT128):
                    j = i % 2
                    t.wait_ge(ldx[j].h, 16 * (i // 2 + 1))
                    for k in range(8):
                        gidx = 16 * i + k
                        if gidx >= 2:
                            t.wait_ge(evx.h, gidx - 1)
                        t.transpose(psT[gidx % 2][:, :],
                                    xts[j][:, k * 128:(k + 1) * 128],
                                    ids[:, :]).then_inc(tpx.h, tpx.inc())
                    t.wait_ge(dap.h, i + 1)
                    for k in range(8):
                        gidx = 16 * i + 8 + k
                        t.wait_ge(evx.h, gidx - 1)
                        t.transpose(psT[gidx % 2][:, :],
                                    h1t[:, k * 128:(k + 1) * 128],
                                    ids[:, :]).then_inc(tpx.h, tpx.inc())
                    if i >= 2:
                        t.wait_ge(sev.h, i - 1)
                    for k in range(KC):
                        t.wait_ge(evx.h, 16 * i + 8 + k + 1)
                        inst = t.matmul(psG[i % 2][:, :],
                                        gateT[:, k * 8:(k + 1) * 8],
                                        hst[:, k * 128:(k + 1) * 128],
                                        start=(k == 0), stop=(k == KC - 1))
                    inst.then_inc(gmm.h, gmm.inc())
                    t.wait_ge(gev.h, i + 1)
                    if i >= 1:
                        t.wait_ge(top.h, 2 * (i - 1) + 1)
                    t.transpose(psS[:, :], gevr[:, :],
                                ids[0:8, 0:8]).then_inc(smm.h, smm.inc())
                    t.wait_ge(top.h, 2 * i + 2)
                    t.transpose(psG[i % 2][:, :], msc[:, :],
                                ids[:, :]).then_inc(smm.h, smm.inc())

            @block.gpsimd
            def _(g):
                for i in range(NT128):
                    g.wait_ge(evx.h, 16 * i + 8)
                    if i >= 1:
                        g.wait_ge(wbx.h, 16 * i)
                    g.dma_start(x0T3[:, :, i * 128:(i + 1) * 128],
                                xst[:, :].rearrange("p (c t) -> p c t", c=KC)
                                ).then_inc(wbx.h, wbx.inc(16))
                    g.wait_ge(evx.h, 16 * i + 16)
                    if i >= 1:
                        g.wait_ge(wbh.h, 16 * i)
                    g.dma_start(hT3[:, :, i * 128:(i + 1) * 128],
                                hst[:, :].rearrange("p (c t) -> p c t", c=KC)
                                ).then_inc(wbh.h, wbh.inc(16))
                    g.wait_ge(sev.h, i + 1)
                    if i >= 1:
                        g.wait_ge(wbm.h, 16 * i)
                    g.dma_start(sc["masksd"][:, i * 128:(i + 1) * 128],
                                mst[i % 2][:, :]).then_inc(wbm.h, wbm.inc(16))
                g.wait_ge(wbx.h, wbx.n)
                g.wait_ge(wbh.h, wbh.n)
                g.wait_ge(wbm.h, wbm.n)


def emit_moe(b, io, sc, per, inv):
    nc = b.nc
    ones_r = per["ones_r"]
    hT3 = fm(sc["hT"])
    facc3 = fm(sc["facc"])
    with ExitStack() as ph:
        P = ph.enter_context
        A = P(nc.sbuf_tensor(f"m{inv}_A", [128, KC * HID], BF16))
        Bm = P(nc.sbuf_tensor(f"m{inv}_B", [128, KC * HID], BF16))
        C = P(nc.sbuf_tensor(f"m{inv}_C", [128, NHC * D], BF16))
        ht = P(nc.sbuf_tensor(f"m{inv}_h", [128, KC * TT], BF16))
        xe = P(nc.sbuf_tensor(f"m{inv}_xe", [128, KC * TT], BF16))
        ag = P(nc.sbuf_tensor(f"m{inv}_ag", [128, NHC * TT], BF16))
        outT = P(nc.sbuf_tensor(f"m{inv}_o", [128, KC * TT], F32))
        mrow = [P(nc.sbuf_tensor(f"m{inv}_mr{j}", [1, TT], F32)) for j in range(2)]
        jk = P(nc.sbuf_tensor(f"m{inv}_jk", [128, 2], F32))
        psA = [P(nc.psum_tensor(f"m{inv}_psA{j}", [128, TT], F32)) for j in range(3)]
        psC = [P(nc.psum_tensor(f"m{inv}_psC{j}", [128, TT], F32)) for j in range(2)]
        psM = P(nc.psum_tensor(f"m{inv}_psM", [128, TT], F32))

        A3 = A[:, :].rearrange("p (c n) -> p c n", c=KC)
        B3 = Bm[:, :].rearrange("p (c n) -> p c n", c=KC)
        C3 = C[:, :].rearrange("p (c n) -> p c n", c=NHC)
        ht3 = ht[:, :].rearrange("p (c t) -> p c t", c=KC)
        xe3 = xe[:, :].rearrange("p (c t) -> p c t", c=KC)
        ag3 = ag[:, :].rearrange("p (c t) -> p c t", c=NHC)
        oT3 = outT[:, :].rearrange("p (c t) -> p c t", c=KC)

        wldA, wldB, wldC = b.sem(), b.sem(), b.sem()
        wfA, wfB, wfC = b.sem(), b.sem(), b.sem()
        hld = b.sem()
        mld = [b.sem() for _ in range(2)]
        bcm = b.sem()
        xeb = b.sem()
        mmA = b.sem()
        evA = b.sem()
        evB = b.sem()
        mmC = b.sem()
        evC = b.sem()
        ord_ = b.sem()
        oacc = b.sem()
        phw3 = b.sem()

        w1r = io["w1"].rearrange("e (c p) n -> e p c n", p=128)
        w3r = io["w3"].rearrange("e (c p) n -> e p c n", p=128)
        w2r = io["w2"].rearrange("e (c p) n -> e p c n", p=128)

        with nc.Block(f"moe{inv}") as block:
            @block.gpsimd
            def _(g):
                g.dma_start(A3, w1r[0]).then_inc(wldA.h, wldA.inc(16))
                g.dma_start(B3, w3r[0]).then_inc(wldB.h, wldB.inc(16))
                g.dma_start(C3, w2r[0]).then_inc(wldC.h, wldC.inc(16))
                for e in range(E):
                    for ti in range(NTT):
                        q = e * NTT + ti
                        if e + 1 < E and ti == NTT - 2:
                            g.wait_ge(wfA.h, e + 1)
                            g.dma_start(A3, w1r[e + 1]).then_inc(
                                wldA.h, wldA.inc(16))
                        if e + 1 < E and ti == NTT - 1:
                            g.wait_ge(wfB.h, e + 1)
                            g.dma_start(B3, w3r[e + 1]).then_inc(
                                wldB.h, wldB.inc(16))
                        g.wait_ge(ord_.h, q + 1)
                        if q >= 1:
                            g.wait_ge(oacc.h, 16 * q)
                        g.dma_start(
                            facc3[:, :, ti * TT:(ti + 1) * TT], oT3,
                            accum_op=(ALU.bypass if e == 0 else ALU.add)
                        ).then_inc(oacc.h, oacc.inc(16))
                        if e + 1 < E and ti == NTT - 1:
                            g.wait_ge(wfC.h, e + 1)
                            g.dma_start(C3, w2r[e + 1]).then_inc(
                                wldC.h, wldC.inc(16))
                g.wait_ge(oacc.h, oacc.n)

            @block.sync
            def _(s):
                for e in range(E):
                    for ti in range(NTT):
                        q = e * NTT + ti
                        if q >= 1:
                            s.wait_ge(xeb.h, q)
                        s.dma_start(ht3, hT3[:, :, ti * TT:(ti + 1) * TT]
                                    ).then_inc(hld.h, hld.inc(16))
                        j = q % 2
                        if q >= 2:
                            s.wait_ge(xeb.h, q - 1)
                        s.dma_start(mrow[j][:, :],
                                    sc["masksd"][e:e + 1, ti * TT:(ti + 1) * TT]
                                    ).then_inc(mld[j].h, mld[j].inc(16))

            @block.tensor
            def _(t):
                for e in range(E):
                    for ti in range(NTT):
                        q = e * NTT + ti
                        j = q % 2
                        t.wait_ge(mld[j].h, 16 * (q // 2 + 1))
                        if q >= 1:
                            t.wait_ge(xeb.h, q)
                        t.matmul(psM[:, :], ones_r[:, :], mrow[j][:, :],
                                 start=True, stop=True).then_inc(bcm.h, bcm.inc())
                        if ti == 0:
                            t.wait_ge(wldA.h, 16 * (e + 1))
                        t.wait_ge(xeb.h, q + 1)
                        for m in range(NHC):
                            if m >= 3:
                                t.wait_ge(evA.h, 22 * q + m - 2)
                            elif q >= 1:
                                t.wait_ge(evB.h, 22 * (q - 1) + 20 + m)
                            for k in range(KC):
                                inst = t.matmul(
                                    psA[(44 * q + m) % 3][:, :],
                                    A3[:, k, m * 128:(m + 1) * 128],
                                    xe3[:, k, :],
                                    start=(k == 0), stop=(k == KC - 1))
                            inst.then_inc(mmA.h, mmA.inc())
                        if ti == NTT - 1:
                            t.nop().then_inc(wfA.h, wfA.inc())
                        if ti == 0:
                            t.wait_ge(wldB.h, 16 * (e + 1))
                        for m in range(NHC):
                            if m >= 3:
                                t.wait_ge(evB.h, 22 * q + m - 2)
                            else:
                                t.wait_ge(evA.h, 22 * q + 20 + m)
                            for k in range(KC):
                                inst = t.matmul(
                                    psA[(44 * q + NHC + m) % 3][:, :],
                                    B3[:, k, m * 128:(m + 1) * 128],
                                    xe3[:, k, :],
                                    start=(k == 0), stop=(k == KC - 1))
                            inst.then_inc(mmA.h, mmA.inc())
                        if ti == NTT - 1:
                            t.nop().then_inc(wfB.h, wfB.inc())
                        t.nop().then_inc(phw3.h, phw3.inc())
                        if ti == 0:
                            t.wait_ge(wldC.h, 16 * (e + 1))
                        t.wait_ge(evB.h, 22 * (q + 1))
                        for mo in range(KC):
                            ci = 8 * q + mo
                            if ci >= 2:
                                t.wait_ge(evC.h, ci - 1)
                            for k2 in range(NHC):
                                inst = t.matmul(
                                    psC[ci % 2][:, :],
                                    C3[:, k2, mo * 128:(mo + 1) * 128],
                                    ag3[:, k2, :],
                                    start=(k2 == 0), stop=(k2 == NHC - 1))
                            inst.then_inc(mmC.h, mmC.inc())
                        if ti == NTT - 1:
                            t.nop().then_inc(wfC.h, wfC.inc())

            @block.vector
            def _(v):
                v.memset(jk[:, :], 0.0)
                for e in range(E):
                    for ti in range(NTT):
                        q = e * NTT + ti
                        v.wait_ge(hld.h, 16 * (q + 1))
                        v.wait_ge(bcm.h, q + 1)
                        if q >= 1:
                            v.wait_ge(phw3.h, q)
                        for k in range(KC):
                            inst = v.tensor_mul(xe3[:, k, :], ht3[:, k, :],
                                                psM[:, :])
                        inst.then_inc(xeb.h, xeb.inc())
                        for m in range(NHC):
                            gi = 44 * q + NHC + m
                            v.wait_ge(mmA.h, gi + 1)
                            v.wait_ge(evA.h, 22 * q + m + 1)
                            v.tensor_mul(ag3[:, m, :], ag3[:, m, :],
                                         psA[gi % 3][:, :]).then_inc(
                                evB.h, evB.inc())
                        for mo in range(KC):
                            ci = 8 * q + mo
                            v.wait_ge(mmC.h, ci + 1)
                            if q >= 1 and mo == 0:
                                v.wait_ge(oacc.h, 16 * q)
                            inst = v.tensor_copy(oT3[:, mo, :],
                                                 psC[ci % 2][:, :])
                            inst.then_inc(evC.h, evC.inc())
                        v.tensor_scalar_mul(jk[:, 0:1], jk[:, 1:2], 1.0
                                            ).then_inc(ord_.h, ord_.inc())

            @block.scalar
            def _(s):
                for e in range(E):
                    for ti in range(NTT):
                        q = e * NTT + ti
                        for m in range(NHC):
                            gi = 44 * q + m
                            s.wait_ge(mmA.h, gi + 1)
                            s.activation(ag3[:, m, :], psA[gi % 3][:, :],
                                         AF.Silu).then_inc(evA.h, evA.inc())


def emit_combine_pot(b, io, sc, per):
    nc = b.nc
    ones_c, ones_r, gv, bv = per["ones_c"], per["ones_r"], per["gv"], per["bv"]
    ids = per["ids"]
    x0T3, facc3, x1T3 = fm(sc["x0T"]), fm(sc["facc"]), fm(sc["x1T"])
    with ExitStack() as ph:
        P = ph.enter_context
        x0t = [P(nc.sbuf_tensor(f"c_x0{j}", [128, KC * TT], F32)) for j in range(2)]
        fat = [P(nc.sbuf_tensor(f"c_fa{j}", [128, KC * TT], F32)) for j in range(2)]
        x1t = P(nc.sbuf_tensor("c_x1", [128, KC * TT], F32))
        sqt = P(nc.sbuf_tensor("c_sqt", [128, KC * TT], F32))
        wp = P(nc.sbuf_tensor("c_wp", [128, KC * 2], F32))
        wt = P(nc.sbuf_tensor("c_wt", [128, KC * 2], F32))
        mv3 = P(nc.sbuf_tensor("c_mv", [128, KC * 3], F32))
        bp = P(nc.sbuf_tensor("c_bp", [1, 2], F32))
        crow = P(nc.sbuf_tensor("c_cr", [1, 8], F32))
        s1tok = P(nc.sbuf_tensor("c_s1", [128, NT128], F32))
        s2tok = P(nc.sbuf_tensor("c_s2", [128, NT128], F32))
        praw = P(nc.sbuf_tensor("c_pr", [128, 2 * NT128], F32))
        stt = P(nc.sbuf_tensor("c_stt", [128, 8 * NT128], F32))
        gam = P(nc.sbuf_tensor("c_gam", [128, NT128], F32))
        colb = P(nc.sbuf_tensor("c_col", [128, 8], F32))
        tpo = P(nc.sbuf_tensor("c_tpo", [32, 128], F32))
        jk = P(nc.sbuf_tensor("c_jk", [128, 2], F32))
        ell32 = P(nc.sbuf_tensor("c_ell", [32, 128], F32))
        cum32 = P(nc.sbuf_tensor("c_cum", [32, 128], F32))
        one32 = P(nc.sbuf_tensor("c_one", [32, 128], F32))
        psSq = [P(nc.psum_tensor(f"c_pQ{j}", [128, 512], F32)) for j in range(2)]
        psP = [P(nc.psum_tensor(f"c_pP{j}", [128, 512], F32)) for j in range(2)]
        psT2 = P(nc.psum_tensor("c_pT", [128, 512], F32))
        psX = P(nc.psum_tensor("c_pX", [128, 512], F32))

        ldx = [b.sem() for _ in range(2)]
        ldf = [b.sem() for _ in range(2)]
        ldw = b.sem()
        dvx = b.sem()
        smm = b.sem()
        sev = b.sem()
        wb1 = b.sem()
        fin = b.sem()
        wbv = b.sem()

        m1v = lambda: stt[:, 0:32]
        varv = lambda: stt[:, 32:64]
        rstd1v = lambda: stt[:, 64:96]
        rstd12v = lambda: stt[:, 96:128]
        tmpv = lambda: stt[:, 128:160]
        potVv = lambda: stt[:, 160:192]
        abv = lambda: stt[:, 192:224]

        with nc.Block("combpot") as block:
            @block.sync
            def _(s):
                s.dma_start(wp[:, :].rearrange("p (c j) -> p c j", c=KC),
                            io["w_pot"].rearrange("(c p) j -> p c j", p=128),
                            allow_slow_non_contiguous=True
                            ).then_inc(ldw.h, ldw.inc(16))
                s.dma_start(bp[:, :], io["b_pot"].rearrange("(r j) -> r j", r=1)
                            ).then_inc(ldw.h, ldw.inc(16))
                for ti in range(NTT):
                    j = ti % 2
                    if ti >= 2:
                        s.wait_ge(dvx.h, ti - 1)
                    s.dma_start(x0t[j][:, :].rearrange("p (c t) -> p c t", c=KC),
                                x0T3[:, :, ti * TT:(ti + 1) * TT]
                                ).then_inc(ldx[j].h, ldx[j].inc(16))
                    s.dma_start(fat[j][:, :].rearrange("p (c t) -> p c t", c=KC),
                                facc3[:, :, ti * TT:(ti + 1) * TT]
                                ).then_inc(ldf[j].h, ldf[j].inc(16))

            @block.gpsimd
            def _(g):
                for ti in range(NTT):
                    g.wait_ge(dvx.h, ti + 1)
                    if ti >= 1:
                        g.wait_ge(wb1.h, 16 * ti)
                    g.dma_start(x1T3[:, :, ti * TT:(ti + 1) * TT],
                                x1t[:, :].rearrange("p (c t) -> p c t", c=KC)
                                ).then_inc(wb1.h, wb1.inc(16))
                for r, row in enumerate([V_M1, V_RSTD1, V_POTV, V_POTG]):
                    g.wait_ge(fin.h, 15 + 2 * r)
                    g.dma_start(
                        sc["vecs"][row:row + 1, :].rearrange(
                            "r (c t) -> (r c) t", c=32),
                        tpo[:, :]).then_inc(wbv.h, wbv.inc(16))
                g.wait_ge(fin.h, 22)
                g.dma_start(
                    sc["vecs"][V_CUM:V_CUM + 1, :].rearrange(
                        "r (c t) -> (r c) t", c=32),
                    cum32[:, :]).then_inc(wbv.h, wbv.inc(16))
                g.wait_ge(wbv.h, wbv.n)

            @block.vector
            def _(v):
                sp = mk_sp(v, jk)
                v.memset(jk[:, :], 0.0)
                v.wait_ge(ldw.h, 32)
                wp3 = wp[:, :].rearrange("p (c j) -> p c j", c=KC)
                wt3 = wt[:, :].rearrange("p (c j) -> p c j", c=KC)
                v.memset(one32[:, :], 1.0)
                v.tensor_mul(wt3[:, :, 0], wp3[:, :, 0], gv[:, 0:KC])
                v.tensor_mul(wt3[:, :, 1], wp3[:, :, 1], gv[:, 0:KC])
                v.memset(mv3[:, 0:3 * KC:3], 1.0)
                v.tensor_copy(mv3[:, 1:3 * KC:3], wt[:, 0:2 * KC:2])
                v.tensor_copy(mv3[:, 2:3 * KC:3], wt[:, 1:2 * KC:2]
                              ).then_inc(fin.h, 1)
                for ti in range(NTT):
                    j = ti % 2
                    v.wait_ge(ldx[j].h, 16 * (ti // 2 + 1))
                    v.wait_ge(ldf[j].h, 16 * (ti // 2 + 1))
                    if ti >= 1:
                        v.wait_ge(wb1.h, 16 * ti)
                        v.wait_ge(smm.h, 4 * ti)
                    x03 = x0t[j][:, :].rearrange("p (c t) -> p c t", c=KC)
                    fa3 = fat[j][:, :].rearrange("p (c t) -> p c t", c=KC)
                    x13 = x1t[:, :].rearrange("p (c t) -> p c t", c=KC)
                    sq3 = sqt[:, :].rearrange("p (c t) -> p c t", c=KC)
                    for k in range(KC):
                        v.scalar_tensor_tensor(
                            x13[:, k, :], fa3[:, k, :], 0.5, x03[:, k, :],
                            ALU.mult, ALU.add)
                    for k in range(KC):
                        inst = v.tensor_mul(sq3[:, k, :], x13[:, k, :],
                                            x13[:, k, :])
                    inst.then_inc(dvx.h, dvx.inc())
                    for g in range(4):
                        gi = 4 * ti + g
                        v.wait_ge(smm.h, gi + 1)
                        v.tensor_copy(s1tok[:, gi:gi + 1], psP[gi % 2][:, 0:1])
                        v.tensor_copy(s2tok[:, gi:gi + 1], psSq[gi % 2][:, 0:1])
                        v.tensor_copy(praw[:, 2 * gi:2 * gi + 2],
                                      psP[gi % 2][:, 1:3]).then_inc(
                            sev.h, sev.inc())
                v.tensor_scalar_mul(m1v(), s1tok[:, :], 1.0 / D)
                v.tensor_scalar_mul(tmpv(), s2tok[:, :], 1.0 / D)
                v.tensor_mul(varv(), m1v(), m1v())
                sp()
                v.tensor_sub(varv(), tmpv(), varv()).then_inc(fin.h, 1)
                v.wait_ge(fin.h, 3)
                v.tensor_scalar(tmpv(), varv(), EPS, None, ALU.add)
                sp()
                v.reciprocal(tmpv(), tmpv())
                sp()
                v.tensor_mul(tmpv(), varv(), tmpv()).then_inc(fin.h, 1)
                v.wait_ge(fin.h, 5)
                v.tensor_mul(rstd12v(), rstd1v(), tmpv())
                v.wait_ge(fin.h, 7)
                v.tensor_copy(crow[0:1, 0:2], psX[0:1, 0:2])
                v.tensor_copy(crow[0:1, 2:4], psX[0:1, 2:4])
                sp()
                v.tensor_add(crow[0:1, 2:4], crow[0:1, 2:4], bp[0:1, 0:2]
                             ).then_inc(fin.h, 1)
                v.wait_ge(fin.h, 9)
                v.tensor_copy(colb[:, 0:4], psX[:, 4:8])
                sp()
                sp()
                prV = praw[:, 0:2 * NT128:2]
                prG = praw[:, 1:2 * NT128:2]
                v.tensor_scalar_mul(potVv(), m1v(), colb[:, 0:1])
                v.tensor_scalar_mul(tmpv(), m1v(), colb[:, 1:2])
                v.tensor_sub(potVv(), prV, potVv())
                v.tensor_sub(tmpv(), prG, tmpv())
                v.tensor_mul(potVv(), potVv(), rstd12v())
                v.tensor_mul(tmpv(), tmpv(), rstd12v())
                v.tensor_scalar_add(potVv(), potVv(), colb[:, 2:3])
                v.tensor_scalar_add(tmpv(), tmpv(), colb[:, 3:4]
                                    ).then_inc(fin.h, 1)
                v.wait_ge(fin.h, 12)
                v.tensor_scalar_max(gam[:, :], tmpv(), 0.0)
                sp()
                v.tensor_add(gam[:, :], gam[:, :], abv())
                sp()
                v.tensor_scalar_add(gam[:, :], gam[:, :], GFLOOR
                                    ).then_inc(fin.h, 1)
                for r in range(4):
                    v.wait_ge(fin.h, 14 + 2 * r)
                    v.wait_ge(wbv.h, 16 * r)
                    v.tensor_copy(tpo[:, :], psT2[0:32, 0:128]).then_inc(
                        fin.h, 1)
                sp()
                v.tensor_scalar_mul(ell32[:, :], tpo[:, :], -DTC)
                v.tensor_tensor_scan(cum32[:, :], one32[:, :], ell32[:, :],
                                     0.0, ALU.mult, ALU.add).then_inc(
                    fin.h, 1)

            @block.scalar
            def _(s):
                s.wait_ge(fin.h, 2)
                s.activation(rstd1v(), varv(), AF.Ln, bias=EPS, scale=1.0)
                s.activation(rstd1v(), rstd1v(), AF.Exp, bias=0.0, scale=-0.5
                             ).then_inc(fin.h, 1)
                s.wait_ge(fin.h, 4)
                s.activation(tmpv(), tmpv(), AF.Ln, bias=EPS, scale=1.0)
                s.activation(tmpv(), tmpv(), AF.Exp, bias=0.0, scale=-0.5
                             ).then_inc(fin.h, 1)
                s.wait_ge(fin.h, 10)
                s.activation(abv(), tmpv(), AF.Abs).then_inc(fin.h, 1)
                s.activation(abv(), abv(), AF.Exp, bias=0.0, scale=-1.0)
                s.activation(abv(), abv(), AF.Ln, bias=1.0, scale=1.0
                             ).then_inc(fin.h, 1)

            @block.tensor
            def _(t):
                t.wait_ge(fin.h, 1)
                for ti in range(NTT):
                    t.wait_ge(dvx.h, ti + 1)
                    x13 = x1t[:, :].rearrange("p (c t) -> p c t", c=KC)
                    sq3 = sqt[:, :].rearrange("p (c t) -> p c t", c=KC)
                    wt3 = wt[:, :].rearrange("p (c j) -> p c j", c=KC)
                    for g in range(4):
                        gi = 4 * ti + g
                        gs = slice(g * 128, (g + 1) * 128)
                        if gi >= 2:
                            t.wait_ge(sev.h, gi - 1)
                        for k in range(KC):
                            t.matmul(psP[gi % 2][:, 0:3], x13[:, k, gs],
                                     mv3[:, 3 * k:3 * k + 3], start=(k == 0),
                                     stop=(k == KC - 1))
                        for k in range(KC):
                            inst = t.matmul(psSq[gi % 2][:, 0:1], sq3[:, k, gs],
                                            ones_c[:, :], start=(k == 0),
                                            stop=(k == KC - 1))
                        inst.then_inc(smm.h, smm.inc())
                t.wait_ge(fin.h, 5)
                wt3 = wt[:, :].rearrange("p (c j) -> p c j", c=KC)
                wp3 = wp[:, :].rearrange("p (c j) -> p c j", c=KC)
                for k in range(KC):
                    t.matmul(psX[0:1, 0:2], ones_c[:, :], wt3[:, k, :],
                             start=(k == 0), stop=(k == KC - 1))
                for k in range(KC):
                    inst = t.matmul(psX[0:1, 2:4], bv[:, k:k + 1], wp3[:, k, :],
                                    start=(k == 0), stop=(k == KC - 1))
                inst.then_inc(fin.h, 2)
                t.wait_ge(fin.h, 8)
                for jj in range(4):
                    inst = t.matmul(psX[:, 4 + jj:5 + jj], ones_r[:, :],
                                    crow[0:1, jj:jj + 1], start=True, stop=True)
                inst.then_inc(fin.h, 1)
                t.wait_ge(fin.h, 13)
                srcs = [m1v(), rstd1v(), potVv(), gam[:, :]]
                for r, sr in enumerate(srcs):
                    if r >= 1:
                        t.wait_ge(fin.h, 13 + 2 * r)
                    t.transpose(psT2[0:32, 0:128], sr, ids[:, :]).then_inc(
                        fin.h, 1)


def emit_bk_scan(b, io, sc, per):
    nc = b.nc
    with ExitStack() as ph:
        P = ph.enter_context
        dre = P(nc.sbuf_tensor("k_dre", [32, 128], F32))
        dim = P(nc.sbuf_tensor("k_dim", [32, 128], F32))
        sfac = P(nc.sbuf_tensor("k_sf", [32, 128], F32))
        drs = P(nc.sbuf_tensor("k_drs", [32, 128], F32))
        Rf = P(nc.sbuf_tensor("k_Rf", [32, 128 * 8], F32))
        Rb = P(nc.sbuf_tensor("k_Rb", [32, 128 * 8], F32))
        sc4 = P(nc.sbuf_tensor("k_sc4", [32, 8], F32))
        sc4b = P(nc.sbuf_tensor("k_sc4b", [32, 8], F32))
        lrow = P(nc.sbuf_tensor("k_lrow", [1, 512], F32))
        srow = P(nc.sbuf_tensor("k_srow", [1, 512], F32))
        csg = P(nc.sbuf_tensor("k_csg", [1, 512], F32))
        v8 = [P(nc.sbuf_tensor(f"k_v8{j}", [1, 8], F32)) for j in range(4)]
        pscr = [P(nc.sbuf_tensor(f"k_pscr{j}", [1, 8], F32)) for j in range(2)]
        vbrow = P(nc.sbuf_tensor("k_vb", [1, 256], F32))
        vb32 = P(nc.sbuf_tensor("k_vb32", [32, 8], F32))
        ar = P(nc.sbuf_tensor("k_ar", [32, 128], F32))
        ai = P(nc.sbuf_tensor("k_ai", [32, 128], F32))
        br = P(nc.sbuf_tensor("k_br", [32, 128], F32))
        bi = P(nc.sbuf_tensor("k_bi", [32, 128], F32))
        t1 = P(nc.sbuf_tensor("k_t1", [32, 128], F32))
        t2 = P(nc.sbuf_tensor("k_t2", [32, 128], F32))
        t3 = P(nc.sbuf_tensor("k_t3", [32, 128], F32))
        gout = P(nc.sbuf_tensor("k_go", [32, 256], F32))

        ld = b.sem()
        fin = b.sem()
        wbv = b.sem()

        vrow = lambda r: sc["vecs"][r:r + 1, :].rearrange(
            "r (c t) -> (r c) t", c=32)

        def lvl1_gen(v, R, rev, s4):
            R8 = R[:, :].rearrange("p (t x) -> p t x", t=128)
            slot = (lambda ts: 127 - ts) if rev else (lambda ts: ts)
            p0 = 127 if rev else 0
            s0 = slot(0)
            yield v.tensor_mul(R8[:, s0, 0:1], dre[:, p0:p0 + 1],
                               sfac[:, p0:p0 + 1])
            yield v.tensor_mul(R8[:, s0, 1:2], dim[:, p0:p0 + 1],
                               sfac[:, p0:p0 + 1])
            yield v.tensor_scalar_mul(R8[:, s0, 2:3], sfac[:, p0:p0 + 1], -1.0)
            yield v.memset(R8[:, s0, 3:4], 0.0)
            yield v.tensor_copy(R8[:, s0, 4:5], sfac[:, p0:p0 + 1])
            yield v.memset(R8[:, s0, 5:8], 0.0)
            for ts in range(1, 128):
                pos = 127 - ts if rev else ts
                prev = R8[:, slot(ts - 1), :]
                cur = R8[:, slot(ts), :]
                dcol = dre[:, pos:pos + 1]
                icol = dim[:, pos:pos + 1]
                scol = sfac[:, pos:pos + 1]
                yield v.tensor_scalar_mul(s4[:, 0:4], prev[:, 0:4], dcol)
                yield v.tensor_scalar_mul(s4[:, 4:8], prev[:, 0:4], icol)
                yield v.tensor_sub(cur[:, 0:4:2], s4[:, 0:4:2], s4[:, 5:8:2])
                yield v.tensor_sub(cur[:, 0:4:2], cur[:, 0:4:2], prev[:, 4:8:2])
                yield v.tensor_scalar_mul(cur[:, 0:4:2], cur[:, 0:4:2], scol)
                yield v.tensor_add(cur[:, 1:4:2], s4[:, 1:4:2], s4[:, 4:8:2])
                yield v.tensor_sub(cur[:, 1:4:2], cur[:, 1:4:2], prev[:, 5:8:2])
                yield v.tensor_scalar_mul(cur[:, 1:4:2], cur[:, 1:4:2], scol)
                yield v.tensor_scalar_mul(cur[:, 4:8], prev[:, 0:4], scol)

        def lvl2_gen(v, half, va, vb_, ps):
            rev = (half == 1)
            base = 256 * half
            yield v.memset(va[:, :], 0.0)
            yield v.memset(va[:, 0:1], 1.0)
            yield v.memset(va[:, 5:6], 1.0)
            first = 31 if rev else 0
            yield v.memset(
                vbrow[:, 8 * first + 4 * half:8 * first + 4 * half + 4], 0.0)
            yield v.memset(
                vbrow[:, 8 * first + 4 * half:8 * first + 4 * half + 1], 1.0)
            order = list(range(31, 0, -1)) if rev else list(range(31))
            cur, nxt = va, vb_
            for c in order:
                nxtc = c - 1 if rev else c + 1
                g0 = base + 8 * c
                yield v.tensor_mul(ps[:, 0:4], srow[0:1, g0:g0 + 4],
                                   cur[:, 0:4])
                yield v.reduce_sum(nxt[:, 0:1], ps[:, 0:4], AX.X)
                yield v.tensor_mul(ps[:, 0:4], lrow[0:1, g0:g0 + 4],
                                   cur[:, 4:8])
                yield v.reduce_sum(nxt[:, 1:2], ps[:, 0:4], AX.X)
                yield v.tensor_mul(ps[:, 0:4], srow[0:1, g0 + 4:g0 + 8],
                                   cur[:, 0:4])
                yield v.reduce_sum(nxt[:, 2:3], ps[:, 0:4], AX.X)
                yield v.tensor_mul(ps[:, 0:4], lrow[0:1, g0 + 4:g0 + 8],
                                   cur[:, 4:8])
                yield v.reduce_sum(nxt[:, 3:4], ps[:, 0:4], AX.X)
                yield v.tensor_scalar_mul(ps[:, 4:8], nxt[:, 0:4], -1.0)
                yield v.tensor_max(ps[:, 0:4], nxt[:, 0:4], ps[:, 4:8])
                yield v.reduce_sum(ps[:, 4:5], ps[:, 0:4], AX.X)
                yield v.reciprocal(ps[:, 5:6], ps[:, 4:5])
                yield v.tensor_scalar_mul(nxt[:, 0:4], nxt[:, 0:4],
                                          ps[0:1, 5:6])
                yield v.tensor_copy(nxt[:, 4:5], nxt[:, 1:2])
                yield v.tensor_copy(nxt[:, 5:6], nxt[:, 0:1])
                yield v.tensor_copy(nxt[:, 6:7], nxt[:, 3:4])
                yield v.tensor_copy(nxt[:, 7:8], nxt[:, 2:3])
                yield v.tensor_copy(
                    vbrow[:, 8 * nxtc + 4 * half:8 * nxtc + 4 * half + 4],
                    nxt[:, 0:4])
                cur, nxt = nxt, cur

        def interleave(*gens):
            gens = list(gens)
            last = None
            while gens:
                nxt_gens = []
                for g in gens:
                    try:
                        r = next(g)
                        if r is not None:
                            last = r
                        nxt_gens.append(g)
                    except StopIteration:
                        pass
                gens = nxt_gens
            return last

        def apply_dir(v, R, half):
            R8 = R[:, :].rearrange("p (t x) -> p t x", t=128)
            o = 4 * half
            outr = ar if half == 0 else br
            outi = ai if half == 0 else bi
            vc = lambda k: vb32[:, o + k:o + k + 1]
            v.tensor_scalar_mul(t1[:, :], R8[:, :, 0], vc(0))
            v.tensor_scalar_mul(t2[:, :], R8[:, :, 1], vc(1))
            v.tensor_sub(t1[:, :], t1[:, :], t2[:, :])
            v.tensor_scalar_mul(t2[:, :], R8[:, :, 2], vc(2))
            v.tensor_add(t1[:, :], t1[:, :], t2[:, :])
            v.tensor_scalar_mul(t2[:, :], R8[:, :, 3], vc(3))
            v.tensor_sub(t1[:, :], t1[:, :], t2[:, :])
            v.tensor_scalar_mul(t2[:, :], R8[:, :, 0], vc(1))
            v.tensor_scalar_mul(t3[:, :], R8[:, :, 1], vc(0))
            v.tensor_add(t2[:, :], t2[:, :], t3[:, :])
            v.tensor_scalar_mul(t3[:, :], R8[:, :, 2], vc(3))
            v.tensor_add(t2[:, :], t2[:, :], t3[:, :])
            v.tensor_scalar_mul(t3[:, :], R8[:, :, 3], vc(2))
            v.tensor_add(t2[:, :], t2[:, :], t3[:, :])
            v.tensor_scalar_mul(outr[:, :], R8[:, :, 4], vc(0))
            v.tensor_scalar_mul(t3[:, :], R8[:, :, 5], vc(1))
            v.tensor_sub(outr[:, :], outr[:, :], t3[:, :])
            v.tensor_scalar_mul(t3[:, :], R8[:, :, 6], vc(2))
            v.tensor_add(outr[:, :], outr[:, :], t3[:, :])
            v.tensor_scalar_mul(t3[:, :], R8[:, :, 7], vc(3))
            v.tensor_sub(outr[:, :], outr[:, :], t3[:, :])
            v.tensor_scalar_mul(outi[:, :], R8[:, :, 4], vc(1))
            v.tensor_scalar_mul(t3[:, :], R8[:, :, 5], vc(0))
            v.tensor_add(outi[:, :], outi[:, :], t3[:, :])
            v.tensor_scalar_mul(t3[:, :], R8[:, :, 6], vc(3))
            v.tensor_add(outi[:, :], outi[:, :], t3[:, :])
            v.tensor_scalar_mul(t3[:, :], R8[:, :, 7], vc(2))
            v.tensor_add(outi[:, :], outi[:, :], t3[:, :])
            v.tensor_mul(t3[:, :], outr[:, :], outr[:, :])
            v.tensor_mul(drs[:, :], outi[:, :], outi[:, :])
            v.tensor_add(t3[:, :], t3[:, :], drs[:, :])
            v.reciprocal(t3[:, :], t3[:, :])
            v.tensor_mul(drs[:, :], t1[:, :], outr[:, :])
            v.tensor_mul(sfac[:, :], t2[:, :], outi[:, :])
            v.tensor_add(drs[:, :], drs[:, :], sfac[:, :])
            v.tensor_mul(sfac[:, :], t2[:, :], outr[:, :])
            v.tensor_mul(t1[:, :], t1[:, :], outi[:, :])
            v.tensor_sub(sfac[:, :], sfac[:, :], t1[:, :])
            v.tensor_mul(outr[:, :], drs[:, :], t3[:, :])
            v.tensor_mul(outi[:, :], sfac[:, :], t3[:, :])

        with nc.Block("bkscan") as block:
            @block.sync
            def _(s):
                s.dma_start(dre[:, :], vrow(V_POTV)).then_inc(ld.h, ld.inc(16))
                s.dma_start(dim[:, :], vrow(V_POTG)).then_inc(ld.h, ld.inc(16))
                s.dma_start(csg[:, :], io["csign"]).then_inc(ld.h, ld.inc(16))

            @block.gpsimd
            def _(g):
                g.wait_ge(fin.h, 3)
                g.dma_start(sc["vecs"][24:25, 0:256].rearrange(
                    "r (p x) -> (r p) x", p=32),
                    Rf[:, :].rearrange("p (t x) -> p t x", t=128)[:, 127, :]
                ).then_inc(wbv.h, wbv.inc(16))
                g.wait_ge(wbv.h, 16)
                g.dma_start(sc["vecs"][25:26, 0:256].rearrange(
                    "r (p x) -> (r p) x", p=32),
                    Rb[:, :].rearrange("p (t x) -> p t x", t=128)[:, 0, :]
                ).then_inc(wbv.h, wbv.inc(16))
                g.wait_ge(wbv.h, 32)
                g.dma_start(lrow[0:1, 0:256], sc["vecs"][24:25, 0:256]
                            ).then_inc(wbv.h, wbv.inc(16))
                g.wait_ge(wbv.h, 48)
                g.dma_start(lrow[0:1, 256:512], sc["vecs"][25:26, 0:256]
                            ).then_inc(wbv.h, wbv.inc(16))
                g.wait_ge(wbv.h, 64)
                g.nop().then_inc(fin.h, 1)
                g.wait_ge(fin.h, 5)
                g.dma_start(sc["vecs"][24:25, 0:256], vbrow[0:1, 0:256]
                            ).then_inc(wbv.h, wbv.inc(16))
                g.wait_ge(wbv.h, 80)
                g.dma_start(vb32[:, :], sc["vecs"][24:25, 0:256].rearrange(
                    "r (c x) -> (r c) x", c=32)).then_inc(wbv.h, wbv.inc(16))
                g.wait_ge(wbv.h, 96)
                g.nop().then_inc(fin.h, 1)
                g.wait_ge(fin.h, 7)
                g.dma_start(vrow(V_BKRE), gout[:, 0:128]).then_inc(
                    wbv.h, wbv.inc(16))
                g.wait_ge(wbv.h, 112)
                g.dma_start(vrow(V_BKIM), gout[:, 128:256]).then_inc(
                    wbv.h, wbv.inc(16))
                g.wait_ge(wbv.h, wbv.n)

            @block.vector
            def _(v):
                v.wait_ge(ld.h, 48)
                v.tensor_scalar_add(dre[:, :], dre[:, :], 2.0)
                v.tensor_scalar_mul(dim[:, :], dim[:, :], -1.0)
                v.tensor_mul(t1[:, :], dre[:, :], dre[:, :])
                v.tensor_mul(t2[:, :], dim[:, :], dim[:, :])
                v.tensor_add(t1[:, :], t1[:, :], t2[:, :])
                v.tensor_scalar_add(t1[:, :], t1[:, :], 1.0
                                    ).then_inc(fin.h, 1)
                v.wait_ge(fin.h, 2)
                interleave(lvl1_gen(v, Rf, False, sc4),
                           lvl1_gen(v, Rb, True, sc4b)
                           ).then_inc(fin.h, 1)
                v.wait_ge(fin.h, 4)
                v.tensor_mul(srow[:, :], lrow[:, :], csg[:, :])
                interleave(lvl2_gen(v, 0, v8[0], v8[1], pscr[0]),
                           lvl2_gen(v, 1, v8[2], v8[3], pscr[1])
                           ).then_inc(fin.h, 1)
                v.wait_ge(fin.h, 6)
                apply_dir(v, Rf, 0)
                apply_dir(v, Rb, 1)
                v.tensor_add(t1[:, :], ar[:, :], br[:, :])
                v.tensor_sub(t1[:, :], t1[:, :], dre[:, :])
                v.tensor_add(t2[:, :], ai[:, :], bi[:, :])
                v.tensor_sub(t2[:, :], t2[:, :], dim[:, :])
                v.tensor_mul(t3[:, :], t1[:, :], t1[:, :])
                v.tensor_mul(drs[:, :], t2[:, :], t2[:, :])
                v.tensor_add(t3[:, :], t3[:, :], drs[:, :])
                v.reciprocal(t3[:, :], t3[:, :])
                v.tensor_mul(gout[:, 0:128], t1[:, :], t3[:, :])
                v.tensor_mul(gout[:, 128:256], t2[:, :], t3[:, :])
                v.tensor_scalar_mul(gout[:, 128:256], gout[:, 128:256], -1.0
                                    ).then_inc(fin.h, 1)

            @block.scalar
            def _(s):
                s.wait_ge(fin.h, 1)
                s.activation(t1[:, :], t1[:, :], AF.Ln)
                s.activation(sfac[:, :], t1[:, :], AF.Exp, bias=0.0,
                             scale=-0.5).then_inc(fin.h, 1)


def emit_attn_core(b, io, sc, per):
    nc = b.nc
    x1T3, y1T3, y2T3 = fm(sc["x1T"]), fm(sc["y1T"]), fm(sc["y2T"])
    with ExitStack() as outer:
        O_ = outer.enter_context
        qkvT = O_(nc.sbuf_tensor("a_qkv", [128, 12 * T], BF16))
        qkvT3 = qkvT[:, :].rearrange("p (c t) -> p c t", c=12)

        _emit_p5a(b, io, sc, per, x1T3, y1T3)
        b.new_phase()
        _emit_p5b(b, io, sc, per, y1T3, qkvT3)
        b.new_phase()
        _emit_p6(b, io, sc, per, qkvT3, y1T3, y2T3)


def _emit_p5a(b, io, sc, per, x1T3, y1T3):
    nc = b.nc
    ones_r, ones_c, bbkc, ids = (per["ones_r"], per["ones_c"], per["bbkc"],
                                 per["ids"])
    with ExitStack() as ph:
        P = ph.enter_context
        x1t = [P(nc.sbuf_tensor(f"a_x1{j}", [128, KC * TT], F32))
               for j in range(2)]
        y1t = P(nc.sbuf_tensor("a_y1", [128, KC * TT], F32))
        sqt = P(nc.sbuf_tensor("a_sq", [128, KC * TT], F32))
        rowm = [P(nc.sbuf_tensor(f"a_rm{j}", [1, TT], F32)) for j in range(2)]
        rowr = [P(nc.sbuf_tensor(f"a_rr{j}", [1, TT], F32)) for j in range(2)]
        bkT = [P(nc.sbuf_tensor(f"a_bk{j}", [2, TT], F32)) for j in range(2)]
        wbk = P(nc.sbuf_tensor("a_wbk", [2, D], F32))
        s1tok = P(nc.sbuf_tensor("a_s1", [128, NT128], F32))
        s2tok = P(nc.sbuf_tensor("a_s2", [128, NT128], F32))
        stt = P(nc.sbuf_tensor("a_stt", [128, 4 * NT128], F32))
        tpo = P(nc.sbuf_tensor("a_tpo", [32, 128], F32))
        jk = P(nc.sbuf_tensor("a_jk", [128, 2], F32))
        psB1 = P(nc.psum_tensor("a_pB1", [128, 512], F32))
        psB2 = P(nc.psum_tensor("a_pB2", [128, 512], F32))
        psY = [P(nc.psum_tensor(f"a_pY{j}", [128, 512], F32)) for j in range(2)]
        psSum = [P(nc.psum_tensor(f"a_pS{j}", [128, 512], F32))
                 for j in range(2)]
        psT2 = P(nc.psum_tensor("a_pT", [128, 512], F32))

        ldx = [b.sem() for _ in range(2)]
        ldr = b.sem()
        ldr2 = [b.sem() for _ in range(2)]
        bcd = b.sem()
        ykd = b.sem()
        dvy = b.sem()
        sqd = b.sem()
        smm = b.sem()
        sev = b.sem()
        wb1 = b.sem()
        fin = b.sem()
        wbv = b.sem()

        m1v = lambda: stt[:, 0:32]
        varv = lambda: stt[:, 32:64]
        rstdv = lambda: stt[:, 64:96]
        tmpv = lambda: stt[:, 96:128]

        with nc.Block("p5a") as block:
            @block.sync
            def _(s):
                s.dma_start(wbk[:, :], io["w_bk"]).then_inc(ldr.h, ldr.inc(16))
                for ti in range(NTT):
                    j = ti % 2
                    ts_ = slice(ti * TT, (ti + 1) * TT)
                    if ti >= 2:
                        s.wait_ge(dvy.h, 8 * (ti - 1))
                        s.wait_ge(bcd.h, ti - 1)
                    s.dma_start(
                        x1t[j][:, :].rearrange("p (c t) -> p c t", c=KC),
                        x1T3[:, :, ts_]).then_inc(ldx[j].h, ldx[j].inc(16))
                    s.dma_start(rowm[j][:, :], sc["vecs"][V_M1:V_M1 + 1, ts_]
                                ).then_inc(ldr2[j].h, ldr2[j].inc(16))
                    s.dma_start(rowr[j][:, :],
                                sc["vecs"][V_RSTD1:V_RSTD1 + 1, ts_]
                                ).then_inc(ldr2[j].h, ldr2[j].inc(16))
                    s.dma_start(bkT[j][0:1, :],
                                sc["vecs"][V_BKRE:V_BKRE + 1, ts_]
                                ).then_inc(ldr2[j].h, ldr2[j].inc(16))
                    s.dma_start(bkT[j][1:2, :],
                                sc["vecs"][V_BKIM:V_BKIM + 1, ts_]
                                ).then_inc(ldr2[j].h, ldr2[j].inc(16))

            @block.tensor
            def _(t):
                t.wait_ge(ldr.h, 16)
                for ti in range(NTT):
                    j = ti % 2
                    t.wait_ge(ldr2[j].h, 64 * (ti // 2 + 1))
                    if ti >= 1:
                        t.wait_ge(dvy.h, 8 * ti)
                    t.matmul(psB1[:, :], ones_r[:, :], rowm[j][0:1, :],
                             start=True, stop=True)
                    t.matmul(psB2[:, :], ones_r[:, :], rowr[j][0:1, :],
                             start=True, stop=True).then_inc(bcd.h, bcd.inc())
                    for k in range(KC):
                        gi = 8 * ti + k
                        if gi >= 2:
                            t.wait_ge(dvy.h, gi - 1)
                        t.matmul(psY[gi % 2][:, :],
                                 wbk[:, k * 128:(k + 1) * 128],
                                 bkT[j][:, :], start=True, stop=True
                                 ).then_inc(ykd.h, ykd.inc())
                    t.wait_ge(dvy.h, 8 * (ti + 1))
                    t.wait_ge(sqd.h, ti + 1)
                    y13 = y1t[:, :].rearrange("p (c t) -> p c t", c=KC)
                    sq3 = sqt[:, :].rearrange("p (c t) -> p c t", c=KC)
                    for g in range(4):
                        gi4 = 4 * ti + g
                        gs = slice(g * 128, (g + 1) * 128)
                        if gi4 >= 2:
                            t.wait_ge(sev.h, gi4 - 1)
                        for k in range(KC):
                            t.matmul(psSum[gi4 % 2][:, 0:1], y13[:, k, gs],
                                     ones_c[:, :], start=(k == 0),
                                     stop=(k == KC - 1))
                        for k in range(KC):
                            inst = t.matmul(psSum[gi4 % 2][:, 1:2],
                                            sq3[:, k, gs], ones_c[:, :],
                                            start=(k == 0), stop=(k == KC - 1))
                        inst.then_inc(smm.h, smm.inc())
                t.wait_ge(fin.h, 3)
                t.transpose(psT2[0:32, 0:128], m1v(), ids[:, :]).then_inc(
                    fin.h, 1)
                t.wait_ge(fin.h, 5)
                t.transpose(psT2[0:32, 0:128], rstdv(), ids[:, :]).then_inc(
                    fin.h, 1)

            @block.vector
            def _(v):
                v.memset(jk[:, :], 0.0)
                for ti in range(NTT):
                    j = ti % 2
                    v.wait_ge(ldx[j].h, 16 * (ti // 2 + 1))
                    v.wait_ge(bcd.h, ti + 1)
                    if ti >= 1:
                        v.wait_ge(wb1.h, 16 * ti)
                        v.wait_ge(smm.h, 4 * ti)
                        v.wait_ge(sqd.h, ti)
                    x13 = x1t[j][:, :].rearrange("p (c t) -> p c t", c=KC)
                    y13 = y1t[:, :].rearrange("p (c t) -> p c t", c=KC)
                    sq3 = sqt[:, :].rearrange("p (c t) -> p c t", c=KC)
                    for k in range(KC):
                        gi = 8 * ti + k
                        v.wait_ge(ykd.h, gi + 1)
                        v.tensor_sub(y13[:, k, :], x13[:, k, :], psB1[:, :])
                        v.tensor_mul(y13[:, k, :], y13[:, k, :], psB2[:, :])
                        v.scalar_tensor_tensor(
                            y13[:, k, :], psY[gi % 2][:, :], bbkc[:, k:k + 1],
                            y13[:, k, :], ALU.add, ALU.add
                        ).then_inc(dvy.h, dvy.inc())
                    for g in range(4):
                        gi4 = 4 * ti + g
                        v.wait_ge(smm.h, gi4 + 1)
                        v.tensor_copy(s1tok[:, gi4:gi4 + 1],
                                      psSum[gi4 % 2][:, 0:1])
                        v.tensor_copy(s2tok[:, gi4:gi4 + 1],
                                      psSum[gi4 % 2][:, 1:2]).then_inc(
                            sev.h, sev.inc())
                v.tensor_scalar_mul(m1v(), s1tok[:, :], 1.0 / D)
                v.tensor_scalar_mul(tmpv(), s2tok[:, :], 1.0 / D)
                v.tensor_mul(varv(), m1v(), m1v())
                v.tensor_scalar_mul(jk[:, 0:1], jk[:, 1:2], 1.0)
                v.tensor_sub(varv(), tmpv(), varv()).then_inc(fin.h, 1)
                v.wait_ge(fin.h, 2)
                v.nop().then_inc(fin.h, 1)
                v.wait_ge(fin.h, 4)
                v.tensor_copy(tpo[:, :], psT2[0:32, 0:128]).then_inc(
                    fin.h, 1)
                v.wait_ge(fin.h, 6)
                v.wait_ge(wbv.h, 16)
                v.tensor_copy(tpo[:, :], psT2[0:32, 0:128]).then_inc(
                    fin.h, 1)

            @block.scalar
            def _(s):
                y13 = y1t[:, :].rearrange("p (c t) -> p c t", c=KC)
                sq3 = sqt[:, :].rearrange("p (c t) -> p c t", c=KC)
                for ti in range(NTT):
                    s.wait_ge(dvy.h, 8 * (ti + 1))
                    if ti >= 1:
                        s.wait_ge(smm.h, 4 * ti)
                    for k in range(KC):
                        inst = s.activation(sq3[:, k, :], y13[:, k, :],
                                            AF.Square)
                    inst.then_inc(sqd.h, sqd.inc())
                s.wait_ge(fin.h, 1)
                s.activation(rstdv(), varv(), AF.Ln, bias=EPS, scale=1.0)
                s.activation(rstdv(), rstdv(), AF.Exp, bias=0.0, scale=-0.5
                             ).then_inc(fin.h, 1)

            @block.gpsimd
            def _(g):
                for ti in range(NTT):
                    g.wait_ge(dvy.h, 8 * (ti + 1))
                    if ti >= 1:
                        g.wait_ge(wb1.h, 16 * ti)
                    g.dma_start(y1T3[:, :, ti * TT:(ti + 1) * TT],
                                y1t[:, :].rearrange("p (c t) -> p c t", c=KC)
                                ).then_inc(wb1.h, wb1.inc(16))
                for r, row in enumerate([V_MA, V_RSTDA]):
                    g.wait_ge(fin.h, 5 + 2 * r)
                    g.dma_start(sc["vecs"][row:row + 1, :].rearrange(
                        "r (c t) -> (r c) t", c=32),
                        tpo[:, :]).then_inc(wbv.h, wbv.inc(16))
                g.wait_ge(wbv.h, wbv.n)


def _emit_p5b(b, io, sc, per, y1T3, qkvT3):
    nc = b.nc
    ones_r, gv, bv, bqkvc = (per["ones_r"], per["gv"], per["bv"],
                             per["bqkvc"])
    with ExitStack() as ph:
        P = ph.enter_context
        y1b = [P(nc.sbuf_tensor(f"b_y1{j}", [128, KC * TT], F32))
               for j in range(2)]
        xnt = [P(nc.sbuf_tensor(f"b_xn{j}", [128, KC * TT], BF16))
               for j in range(2)]
        wq = P(nc.sbuf_tensor("b_wq", [128, KC * 1536], BF16))
        rowm = [P(nc.sbuf_tensor(f"b_rm{j}", [1, TT], F32)) for j in range(2)]
        rowr = [P(nc.sbuf_tensor(f"b_rr{j}", [1, TT], F32)) for j in range(2)]
        psB1 = [P(nc.psum_tensor(f"b_pB1{j}", [128, 512], F32))
                for j in range(2)]
        psB2 = [P(nc.psum_tensor(f"b_pB2{j}", [128, 512], F32))
                for j in range(2)]
        psQ = [P(nc.psum_tensor(f"b_pQ{j}", [128, 512], F32)) for j in range(2)]
        wq3 = wq[:, :].rearrange("p (c n) -> p c n", c=KC)

        ldx = [b.sem() for _ in range(2)]
        ldr2 = [b.sem() for _ in range(2)]
        ldw = b.sem()
        bcd = b.sem()
        dvx = b.sem()
        qmm = b.sem()
        qev = b.sem()

        with nc.Block("p5b") as block:
            @block.sync
            def _(s):
                for ti in range(NTT):
                    j = ti % 2
                    ts_ = slice(ti * TT, (ti + 1) * TT)
                    if ti >= 2:
                        s.wait_ge(dvx.h, ti - 1)
                        s.wait_ge(bcd.h, ti - 1)
                    s.dma_start(
                        y1b[j][:, :].rearrange("p (c t) -> p c t", c=KC),
                        y1T3[:, :, ts_]).then_inc(ldx[j].h, ldx[j].inc(16))
                    s.dma_start(rowm[j][:, :], sc["vecs"][V_MA:V_MA + 1, ts_]
                                ).then_inc(ldr2[j].h, ldr2[j].inc(16))
                    s.dma_start(rowr[j][:, :],
                                sc["vecs"][V_RSTDA:V_RSTDA + 1, ts_]
                                ).then_inc(ldr2[j].h, ldr2[j].inc(16))

            @block.gpsimd
            def _(g):
                g.dma_start(wq3, io["w_qkv"].rearrange(
                    "(c p) n -> p c n", p=128)).then_inc(ldw.h, ldw.inc(16))

            @block.tensor
            def _(t):
                t.wait_ge(ldw.h, 16)

                def bcast(k):
                    jj = k % 2
                    t.wait_ge(ldr2[jj].h, 32 * (k // 2 + 1))
                    if k >= 2:
                        t.wait_ge(dvx.h, k - 1)
                    t.matmul(psB1[k % 2][:, :], ones_r[:, :], rowm[jj][0:1, :],
                             start=True, stop=True)
                    t.matmul(psB2[k % 2][:, :], ones_r[:, :], rowr[jj][0:1, :],
                             start=True, stop=True).then_inc(bcd.h, bcd.inc())

                bcast(0)
                for ti in range(NTT):
                    if ti + 1 < NTT:
                        bcast(ti + 1)
                    t.wait_ge(dvx.h, ti + 1)
                    xn3 = xnt[ti % 2][:, :].rearrange("p (c t) -> p c t", c=KC)
                    for n in range(12):
                        gi = 12 * ti + n
                        if gi >= 2:
                            t.wait_ge(qev.h, gi - 1)
                        for k in range(KC):
                            inst = t.matmul(
                                psQ[gi % 2][:, :],
                                wq3[:, k, n * 128:(n + 1) * 128],
                                xn3[:, k, :], start=(k == 0),
                                stop=(k == KC - 1))
                        inst.then_inc(qmm.h, qmm.inc())

            @block.vector
            def _(v):
                for ti in range(NTT):
                    j = ti % 2
                    v.wait_ge(ldx[j].h, 16 * (ti // 2 + 1))
                    v.wait_ge(bcd.h, ti + 1)
                    if ti >= 2:
                        v.wait_ge(qmm.h, 12 * (ti - 1))
                    y13 = y1b[j][:, :].rearrange("p (c t) -> p c t", c=KC)
                    xn3 = xnt[ti % 2][:, :].rearrange("p (c t) -> p c t", c=KC)
                    for k in range(KC):
                        v.tensor_sub(y13[:, k, :], y13[:, k, :],
                                     psB1[ti % 2][:, :])
                        v.tensor_mul(y13[:, k, :], y13[:, k, :],
                                     psB2[ti % 2][:, :])
                        inst = v.tensor_scalar(
                            xn3[:, k, :], y13[:, k, :],
                            gv[:, KC + k:KC + k + 1], bv[:, KC + k:KC + k + 1],
                            ALU.mult, ALU.add)
                    inst.then_inc(dvx.h, dvx.inc())

            @block.scalar
            def _(s):
                for ti in range(NTT):
                    ts_ = slice(ti * TT, (ti + 1) * TT)
                    for n in range(12):
                        gi = 12 * ti + n
                        s.wait_ge(qmm.h, gi + 1)
                        s.activation(
                            qkvT3[:, n, ts_], psQ[gi % 2][:, :], AF.Identity,
                            bias=bqkvc[:, n:n + 1], scale=1.0
                        ).then_inc(qev.h, qev.inc())


def _emit_p6(b, io, sc, per, qkvT3, y1T3, y2T3):
    nc = b.nc
    ones_r, ids, idsb, trileta, boutc = (per["ones_r"], per["ids"],
                                         per["idsb"], per["trileta"],
                                         per["boutc"])
    with ExitStack() as ph:
        P = ph.enter_context
        cumr = P(nc.sbuf_tensor("h_cum", [1, T], F32))
        epr = P(nc.sbuf_tensor("h_ep", [1, T], F32))
        scr = P(nc.sbuf_tensor("h_sc", [1, 32], F32))
        one1 = P(nc.sbuf_tensor("h_one1", [1, 1], F32))
        Dm = P(nc.sbuf_tensor("h_Dm", [128, 128], F32))
        kcol = P(nc.sbuf_tensor("h_kc", [128, 1], F32))
        epcol = P(nc.sbuf_tensor("h_epc", [128, 1], F32))
        sccol = P(nc.sbuf_tensor("h_scc", [128, 1], F32))
        ktm = P(nc.sbuf_tensor("h_ktm", [128, 128], BF16))
        vtm = P(nc.sbuf_tensor("h_vtm", [128, 128], BF16))
        qtm = P(nc.sbuf_tensor("h_qtm", [128, 128], BF16))
        qfm = P(nc.sbuf_tensor("h_qfm", [128, 128], BF16))
        Pt = [P(nc.sbuf_tensor(f"h_Pt{j}", [128, 128], BF16)) for j in range(2)]
        Wst = P(nc.sbuf_tensor("h_W", [128, 4 * 64], F32))
        Wb = P(nc.sbuf_tensor("h_Wb", [128, 64], BF16))
        jk = P(nc.sbuf_tensor("h_jk", [128, 2], F32))
        oc = [P(nc.sbuf_tensor(f"h_oc{j}", [64, 8 * 128], BF16))
              for j in range(2)]
        wo = P(nc.sbuf_tensor("h_wo", [64, 8 * D], BF16))
        y1s = [P(nc.sbuf_tensor(f"h_y1{j}", [128, KC * 128], F32))
               for j in range(2)]
        y2s = [P(nc.sbuf_tensor(f"h_y2{j}", [128, KC * 128], F32))
               for j in range(2)]
        pB1 = P(nc.psum_tensor("h_B1", [128, 1024], BF16))
        pB2 = P(nc.psum_tensor("h_B2", [128, 1024], BF16))
        pB3 = P(nc.psum_tensor("h_B3", [128, 1024], BF16))
        pF1 = P(nc.psum_tensor("h_F1", [128, 512], F32))
        pF2 = P(nc.psum_tensor("h_F2", [128, 512], F32))
        pDX = P(nc.psum_tensor("h_DX", [128, 512], F32))
        pW = [P(nc.psum_tensor(f"h_pW{j}", [128, 512], F32)) for j in range(2)]

        ldr = b.sem()
        ldw = b.sem()
        ldy = [b.sem() for _ in range(2)]
        acr = b.sem()
        pe6 = b.sem()
        dv6 = b.sem()
        wb2 = b.sem()
        Wst3 = Wst[:, :].rearrange("p (c x) -> p c x", c=4)
        wo3 = wo[:, :].rearrange("p (c n) -> p c n", c=8)

        PEC = 45

        def pe_cnt(j, c2, e):
            return PEC * j + (1 if e == 0 else 1 + 9 * c2 + e)

        def pe_wout(j, k):
            return PEC * j + 37 + k + 1

        def dv_cnt(j, c2, e):
            return 1 + PEC * j + (1 if e == 0 else 1 + 9 * c2 + e)

        def dv_wout(j, k):
            return 1 + PEC * j + 37 + k + 1

        def prev_pair(j, c2):
            return (j, c2 - 1) if c2 > 0 else (j - 1, 3)

        with nc.Block("p6") as block:
            @block.sync
            def _(s):
                s.dma_start(cumr[:, :], sc["vecs"][V_CUM:V_CUM + 1, :]
                            ).then_inc(ldr.h, ldr.inc(16))
                s.dma_start(scr[:, :], sc["vecs"][V_CUM:V_CUM + 1, 127:T:128],
                            allow_slow_non_contiguous=True
                            ).then_inc(ldr.h, ldr.inc(16))
                for j2 in range(NHCH):
                    sl = j2 % 2
                    if j2 >= 2:
                        s.wait_ge(dv6.h, dv_wout(j2 - 2, 7))
                    s.dma_start(
                        y1s[sl][:, :].rearrange("p (c t) -> p c t", c=KC),
                        y1T3[:, :, j2 * 128:(j2 + 1) * 128]
                    ).then_inc(ldy[sl].h, ldy[sl].inc(16))

            @block.gpsimd
            def _(g):
                g.dma_start(wo3, io["w_out"].rearrange("(c p) n -> p c n",
                                                       p=64)
                            ).then_inc(ldw.h, ldw.inc(16))
                for j2 in range(NHCH):
                    sl = j2 % 2
                    g.wait_ge(dv6.h, dv_wout(j2, 7))
                    if j2 >= 2:
                        g.wait_ge(wb2.h, 16 * (j2 - 1))
                    g.dma_start(y2T3[:, :, j2 * 128:(j2 + 1) * 128],
                                y2s[sl][:, :].rearrange("p (c t) -> p c t",
                                                        c=KC)
                                ).then_inc(wb2.h, wb2.inc(16))
                g.wait_ge(wb2.h, wb2.n)

            @block.scalar
            def _(s):
                s.wait_ge(ldr.h, 32)
                s.activation(epr[:, :], cumr[:, :], AF.Exp)
                s.activation(cumr[:, :], cumr[:, :], AF.Exp, bias=0.0,
                             scale=-1.0)
                s.activation(scr[:, :], scr[:, :], AF.Exp).then_inc(
                    acr.h, acr.inc())

            enr = cumr

            @block.tensor
            def _(t):
                t.wait_ge(acr.h, 1)
                t.wait_ge(dv6.h, 1)
                t.wait_ge(ldw.h, 16)
                for j in range(NHCH):
                    js = slice(j * HCH, (j + 1) * HCH)
                    if j >= 1:
                        t.wait_ge(dv6.h, dv_cnt(j - 1, 0, 0))
                    t.matmul(pDX[:, 0:128], enr[0:1, js], epr[0:1, js],
                             start=True, stop=True)
                    t.matmul(pDX[:, 128:129], enr[0:1, js], scr[0:1, j:j + 1],
                             start=True, stop=True)
                    t.matmul(pDX[:, 129:130], ones_r[:, :], scr[0:1, j:j + 1],
                             start=True, stop=True)
                    t.matmul(pDX[:, 130:131], epr[0:1, js], one1[0:1, 0:1],
                             start=True, stop=True).then_inc(pe6.h, 1)
                    for c2 in range(4):
                        pj, pc = prev_pair(j, c2)
                        kview = qkvT3[:, 4 + c2, js]
                        vview = qkvT3[:, 8 + c2, js]
                        qview = qkvT3[:, c2, js]
                        if pj >= 0:
                            t.wait_ge(dv6.h, dv_cnt(pj, pc, 4))
                        t.transpose(pB1[:, 0:128], kview, idsb[:, :]
                                    ).then_inc(pe6.h, 1)
                        if pj >= 0:
                            t.wait_ge(dv6.h, dv_cnt(pj, pc, 2))
                        t.transpose(pB2[:, 0:128], vview, idsb[:, :]
                                    ).then_inc(pe6.h, 1)
                        if pj >= 0:
                            t.wait_ge(dv6.h, dv_cnt(pj, pc, 3))
                        t.transpose(pB3[:, 0:128], qview, idsb[:, :]
                                    ).then_inc(pe6.h, 1)
                        t.wait_ge(dv6.h, dv_cnt(j, c2, 3))
                        t.transpose(pB1[:, 0:128], qtm[:, :], idsb[:, :]
                                    ).then_inc(pe6.h, 1)
                        if pj >= 0:
                            t.wait_ge(dv6.h, dv_cnt(pj, pc, 9))
                        t.matmul(pF1[:, 0:128], kview[0:64, :], qview[0:64, :],
                                 start=True, stop=True).then_inc(pe6.h, 1)
                        if pj >= 0:
                            t.wait_ge(dv6.h, dv_cnt(pj, pc, 8))
                        t.matmul(pF2[:, 0:128], kview[64:128, :],
                                 qview[64:128, :], start=True, stop=True
                                 ).then_inc(pe6.h, 1)
                        t.wait_ge(dv6.h, dv_cnt(j, c2, 5))
                        t.matmul(pF1[:, 0:128], ktm[:, :], vtm[:, :],
                                 start=True, stop=True).then_inc(pe6.h, 1)
                        t.wait_ge(dv6.h, dv_cnt(j, c2, 7))
                        t.matmul(pF2[0:64, 0:128], Wb[0:64, :], qfm[0:64, :],
                                 start=True, stop=False)
                        t.matmul(pF2[0:64, 0:128], vtm[:, 0:64], Pt[0][:, :],
                                 start=False, stop=True).then_inc(pe6.h, 1)
                        t.matmul(pF1[0:64, 0:128], Wb[64:128, :],
                                 qfm[64:128, :], start=True, stop=False)
                        t.matmul(pF1[0:64, 0:128], vtm[:, 64:128], Pt[1][:, :],
                                 start=False, stop=True).then_inc(pe6.h, 1)
                    t.wait_ge(dv6.h, dv_cnt(j, 3, 9))
                    oc3 = oc[j % 2][:, :].rearrange("p (c t) -> p c t", c=8)
                    for k in range(KC):
                        if pe_wout(j, k) > 2:
                            pass
                        if k >= 2 or j >= 1:
                            kk_prev = (j, k - 2) if k >= 2 else (j - 1, k + 6)
                            t.wait_ge(dv6.h, dv_wout(*kk_prev))
                        for kk in range(8):
                            inst = t.matmul(
                                pW[k % 2][:, 0:128],
                                wo3[:, kk, k * 128:(k + 1) * 128],
                                oc3[:, kk, :], start=(kk == 0), stop=(kk == 7))
                        inst.then_inc(pe6.h, 1)

            @block.vector
            def _(v):
                v.memset(Wst[:, :], 0.0)
                v.memset(one1[:, :], 1.0)
                v.memset(jk[:, :], 0.0)
                v.tensor_scalar_mul(jk[:, 0:1], jk[:, 1:2], 1.0)
                v.tensor_scalar_mul(jk[:, 0:1], jk[:, 1:2], 1.0
                                    ).then_inc(dv6.h, 1)
                for j in range(NHCH):
                    js = slice(j * HCH, (j + 1) * HCH)
                    sl = j % 2
                    v.wait_ge(pe6.h, pe_cnt(j, 0, 0))
                    v.tensor_mul(Dm[:, :], pDX[:, 0:128], trileta[:, :])
                    v.tensor_copy(kcol[:, :], pDX[:, 128:129])
                    v.tensor_copy(sccol[:, :], pDX[:, 129:130])
                    v.tensor_copy(epcol[:, :], pDX[:, 130:131]).then_inc(
                        dv6.h, 1)
                    for c2 in range(4):
                        v.wait_ge(pe6.h, pe_cnt(j, c2, 1))
                        v.tensor_scalar_mul(ktm[:, :], pB1[:, 0:128],
                                            kcol[:, 0:1]).then_inc(dv6.h, 1)
                        v.wait_ge(pe6.h, pe_cnt(j, c2, 2))
                        v.tensor_copy(vtm[:, :], pB2[:, 0:128]).then_inc(
                            dv6.h, 1)
                        v.wait_ge(pe6.h, pe_cnt(j, c2, 3))
                        v.tensor_scalar_mul(qtm[:, :], pB3[:, 0:128],
                                            epcol[:, 0:1]).then_inc(dv6.h, 1)
                        v.wait_ge(pe6.h, pe_cnt(j, c2, 4))
                        v.tensor_copy(qfm[:, :], pB1[:, 0:128]).then_inc(
                            dv6.h, 1)
                        v.wait_ge(pe6.h, pe_cnt(j, c2, 5))
                        v.tensor_mul(Pt[0][:, :], pF1[:, 0:128], Dm[:, :]
                                     ).then_inc(dv6.h, 1)
                        v.wait_ge(pe6.h, pe_cnt(j, c2, 6))
                        v.tensor_mul(Pt[1][:, :], pF2[:, 0:128], Dm[:, :]
                                     ).then_inc(dv6.h, 1)
                        v.wait_ge(pe6.h, pe_cnt(j, c2, 7))
                        v.tensor_copy(Wb[:, :], Wst3[:, c2, :])
                        for hh in range(2):
                            bs = slice(hh * 64, hh * 64 + 64)
                            v.tensor_scalar_mul(Wst3[:, c2, :][bs, :],
                                                Wst3[:, c2, :][bs, :],
                                                sccol[bs, 0:1])
                            v.scalar_tensor_tensor(
                                Wst3[:, c2, :][bs, :],
                                pF1[bs, 0:128][:, slice(hh * 64, hh * 64 + 64)],
                                ETA, Wst3[:, c2, :][bs, :],
                                ALU.mult, ALU.add)
                        v.tensor_scalar_mul(jk[:, 0:1], jk[:, 1:2], 1.0
                                            ).then_inc(dv6.h, 1)
                        oc3 = oc[sl][:, :].rearrange("p (c t) -> p c t", c=8)
                        v.wait_ge(pe6.h, pe_cnt(j, c2, 8))
                        if j >= 2 and c2 == 0:
                            v.wait_ge(pe6.h, pe_wout(j - 2, 7))
                        v.tensor_copy(oc3[:, 2 * c2, :], pF2[0:64, 0:128]
                                      ).then_inc(dv6.h, 1)
                        v.wait_ge(pe6.h, pe_cnt(j, c2, 9))
                        v.tensor_copy(oc3[:, 2 * c2 + 1, :], pF1[0:64, 0:128]
                                      ).then_inc(dv6.h, 1)
                    v.wait_ge(ldy[sl].h, 16 * (j // 2 + 1))
                    if j >= 2:
                        v.wait_ge(wb2.h, 16 * (j - 1))
                    y13 = y1s[sl][:, :].rearrange("p (c t) -> p c t", c=KC)
                    y23 = y2s[sl][:, :].rearrange("p (c t) -> p c t", c=KC)
                    for k in range(KC):
                        v.wait_ge(pe6.h, pe_wout(j, k))
                        v.scalar_tensor_tensor(
                            y23[:, k, :], pW[k % 2][:, 0:128],
                            boutc[:, k:k + 1], y13[:, k, :],
                            ALU.add, ALU.add).then_inc(dv6.h, 1)


def emit_y2stats(b, io, sc, per):
    nc = b.nc
    ones_c, ids = per["ones_c"], per["ids"]
    y2T3 = fm(sc["y2T"])
    with ExitStack() as ph:
        P = ph.enter_context
        y2t = [P(nc.sbuf_tensor(f"w_y2{j}", [128, KC * TT], F32))
               for j in range(2)]
        sqt = P(nc.sbuf_tensor("w_sq", [128, KC * TT], F32))
        s1tok = P(nc.sbuf_tensor("w_s1", [128, NT128], F32))
        s2tok = P(nc.sbuf_tensor("w_s2", [128, NT128], F32))
        stt = P(nc.sbuf_tensor("w_stt", [128, 4 * NT128], F32))
        tpo = P(nc.sbuf_tensor("w_tpo", [32, 128], F32))
        jk = P(nc.sbuf_tensor("w_jk", [128, 2], F32))
        psSum = [P(nc.psum_tensor(f"w_pS{j}", [128, 512], F32))
                 for j in range(2)]
        psT2 = P(nc.psum_tensor("w_pT", [128, 512], F32))

        ldx = [b.sem() for _ in range(2)]
        dvq = b.sem()
        smm = b.sem()
        sev = b.sem()
        fin = b.sem()
        wbv = b.sem()

        m1v = lambda: stt[:, 0:32]
        varv = lambda: stt[:, 32:64]
        rstdv = lambda: stt[:, 64:96]
        tmpv = lambda: stt[:, 96:128]

        with nc.Block("p7b") as block:
            @block.sync
            def _(s):
                for ti in range(NTT):
                    j = ti % 2
                    if ti >= 2:
                        s.wait_ge(smm.h, 4 * (ti - 1))
                    s.dma_start(
                        y2t[j][:, :].rearrange("p (c t) -> p c t", c=KC),
                        y2T3[:, :, ti * TT:(ti + 1) * TT]
                    ).then_inc(ldx[j].h, ldx[j].inc(16))

            @block.vector
            def _(v):
                v.memset(jk[:, :], 0.0)
                for ti in range(NTT):
                    for g in range(4):
                        gi4 = 4 * ti + g
                        v.wait_ge(smm.h, gi4 + 1)
                        v.tensor_copy(s1tok[:, gi4:gi4 + 1],
                                      psSum[gi4 % 2][:, 0:1])
                        v.tensor_copy(s2tok[:, gi4:gi4 + 1],
                                      psSum[gi4 % 2][:, 1:2]).then_inc(
                            sev.h, sev.inc())
                v.tensor_scalar_mul(m1v(), s1tok[:, :], 1.0 / D)
                v.tensor_scalar_mul(tmpv(), s2tok[:, :], 1.0 / D)
                v.tensor_mul(varv(), m1v(), m1v())
                v.tensor_scalar_mul(jk[:, 0:1], jk[:, 1:2], 1.0)
                v.tensor_sub(varv(), tmpv(), varv()).then_inc(fin.h, 1)
                v.wait_ge(fin.h, 2)
                v.nop().then_inc(fin.h, 1)
                v.wait_ge(fin.h, 4)
                v.tensor_copy(tpo[:, :], psT2[0:32, 0:128]).then_inc(
                    fin.h, 1)
                v.wait_ge(fin.h, 6)
                v.wait_ge(wbv.h, 16)
                v.tensor_copy(tpo[:, :], psT2[0:32, 0:128]).then_inc(
                    fin.h, 1)

            @block.tensor
            def _(t):
                for ti in range(NTT):
                    j = ti % 2
                    t.wait_ge(dvq.h, ti + 1)
                    y23 = y2t[j][:, :].rearrange("p (c t) -> p c t", c=KC)
                    sq3 = sqt[:, :].rearrange("p (c t) -> p c t", c=KC)
                    for g in range(4):
                        gi4 = 4 * ti + g
                        gs = slice(g * 128, (g + 1) * 128)
                        if gi4 >= 2:
                            t.wait_ge(sev.h, gi4 - 1)
                        for k in range(KC):
                            t.matmul(psSum[gi4 % 2][:, 0:1], y23[:, k, gs],
                                     ones_c[:, :], start=(k == 0),
                                     stop=(k == KC - 1))
                        for k in range(KC):
                            inst = t.matmul(psSum[gi4 % 2][:, 1:2],
                                            sq3[:, k, gs], ones_c[:, :],
                                            start=(k == 0), stop=(k == KC - 1))
                        inst.then_inc(smm.h, smm.inc())
                t.wait_ge(fin.h, 3)
                t.transpose(psT2[0:32, 0:128], m1v(), ids[:, :]).then_inc(
                    fin.h, 1)
                t.wait_ge(fin.h, 5)
                t.transpose(psT2[0:32, 0:128], rstdv(), ids[:, :]).then_inc(
                    fin.h, 1)

            @block.scalar
            def _(s):
                for ti in range(NTT):
                    j = ti % 2
                    s.wait_ge(ldx[j].h, 16 * (ti // 2 + 1))
                    if ti >= 1:
                        s.wait_ge(smm.h, 4 * ti)
                    y23 = y2t[j][:, :].rearrange("p (c t) -> p c t", c=KC)
                    sq3 = sqt[:, :].rearrange("p (c t) -> p c t", c=KC)
                    for k in range(KC):
                        inst = s.activation(sq3[:, k, :], y23[:, k, :],
                                            AF.Square)
                    inst.then_inc(dvq.h, dvq.inc())
                s.wait_ge(fin.h, 1)
                s.activation(rstdv(), varv(), AF.Ln, bias=EPS, scale=1.0)
                s.activation(rstdv(), rstdv(), AF.Exp, bias=0.0, scale=-0.5
                             ).then_inc(fin.h, 1)

            @block.gpsimd
            def _(g):
                for r, row in enumerate([V_M3, V_RSTD3]):
                    g.wait_ge(fin.h, 5 + 2 * r)
                    g.dma_start(sc["vecs"][row:row + 1, :].rearrange(
                        "r (c t) -> (r c) t", c=32),
                        tpo[:, :]).then_inc(wbv.h, wbv.inc(16))
                g.wait_ge(wbv.h, wbv.n)


def emit_ffn_x2(b, io, sc, per):
    nc = b.nc
    ones_r, ones_c, gv, bv, bff1c, bff2c, ids = (
        per["ones_r"], per["ones_c"], per["gv"], per["bv"], per["bff1c"],
        per["bff2c"], per["ids"])
    y2T3, x1T3, x2T3 = fm(sc["y2T"]), fm(sc["x1T"]), fm(sc["x2T"])
    with ExitStack() as ph:
        P = ph.enter_context
        y2t = [P(nc.sbuf_tensor(f"f_y2{j}", [128, KC * TT], F32))
               for j in range(2)]
        x1t = P(nc.sbuf_tensor("f_x1", [128, KC * TT], F32))
        x2t = P(nc.sbuf_tensor("f_x2", [128, KC * TT], F32))
        sqt = P(nc.sbuf_tensor("f_sq", [128, KC * TT], F32))
        xnt = [P(nc.sbuf_tensor(f"f_xn{j}", [128, KC * TT], BF16))
               for j in range(2)]
        wf1 = [P(nc.sbuf_tensor(f"f_w1{j}", [128, 8 * D], BF16))
               for j in range(2)]
        wf2 = [P(nc.sbuf_tensor(f"f_w2{j}", [128, 8 * D], BF16))
               for j in range(2)]
        glb = [P(nc.sbuf_tensor(f"f_gl{j}", [128, 8 * TT], BF16))
               for j in range(2)]
        ffa = P(nc.sbuf_tensor("f_ffa", [128, KC * TT], F32))
        rowm = [P(nc.sbuf_tensor(f"f_rm{j}", [1, TT], F32)) for j in range(2)]
        rowr = [P(nc.sbuf_tensor(f"f_rr{j}", [1, TT], F32)) for j in range(2)]
        s1tok = P(nc.sbuf_tensor("f_s1", [128, NT128], F32))
        s2tok = P(nc.sbuf_tensor("f_s2", [128, NT128], F32))
        stt = P(nc.sbuf_tensor("f_stt", [128, 4 * NT128], F32))
        tpo = P(nc.sbuf_tensor("f_tpo", [32, 128], F32))
        jk = P(nc.sbuf_tensor("f_jk", [128, 2], F32))
        psB1 = P(nc.psum_tensor("f_pB1", [128, 512], F32))
        psB2 = P(nc.psum_tensor("f_pB2", [128, 512], F32))
        psF = [P(nc.psum_tensor(f"f_pF{j}", [128, 512], F32)) for j in range(2)]
        psC = [P(nc.psum_tensor(f"f_pC{j}", [128, 512], F32)) for j in range(2)]
        psSum = [P(nc.psum_tensor(f"f_pS{j}", [128, 512], F32))
                 for j in range(2)]


        ldy = [b.sem() for _ in range(2)]
        ldx1 = b.sem()
        ldw1 = [b.sem() for _ in range(2)]
        ldw2 = [b.sem() for _ in range(2)]
        ldr2 = [b.sem() for _ in range(2)]
        bcd = b.sem()
        dvn = b.sem()
        sqd = b.sem()
        fmm = b.sem()
        gld = b.sem()
        cmm = b.sem()
        cev = b.sem()
        dvx = b.sem()
        smm = b.sem()
        sev = b.sem()
        wb1 = b.sem()
        fin = b.sem()
        wbv = b.sem()
        pw2 = b.sem()

        m1v = lambda: stt[:, 0:32]
        varv = lambda: stt[:, 32:64]
        rstdv = lambda: stt[:, 64:96]
        tmpv = lambda: stt[:, 96:128]

        w2r = io["w_ff2"].rearrange("(b c p) n -> b p c n", p=128, c=8)
        w1r = io["w_ff1"].rearrange("(c p) (b n) -> b p c n", p=128, b=4)

        with nc.Block("p8") as block:
            @block.sync
            def _(s):
                for ti in range(NTT):
                    j = ti % 2
                    ts_ = slice(ti * TT, (ti + 1) * TT)
                    if ti >= 2:
                        s.wait_ge(dvn.h, ti - 1)
                        s.wait_ge(dvx.h, 8 * (ti - 1))
                        s.wait_ge(bcd.h, ti - 1)
                    s.dma_start(
                        y2t[j][:, :].rearrange("p (c t) -> p c t", c=KC),
                        y2T3[:, :, ts_]).then_inc(ldy[j].h, ldy[j].inc(16))
                    if ti >= 1:
                        s.wait_ge(dvx.h, 8 * ti)
                    s.dma_start(
                        x1t[:, :].rearrange("p (c t) -> p c t", c=KC),
                        x1T3[:, :, ts_]).then_inc(ldx1.h, ldx1.inc(16))
                    s.dma_start(rowm[j][:, :], sc["vecs"][V_M3:V_M3 + 1, ts_]
                                ).then_inc(ldr2[j].h, ldr2[j].inc(16))
                    s.dma_start(rowr[j][:, :],
                                sc["vecs"][V_RSTD3:V_RSTD3 + 1, ts_]
                                ).then_inc(ldr2[j].h, ldr2[j].inc(16))

            @block.gpsimd
            def _(g):
                for ti in range(NTT):
                    for kb in range(4):
                        j = (4 * ti + kb) % 2
                        qb = 4 * ti + kb
                        if qb >= 2:
                            g.wait_ge(fmm.h, 8 * (qb - 1))
                        g.dma_start(wf1[j][:, :].rearrange(
                            "p (c n) -> p c n", c=KC), w1r[kb]
                        ).then_inc(ldw1[j].h, ldw1[j].inc(16))
                        if qb >= 2:
                            g.wait_ge(pw2.h, qb - 1)
                        g.dma_start(wf2[j][:, :].rearrange(
                            "p (c n) -> p c n", c=8), w2r[kb]
                        ).then_inc(ldw2[j].h, ldw2[j].inc(16))
                    g.wait_ge(dvx.h, 8 * (ti + 1))
                    if ti >= 1:
                        g.wait_ge(wb1.h, 16 * ti)
                    g.dma_start(x2T3[:, :, ti * TT:(ti + 1) * TT],
                                x2t[:, :].rearrange("p (c t) -> p c t", c=KC)
                                ).then_inc(wb1.h, wb1.inc(16))
                for r, row in enumerate([V_M4, V_RSTD4]):
                    g.wait_ge(fin.h, 5 + 2 * r)
                    g.dma_start(sc["vecs"][row:row + 1, :].rearrange(
                        "r (c t) -> (r c) t", c=32),
                        tpo[:, :]).then_inc(wbv.h, wbv.inc(16))
                g.wait_ge(wbv.h, wbv.n)

            @block.tensor
            def _(t):
                for ti in range(NTT):
                    j = ti % 2
                    ts_ = slice(ti * TT, (ti + 1) * TT)
                    t.wait_ge(ldr2[j].h, 32 * (ti // 2 + 1))
                    if ti >= 1:
                        t.wait_ge(dvn.h, ti)
                    t.matmul(psB1[:, :], ones_r[:, :], rowm[j][0:1, :],
                             start=True, stop=True)
                    t.matmul(psB2[:, :], ones_r[:, :], rowr[j][0:1, :],
                             start=True, stop=True).then_inc(bcd.h, bcd.inc())
                    t.wait_ge(dvn.h, ti + 1)
                    xn3 = xnt[ti % 2][:, :].rearrange("p (c t) -> p c t", c=KC)
                    for kb in range(4):
                        jw = (4 * ti + kb) % 2
                        qb = 4 * ti + kb
                        wf23 = wf2[jw][:, :].rearrange("p (c n) -> p c n", c=8)
                        wf13 = wf1[jw][:, :].rearrange("p (c n) -> p c n",
                                                       c=KC)
                        t.wait_ge(ldw1[jw].h, 16 * (qb // 2 + 1))
                        for nn in range(8):
                            n = 8 * kb + nn
                            gi = 32 * ti + n
                            if gi >= 2:
                                t.wait_ge(gld.h, gi - 1)
                            for k in range(KC):
                                inst = t.matmul(
                                    psF[gi % 2][:, :],
                                    wf13[:, k, nn * 128:(nn + 1) * 128],
                                    xn3[:, k, :], start=(k == 0),
                                    stop=(k == KC - 1))
                            inst.then_inc(fmm.h, fmm.inc())
                        t.wait_ge(ldw2[jw].h, 16 * (qb // 2 + 1))
                        t.wait_ge(gld.h, 32 * ti + 8 * kb + 8)
                        gl3 = glb[(4 * ti + kb) % 2][:, :].rearrange(
                            "p (c t) -> p c t", c=8)
                        for mo in range(KC):
                            ci = 32 * ti + 8 * kb + mo
                            if ci >= 2:
                                t.wait_ge(cev.h, ci - 1)
                            for k2 in range(8):
                                inst = t.matmul(
                                    psC[ci % 2][:, :],
                                    wf23[:, k2, mo * 128:(mo + 1) * 128],
                                    gl3[:, k2, :], start=(k2 == 0),
                                    stop=(k2 == 7))
                            inst.then_inc(cmm.h, cmm.inc())
                        t.nop().then_inc(pw2.h, pw2.inc())
                    t.wait_ge(dvx.h, 8 * (ti + 1))
                    t.wait_ge(sqd.h, ti + 1)
                    x23 = x2t[:, :].rearrange("p (c t) -> p c t", c=KC)
                    sq3 = sqt[:, :].rearrange("p (c t) -> p c t", c=KC)
                    for g in range(4):
                        gi4 = 4 * ti + g
                        gs = slice(g * 128, (g + 1) * 128)
                        if gi4 >= 2:
                            t.wait_ge(sev.h, gi4 - 1)
                        for k in range(KC):
                            t.matmul(psSum[gi4 % 2][:, 0:1], x23[:, k, gs],
                                     ones_c[:, :], start=(k == 0),
                                     stop=(k == KC - 1))
                        for k in range(KC):
                            inst = t.matmul(psSum[gi4 % 2][:, 1:2],
                                            sq3[:, k, gs], ones_c[:, :],
                                            start=(k == 0), stop=(k == KC - 1))
                        inst.then_inc(smm.h, smm.inc())
                t.wait_ge(fin.h, 3)
                t.transpose(psF[0][0:32, 0:128], m1v(), ids[:, :]).then_inc(
                    fin.h, 1)
                t.wait_ge(fin.h, 5)
                t.transpose(psF[0][0:32, 0:128], rstdv(), ids[:, :]).then_inc(
                    fin.h, 1)

            @block.scalar
            def _(s):
                for ti in range(NTT):
                    for n in range(32):
                        gi = 32 * ti + n
                        s.wait_ge(fmm.h, gi + 1)
                        kb = n // 8
                        gl3 = glb[(4 * ti + kb) % 2][:, :].rearrange(
                            "p (c t) -> p c t", c=8)
                        if n % 8 == 0 and (4 * ti + kb) >= 2:
                            s.wait_ge(cmm.h, 8 * (4 * ti + kb - 1))
                        s.activation(gl3[:, n % 8, :], psF[gi % 2][:, :],
                                     AF.Gelu_apprx_tanh,
                                     bias=bff1c[:, n:n + 1], scale=1.0
                                     ).then_inc(gld.h, gld.inc())
                    s.wait_ge(dvx.h, 8 * (ti + 1))
                    if ti >= 1:
                        s.wait_ge(smm.h, 4 * ti)
                    x23s = x2t[:, :].rearrange("p (c t) -> p c t", c=KC)
                    sq3s = sqt[:, :].rearrange("p (c t) -> p c t", c=KC)
                    for k in range(KC):
                        inst = s.activation(sq3s[:, k, :], x23s[:, k, :],
                                            AF.Square)
                    inst.then_inc(sqd.h, sqd.inc())
                s.wait_ge(fin.h, 1)
                s.activation(rstdv(), varv(), AF.Ln, bias=EPS, scale=1.0)
                s.activation(rstdv(), rstdv(), AF.Exp, bias=0.0, scale=-0.5
                             ).then_inc(fin.h, 1)

            @block.vector
            def _(v):
                v.memset(jk[:, :], 0.0)
                for ti in range(NTT):
                    j = ti % 2
                    v.wait_ge(ldy[j].h, 16 * (ti // 2 + 1))
                    v.wait_ge(bcd.h, ti + 1)
                    if ti >= 2:
                        v.wait_ge(fmm.h, 32 * (ti - 1))
                    y23 = y2t[j][:, :].rearrange("p (c t) -> p c t", c=KC)
                    xn3 = xnt[ti % 2][:, :].rearrange("p (c t) -> p c t", c=KC)
                    for k in range(KC):
                        fa3 = ffa[:, :].rearrange("p (c t) -> p c t", c=KC)
                        v.tensor_sub(fa3[:, k, :], y23[:, k, :], psB1[:, :])
                        v.tensor_mul(fa3[:, k, :], fa3[:, k, :], psB2[:, :])
                        inst = v.tensor_scalar(
                            xn3[:, k, :], fa3[:, k, :],
                            gv[:, 2 * KC + k:2 * KC + k + 1],
                            bv[:, 2 * KC + k:2 * KC + k + 1],
                            ALU.mult, ALU.add)
                    inst.then_inc(dvn.h, dvn.inc())
                    fa3 = ffa[:, :].rearrange("p (c t) -> p c t", c=KC)
                    for kb in range(4):
                        for mo in range(KC):
                            ci = 32 * ti + 8 * kb + mo
                            v.wait_ge(cmm.h, ci + 1)
                            if kb == 0:
                                v.tensor_copy(fa3[:, mo, :], psC[ci % 2][:, :]
                                              ).then_inc(cev.h, cev.inc())
                            else:
                                v.tensor_add(fa3[:, mo, :], fa3[:, mo, :],
                                             psC[ci % 2][:, :]).then_inc(
                                    cev.h, cev.inc())
                    v.wait_ge(ldx1.h, 16 * (ti + 1))
                    if ti >= 1:
                        v.wait_ge(wb1.h, 16 * ti)
                        v.wait_ge(smm.h, 4 * ti)
                        v.wait_ge(sqd.h, ti)
                    x13 = x1t[:, :].rearrange("p (c t) -> p c t", c=KC)
                    x23 = x2t[:, :].rearrange("p (c t) -> p c t", c=KC)
                    for k in range(KC):
                        v.scalar_tensor_tensor(
                            x23[:, k, :], fa3[:, k, :], bff2c[:, k:k + 1],
                            y23[:, k, :], ALU.add, ALU.add)
                        v.tensor_add(x23[:, k, :], x23[:, k, :], x13[:, k, :]
                                     ).then_inc(dvx.h, dvx.inc())
                    for g in range(4):
                        gi4 = 4 * ti + g
                        v.wait_ge(smm.h, gi4 + 1)
                        v.tensor_copy(s1tok[:, gi4:gi4 + 1],
                                      psSum[gi4 % 2][:, 0:1])
                        v.tensor_copy(s2tok[:, gi4:gi4 + 1],
                                      psSum[gi4 % 2][:, 1:2]).then_inc(
                            sev.h, sev.inc())
                v.tensor_scalar_mul(m1v(), s1tok[:, :], 1.0 / D)
                v.tensor_scalar_mul(tmpv(), s2tok[:, :], 1.0 / D)
                v.tensor_mul(varv(), m1v(), m1v())
                v.tensor_scalar_mul(jk[:, 0:1], jk[:, 1:2], 1.0)
                v.tensor_sub(varv(), tmpv(), varv()).then_inc(fin.h, 1)
                v.wait_ge(fin.h, 2)
                v.nop().then_inc(fin.h, 1)
                v.wait_ge(fin.h, 4)
                v.tensor_copy(tpo[:, :], psF[0][0:32, 0:128]).then_inc(
                    fin.h, 1)
                v.wait_ge(fin.h, 6)
                v.wait_ge(wbv.h, 16)
                v.tensor_copy(tpo[:, :], psF[0][0:32, 0:128]).then_inc(
                    fin.h, 1)


def emit_h3_gate(b, io, sc, per):
    nc = b.nc
    ones_r, gateT, ids = per["ones_r"], per["gateT"], per["ids"]
    x2T3, hT3 = fm(sc["x2T"]), fm(sc["hT"])
    with ExitStack() as ph:
        P = ph.enter_context
        x2t = [P(nc.sbuf_tensor(f"g_x2{j}", [128, KC * TT], F32))
               for j in range(2)]
        h3f = [P(nc.sbuf_tensor(f"g_h3{j}", [128, KC * TT], F32))
               for j in range(2)]
        h3b = P(nc.sbuf_tensor("g_hb", [128, KC * TT], BF16))
        rowm = P(nc.sbuf_tensor("g_rm", [1, T], F32))
        rowr = P(nc.sbuf_tensor("g_rr", [1, T], F32))
        gsb = P(nc.sbuf_tensor("g_gsb", [8, TT], F32))
        sctile = P(nc.sbuf_tensor("g_sc", [128, 10], F32))
        msc = P(nc.sbuf_tensor("g_msc", [128, 8], F32))
        jk = P(nc.sbuf_tensor("g_jk", [128, 2], F32))
        mst = [P(nc.sbuf_tensor(f"g_ms{j}", [8, 128], F32)) for j in range(2)]
        psB1 = [P(nc.psum_tensor(f"g_pB1{j}", [128, 512], F32))
                for j in range(2)]
        psB2 = [P(nc.psum_tensor(f"g_pB2{j}", [128, 512], F32))
                for j in range(2)]
        psG = P(nc.psum_tensor("g_pG", [128, 512], F32))
        psS = P(nc.psum_tensor("g_pS", [128, 512], F32))
        psM = P(nc.psum_tensor("g_pM", [128, 512], F32))

        ldx = [b.sem() for _ in range(2)]
        ldr = b.sem()
        bcd = b.sem()
        dvh = b.sem()
        gmm = b.sem()
        gev = b.sem()
        smm = b.sem()
        top = b.sem()
        mmk = b.sem()
        sev = b.sem()
        acr = b.sem()
        wbh = b.sem()
        wbm = b.sem()

        with nc.Block("p9") as block:
            @block.sync
            def _(s):
                s.dma_start(rowm[:, :], sc["vecs"][V_M4:V_M4 + 1, :]
                            ).then_inc(ldr.h, ldr.inc(16))
                s.dma_start(rowr[:, :], sc["vecs"][V_RSTD4:V_RSTD4 + 1, :]
                            ).then_inc(ldr.h, ldr.inc(16))
                for ti in range(NTT):
                    j = ti % 2
                    if ti >= 2:
                        s.wait_ge(dvh.h, ti - 1)
                    s.dma_start(
                        x2t[j][:, :].rearrange("p (c t) -> p c t", c=KC),
                        x2T3[:, :, ti * TT:(ti + 1) * TT]
                    ).then_inc(ldx[j].h, ldx[j].inc(16))

            @block.tensor
            def _(t):
                t.wait_ge(ldr.h, 32)

                def bcast(k):
                    ks_ = slice(k * TT, (k + 1) * TT)
                    if k >= 2:
                        t.wait_ge(dvh.h, k - 1)
                    t.matmul(psB1[k % 2][:, :], ones_r[:, :], rowm[0:1, ks_],
                             start=True, stop=True)
                    t.matmul(psB2[k % 2][:, :], ones_r[:, :], rowr[0:1, ks_],
                             start=True, stop=True).then_inc(bcd.h, bcd.inc())

                bcast(0)
                for ti in range(NTT):
                    if ti + 1 < NTT:
                        bcast(ti + 1)
                    t.wait_ge(dvh.h, ti + 1)
                    if ti >= 1:
                        t.wait_ge(gev.h, ti)
                    h33 = h3f[ti % 2][:, :].rearrange("p (c t) -> p c t", c=KC)
                    for k in range(KC):
                        inst = t.matmul(psG[0:8, :], gateT[:, k * 8:(k + 1) * 8],
                                        h33[:, k, :], start=(k == 0),
                                        stop=(k == KC - 1))
                    inst.then_inc(gmm.h, gmm.inc())
                    for g in range(4):
                        gi = 4 * ti + g
                        t.wait_ge(gev.h, ti + 1)
                        if gi >= 1:
                            t.wait_ge(top.h, 2 * gi - 1)
                        t.transpose(psS[0:128, 0:8], gsb[:, g * 128:(g + 1) * 128],
                                    ids[0:8, 0:8]).then_inc(smm.h, smm.inc())
                        t.wait_ge(top.h, 2 * gi + 2)
                        if gi >= 2:
                            t.wait_ge(sev.h, gi - 1)
                        t.transpose(psM[0:8, 0:128], msc[:, :], ids[:, :]
                                    ).then_inc(mmk.h, mmk.inc())

            @block.vector
            def _(v):
                v.memset(jk[:, :], 0.0)
                for ti in range(NTT):
                    j = ti % 2
                    v.wait_ge(ldx[j].h, 16 * (ti // 2 + 1))
                    v.wait_ge(bcd.h, ti + 1)
                    if ti >= 2:
                        v.wait_ge(gmm.h, ti - 1)
                    if ti >= 1:
                        v.wait_ge(wbh.h, 16 * ti)
                    x23 = x2t[j][:, :].rearrange("p (c t) -> p c t", c=KC)
                    h33 = h3f[ti % 2][:, :].rearrange("p (c t) -> p c t", c=KC)
                    hb3 = h3b[:, :].rearrange("p (c t) -> p c t", c=KC)
                    for k in range(KC):
                        v.tensor_sub(h33[:, k, :], x23[:, k, :],
                                     psB1[ti % 2][:, :])
                        v.tensor_mul(h33[:, k, :], h33[:, k, :],
                                     psB2[ti % 2][:, :])
                        inst = v.tensor_copy(hb3[:, k, :], h33[:, k, :])
                    inst.then_inc(dvh.h, dvh.inc())
                    v.wait_ge(gmm.h, ti + 1)
                    v.tensor_copy(gsb[:, :], psG[0:8, 0:TT]).then_inc(
                        gev.h, gev.inc())
                    sp = mk_sp(v, jk)
                    for g in range(4):
                        gi = 4 * ti + g
                        v.wait_ge(smm.h, gi + 1)
                        s8 = sctile[:, 0:8]
                        v.tensor_copy(s8, psS[0:128, 0:8])
                        sp()
                        v.reduce_max(sctile[:, 8:9], s8, AX.X)
                        sp()
                        v.tensor_scalar_mul(sctile[:, 9:10], sctile[:, 8:9],
                                            -1.0).then_inc(top.h, top.inc())
                        v.wait_ge(acr.h, gi + 1)
                        v.reduce_sum(sctile[:, 8:9], s8, AX.X)
                        sp()
                        v.reciprocal(sctile[:, 9:10], sctile[:, 8:9])
                        sp()
                        v.tensor_scalar_mul(s8, s8, sctile[:, 9:10])
                        sp()
                        v.reduce_max(sctile[:, 8:9], s8, AX.X)
                        sp()
                        v.tensor_single_scalar(msc[:, :], s8, sctile[:, 8:9],
                                               ALU.is_ge)
                        sp()
                        v.tensor_scalar_mul(msc[:, :], msc[:, :], 2.0)
                        sp()
                        v.tensor_sub(msc[:, :], s8, msc[:, :])
                        sp()
                        v.reduce_max(sctile[:, 8:9], msc[:, :], AX.X)
                        sp()
                        v.tensor_single_scalar(msc[:, :], s8, sctile[:, 8:9],
                                               ALU.is_ge)
                        sp()
                        v.tensor_mul(msc[:, :], s8, msc[:, :]).then_inc(
                            top.h, top.inc())
                        v.wait_ge(mmk.h, gi + 1)
                        if gi >= 2:
                            v.wait_ge(wbm.h, 16 * (gi - 1))
                        v.tensor_copy(mst[gi % 2][:, :], psM[0:8, 0:128]
                                      ).then_inc(sev.h, sev.inc())

            @block.scalar
            def _(s):
                for ti in range(NTT):
                    for g in range(4):
                        gi = 4 * ti + g
                        s.wait_ge(top.h, 2 * gi + 1)
                        s.activation(sctile[:, 0:8], sctile[:, 0:8], AF.Exp,
                                     bias=sctile[:, 9:10], scale=1.0
                                     ).then_inc(acr.h, acr.inc())

            @block.gpsimd
            def _(g):
                for ti in range(NTT):
                    g.wait_ge(dvh.h, ti + 1)
                    if ti >= 1:
                        g.wait_ge(wbh.h, 16 * ti)
                    g.dma_start(hT3[:, :, ti * TT:(ti + 1) * TT],
                                h3b[:, :].rearrange("p (c t) -> p c t", c=KC)
                                ).then_inc(wbh.h, wbh.inc(16))
                    for gg in range(4):
                        gi = 4 * ti + gg
                        g.wait_ge(sev.h, gi + 1)
                        if gi >= 1:
                            g.wait_ge(wbm.h, 16 * gi)
                        g.dma_start(
                            sc["masksd"][:, gi * 128:(gi + 1) * 128],
                            mst[gi % 2][:, :]).then_inc(wbm.h, wbm.inc(16))
                g.wait_ge(wbh.h, wbh.n)
                g.wait_ge(wbm.h, wbm.n)


def emit_final(b, io, sc, per):
    nc = b.nc
    ids = per["ids"]
    x2T3, facc3 = fm(sc["x2T"]), fm(sc["facc"])
    with ExitStack() as ph:
        P = ph.enter_context
        x2t = [P(nc.sbuf_tensor(f"z_x2{j}", [128, KC * TT], F32))
               for j in range(2)]
        fat = [P(nc.sbuf_tensor(f"z_fa{j}", [128, KC * TT], F32))
               for j in range(2)]
        ot = P(nc.sbuf_tensor("z_ot", [128, KC * TT], F32))
        ost = [P(nc.sbuf_tensor(f"z_os{j}", [128, D], F32)) for j in range(2)]
        psT = [P(nc.psum_tensor(f"z_pT{j}", [128, 512], F32)) for j in range(2)]

        ldx = [b.sem() for _ in range(2)]
        ldf = [b.sem() for _ in range(2)]
        dvo = b.sem()
        tpx = b.sem()
        evx = b.sem()
        wbo = b.sem()

        with nc.Block("p11") as block:
            @block.sync
            def _(s):
                for ti in range(NTT):
                    j = ti % 2
                    if ti >= 2:
                        s.wait_ge(dvo.h, ti - 1)
                    s.dma_start(
                        x2t[j][:, :].rearrange("p (c t) -> p c t", c=KC),
                        x2T3[:, :, ti * TT:(ti + 1) * TT]
                    ).then_inc(ldx[j].h, ldx[j].inc(16))
                    s.dma_start(
                        fat[j][:, :].rearrange("p (c t) -> p c t", c=KC),
                        facc3[:, :, ti * TT:(ti + 1) * TT]
                    ).then_inc(ldf[j].h, ldf[j].inc(16))

            @block.vector
            def _(v):
                for ti in range(NTT):
                    j = ti % 2
                    v.wait_ge(ldx[j].h, 16 * (ti // 2 + 1))
                    v.wait_ge(ldf[j].h, 16 * (ti // 2 + 1))
                    if ti >= 1:
                        v.wait_ge(tpx.h, 32 * ti)
                    x23 = x2t[j][:, :].rearrange("p (c t) -> p c t", c=KC)
                    fa3 = fat[j][:, :].rearrange("p (c t) -> p c t", c=KC)
                    ot3 = ot[:, :].rearrange("p (c t) -> p c t", c=KC)
                    for k in range(KC):
                        inst = v.scalar_tensor_tensor(
                            ot3[:, k, :], fa3[:, k, :], 0.5, x23[:, k, :],
                            ALU.mult, ALU.add)
                    inst.then_inc(dvo.h, dvo.inc())

            @block.tensor
            def _(t):
                for ti in range(NTT):
                    t.wait_ge(dvo.h, ti + 1)
                    ot3 = ot[:, :].rearrange("p (c t) -> p c t", c=KC)
                    for g in range(4):
                        for k in range(KC):
                            gidx = 32 * ti + 8 * g + k
                            if gidx >= 2:
                                t.wait_ge(evx.h, gidx - 1)
                            t.transpose(psT[gidx % 2][0:128, 0:128],
                                        ot3[:, k, g * 128:(g + 1) * 128],
                                        ids[:, :]).then_inc(tpx.h, tpx.inc())

            @block.scalar
            def _(s):
                for ti in range(NTT):
                    for g in range(4):
                        gi = 4 * ti + g
                        for k in range(KC):
                            gidx = 32 * ti + 8 * g + k
                            if k == 0 and gi >= 2:
                                s.wait_ge(wbo.h, 16 * (gi - 1))
                            s.wait_ge(tpx.h, gidx + 1)
                            s.copy(ost[gi % 2][:, k * 128:(k + 1) * 128],
                                   psT[gidx % 2][0:128, 0:128]).then_inc(
                                evx.h, evx.inc())

            @block.gpsimd
            def _(g):
                for ti in range(NTT):
                    for gg in range(4):
                        gi = 4 * ti + gg
                        g.wait_ge(evx.h, 8 * (gi + 1))
                        if gi >= 1:
                            g.wait_ge(wbo.h, 16 * gi)
                        g.dma_start(io["out"][gi * 128:(gi + 1) * 128, :],
                                    ost[gi % 2][:, :]).then_inc(
                            wbo.h, wbo.inc(16))
                g.wait_ge(wbo.h, wbo.n)


def _host_constants():
    cid = np.eye(128, dtype=np.float32)
    cidb = np.eye(128, dtype=ml_dtypes.bfloat16)
    cidb2 = np.vstack([np.eye(64), np.eye(64)]).astype(ml_dtypes.bfloat16)
    tril = np.triu(np.ones((128, 128), dtype=np.float32)) * ETA
    csign = np.tile(np.array([1.0, -1.0], dtype=np.float32), 256)[None, :]
    return dict(cid=cid, cidb=cidb, cidb2=cidb2, ctrileta=tril,
                csign=csign.astype(np.float32))


_NC_CACHE = {}


def kernel(**inputs):
    if "nc" not in _NC_CACHE:
        _NC_CACHE["nc"] = build_program()
    nc = _NC_CACHE["nc"]
    x = np.ascontiguousarray(np.asarray(inputs["L0_attn_x"], dtype=np.float32))
    consts = _host_constants()
    common = dict(
        ln1_g=inputs["L0_attn_ln1_g"], ln1_b=inputs["L0_attn_ln1_b"],
        ln2_g=inputs["L0_attn_ln2_g"], ln2_b=inputs["L0_attn_ln2_b"],
        ln3_g=inputs["L0_attn_ln3_g"], ln3_b=inputs["L0_attn_ln3_b"],
        w_pot=inputs["L0_attn_w_pot"], b_pot=inputs["L0_attn_b_pot"],
        w_bk=inputs["L0_attn_w_bk"], b_bk=inputs["L0_attn_b_bk"],
        w_qkv=inputs["L0_attn_w_qkv"], b_qkv=inputs["L0_attn_b_qkv"],
        w_out=inputs["L0_attn_w_out"], b_out=inputs["L0_attn_b_out"],
        w_ff1=inputs["L0_attn_w_ff1"], b_ff1=inputs["L0_attn_b_ff1"],
        w_ff2=inputs["L0_attn_w_ff2"], b_ff2=inputs["L0_attn_b_ff2"],
        gate_w=inputs["L0_ffn_gate_w"], w1=inputs["L0_ffn_w1"],
        w2=inputs["L0_ffn_w2"], w3=inputs["L0_ffn_w3"], **consts)
    feed = {}
    for k, v in common.items():
        a = np.asarray(v)
        if k in ("cidb", "cidb2"):
            feed[k] = np.ascontiguousarray(a)
        else:
            feed[k] = np.ascontiguousarray(a.astype(np.float32))
    common = feed
    in_maps = [dict(common, xin=np.ascontiguousarray(x[c]))
               for c in range(8)]
    res = run_bass_kernel_spmd(nc, in_maps, core_ids=list(range(8)))
    out = np.stack([res.results[c]["out"] for c in range(8)], axis=0)
    return out.astype(np.float32)
